# version 08
# baseline (speedup 1.0000x reference)
import numpy as np
import ml_dtypes
import concourse.bass as bass
import concourse.mybir as mybir
from concourse.bass_utils import run_bass_kernel_spmd
from contextlib import ExitStack

F32 = mybir.dt.float32
BF16 = mybir.dt.bfloat16
I32 = mybir.dt.int32
AF = mybir.ActivationFunctionType
ALU = mybir.AluOpType
AX = mybir.AxisListType
bf16 = ml_dtypes.bfloat16

NCORES = 8
SEQ = 2048
NSEQ = 2
T = SEQ * NSEQ
D = 1024
DH = 4096
G = 512
NG = T // G
GPS = SEQ // G
NSUB = 4
HA, DKA = 16, 64
LN_EPS = 1e-5

class Ctr:
    def __init__(self, handle):
        self.h = handle; self.v = 0

class B:
    ENGS = ("sync", "gpsimd", "scalar", "vector", "tensor")
    def __init__(self):
        self.q = {e: [] for e in self.ENGS}
    def wait(self, eng, ctr, val=None):
        v = ctr.v if val is None else val
        if v <= 0: return
        h = ctr.h
        self.q[eng].append(lambda e: e.wait_ge(h, v))
    def _inc(self, eng, fn, ctr, amt):
        if ctr is None:
            self.q[eng].append(fn)
        else:
            h = ctr.h
            self.q[eng].append(lambda e: fn(e).then_inc(h, amt))
            ctr.v += amt

class Arena:
    def __init__(self, tens, width):
        self.t = tens; self.width = width; self.off = 0
    def alloc(self, n):
        assert self.off + n <= self.width, f"arena overflow {self.off}+{n}>{self.width}"
        v = self.t[:, self.off:self.off + n]
        self.off += n
        return v

def host_prep(inputs):
    x = np.asarray(inputs["L0_attn_a_inputs"])
    d = {}
    d["wq"] = np.ascontiguousarray(inputs["L0_attn_a_W_Q"])
    d["wk"] = np.ascontiguousarray(inputs["L0_attn_a_W_K"])
    d["wv"] = np.ascontiguousarray(inputs["L0_attn_a_W_V"])
    d["wfc"] = np.ascontiguousarray(inputs["L0_attn_a_W_fc"])
    d["gamma_bc"] = np.ascontiguousarray(np.broadcast_to(np.asarray(inputs["L0_attn_a_gamma"])[None, :], (128, D)))
    d["beta_bc"] = np.ascontiguousarray(np.broadcast_to(np.asarray(inputs["L0_attn_a_beta"])[None, :], (128, D)))
    d["wqkvT"] = np.ascontiguousarray(np.asarray(inputs["L0_attn_b_W_qkv"]).T)
    d["wprojT"] = np.ascontiguousarray(np.asarray(inputs["L0_attn_b_W_proj"]).T)
    d["bproj_bc"] = np.ascontiguousarray(np.broadcast_to(np.asarray(inputs["L0_attn_b_b_proj"])[None, :], (128, D)))
    d["w1T"] = np.ascontiguousarray(np.asarray(inputs["L0_ffn_w1"]).T)
    d["w2T"] = np.ascontiguousarray(np.asarray(inputs["L0_ffn_w2"]).T)
    cwa = np.asarray(inputs["L0_ffn_conv_w"])
    for j in range(3):
        d[f"cw{j}"] = np.ascontiguousarray(cwa[:, 0, j].reshape(32, 128).T)
    d["cb"] = np.ascontiguousarray(np.asarray(inputs["L0_ffn_conv_b"]).reshape(32, 128).T)
    d["ident"] = np.eye(128, dtype=np.float32)
    per_core = []
    for c in range(NCORES):
        m = dict(d)
        m["xin"] = np.ascontiguousarray(x[2 * c:2 * c + 2].reshape(T, D))
        per_core.append(m)
    return per_core


def build_program(n_sublayers=4, dbg=None):
    nc = bass.Bass("TRN2", target_bir_lowering=False)
    es = ExitStack()
    ap = {}
    def din(name, shape, dt=F32):
        ap[name] = nc.dram_tensor(name, list(shape), dt, kind="ExternalInput").ap()
    din("xin", (T, D))
    for w in ("wq", "wk", "wv", "wfc"):
        din(w, (D, D))
    din("gamma_bc", (128, D)); din("beta_bc", (128, D))
    din("wqkvT", (D, 3 * D)); din("wprojT", (D, D)); din("bproj_bc", (128, D))
    din("w1T", (D, DH)); din("w2T", (DH, D))
    for j in range(3):
        din(f"cw{j}", (128, 32))
    din("cb", (128, 32))
    din("ident", (128, 128))
    out = nc.dram_tensor("out", [T, D], F32, kind="ExternalOutput").ap()
    XS = nc.dram_tensor("XS", [T, D], F32, kind="Internal").ap()
    YB = nc.dram_tensor("YB", [T, D], BF16, kind="Internal").ap()
    OB = nc.dram_tensor("OB", [T, D], BF16, kind="Internal").ap()
    H2 = nc.dram_tensor("H2", [DH, T], F32, kind="Internal").ap()
    WQ1 = nc.dram_tensor("WQ1", [D, DH], BF16, kind="Internal").ap()
    WQ2 = nc.dram_tensor("WQ2", [DH, D], BF16, kind="Internal").ap()

    bfa_t = es.enter_context(nc.sbuf_tensor("bfa", [128, 38400], BF16))
    wta_t = es.enter_context(nc.sbuf_tensor("wta", [128, 32768], BF16))
    f32a_t = es.enter_context(nc.sbuf_tensor("f32a", [128, 10240], F32))
    per_t = es.enter_context(nc.sbuf_tensor("per", [128, 3904], F32))
    i32_t = es.enter_context(nc.sbuf_tensor("i32b", [128, 1024], I32))
    f32b_t = es.enter_context(nc.sbuf_tensor("f32b", [128, 1152], F32))
    gcv_t = es.enter_context(nc.sbuf_tensor("gcv", [128, 1032], F32))
    per = Arena(per_t, 3904)
    numbuf = f32b_t[:, 0:1088]

    pb = [es.enter_context(nc.psum_tensor(f"pb{i}", [128, 512], F32)) for i in range(5)]
    kvps = es.enter_context(nc.psum_tensor("kvps", [128, 1536], F32))

    ident = per.alloc(128)
    gamma = per.alloc(D); beta = per.alloc(D); bproj = per.alloc(D)
    cw = [per.alloc(32) for _ in range(3)]
    cbv = per.alloc(32)
    onesf = per.alloc(128)
    invws1 = per.alloc(1); invws2 = per.alloc(1)
    stat_mu = per.alloc(32); stat_iv = per.alloc(32)
    stat_mu2 = per.alloc(4); stat_iv2 = per.alloc(4)
    aq = per.alloc(32)
    rabq = per.alloc(32)
    Aq = per.alloc(32)
    rAq = per.alloc(32)
    scr = per.alloc(64)
    den8 = per.alloc(32)
    tiny = per.alloc(32)

    b = B()
    sems = {k: Ctr(es.enter_context(nc.semaphore(k))) for k in ("LD", "LW", "LT", "ST", "PE", "DV", "AC", "GP")}
    LD, LW, LT, ST, PE, DV, AC, GP = (sems[k] for k in ("LD", "LW", "LT", "ST", "PE", "DV", "AC", "GP"))

    def dma(qeng, outv, inv, ctr):
        b._inc(qeng, lambda e, o=outv, i=inv: e.dma_start(out=o, in_=i), ctr, 16)
    def dmaT(qeng, outv, inv, ctr):
        b._inc(qeng, lambda e, o=outv, i=inv: e.dma_start_transpose(o, i), ctr, 16)
    def vec(fn, inc=1):
        b._inc("vector", fn, DV if inc else None, 1)
    def act(fn, inc=1):
        b._inc("scalar", fn, AC if inc else None, 1)
    def gp(fn, inc=1):
        b._inc("gpsimd", fn, GP if inc else None, 1)
    def pe(fn, inc=1):
        b._inc("tensor", fn, PE if inc else None, 1)

    dma("sync", ident[:], ap["ident"][:, :], LD)
    dma("sync", gamma[:], ap["gamma_bc"][:, :], LD)
    dma("sync", beta[:], ap["beta_bc"][:, :], LD)
    dma("sync", bproj[:], ap["bproj_bc"][:, :], LD)
    for j in range(3):
        dma("sync", cw[j][:], ap[f"cw{j}"][:, :], LD)
    dma("sync", cbv[:], ap["cb"][:, :], LD)
    b.wait("vector", LD)
    vec(lambda e: e.memset(onesf[:], 1.0), inc=0)
    vec(lambda e: e.memset(scr[:], 0.0))

    def vtiny(fn):
        vec(fn, inc=1)
        for _ in range(16):
            b.q["vector"].append(lambda e: e.engine_nop())
    def atiny(fn):
        act(fn, inc=1)
        b.wait("scalar", AC)

    f32a = Arena(f32a_t, 10240)
    wld = [f32a.alloc(4096), f32a.alloc(4096)]
    wbf_a = Arena(bfa_t, 38400)
    wbf = wbf_a.alloc(4096)

    def quant_weight(wapT, nchunk, wqap, invws_col):
        cols = DH if nchunk == 8 else D
        guard = [DV.v, DV.v]
        accv = tiny[:, 0:nchunk] if nchunk <= 16 else None
        acc32 = per
        accview = tiny[:, 0:8] if nchunk == 8 else scr[:, 0:32]
        for c in range(nchunk):
            buf = wld[c % 2][:, :cols]
            b.wait("sync", DV, guard[c % 2])
            dma("sync", buf, wapT[128 * c:128 * (c + 1), :], LD)
            b.wait("vector", LD)
            vec(lambda e, buf=buf, c=c: e.reduce_sum(accview[:, c:c + 1], buf, axis=AX.X, apply_absolute_value=True))
            guard[c % 2] = DV.v
        b.wait("vector", DV)
        vec(lambda e: e.reduce_sum(scr[:, 40:41], accview, axis=AX.X))
        b.wait("vector", DV)
        b.wait("tensor", DV)
        pe(lambda e: e.matmul(pb[0][0:1, 0:1], onesf[:, 0:1], scr[:, 40:41], start=True, stop=True))
        nelem = float(128 * nchunk * cols)
        b.wait("vector", PE)
        vtiny(lambda e: e.tensor_scalar(scr[0:1, 41:42], pb[0][0:1, 0:1], 1.0 / nelem, 1e-5, op0=ALU.mult, op1=ALU.max))
        b.wait("tensor", DV)
        pe(lambda e: e.matmul(pb[0][:, 4:5], onesf[0:1, :], scr[0:1, 41:42], start=True, stop=True))
        b.wait("vector", PE)
        vtiny(lambda e: e.tensor_copy(invws_col[:, 0:1], pb[0][:, 4:5]))
        vtiny(lambda e: e.reciprocal(scr[:, 42:43], invws_col[:, 0:1]))
        wsv = scr[:, 42:43]
        st_guard = ST.v
        for c in range(nchunk):
            buf = wld[c % 2][:, :cols]
            b.wait("sync", DV, guard[c % 2])
            dma("sync", buf, wapT[128 * c:128 * (c + 1), :], LD)
            b.wait("vector", LD)
            b.wait("vector", ST, st_guard)
            for h0 in range(0, cols, 1024):
                hw = min(1024, cols - h0)
                vec(lambda e, buf=buf, h0=h0, hw=hw: e.tensor_scalar(i32_t[:, :hw], buf[:, h0:h0 + hw], wsv, None, op0=ALU.mult), inc=0)
                vec(lambda e, h0=h0, hw=hw: e.tensor_scalar(wbf[:, h0:h0 + hw], i32_t[:, :hw], -1.0, 1.0, op0=ALU.max, op1=ALU.min), inc=(1 if h0 + hw >= cols else 0))
            guard[c % 2] = DV.v
            b.wait("gpsimd", DV)
            dma("gpsimd", wqap[128 * c:128 * (c + 1), :], wbf[:, :cols], ST)
            st_guard = ST.v
    quant_weight(ap["w1T"], 8, WQ1, invws1)
    quant_weight(ap["w2T"], 32, WQ2, invws2)

    wta = Arena(wta_t, 32768)
    w_at = {}
    for w in ("wk", "wv", "wq", "wfc"):
        w_at[w] = wta.alloc(8 * D).rearrange("p (c n) -> p c n", c=8)
        for dch in range(8):
            dma("gpsimd", w_at[w][:, dch, :], ap[w][128 * dch:128 * (dch + 1), :], LW)
    p1_lw = LW.v

    bfa = Arena(bfa_t, 38400)
    yfm = [bfa.alloc(8 * G).rearrange("p (c n) -> p c n", c=8) for _ in range(2)]
    ofm = [bfa.alloc(8 * G).rearrange("p (c n) -> p c n", c=8) for _ in range(2)]
    Khold = bfa.alloc(D)
    Vext = bfa.alloc(HA * 65).rearrange("p (h c) -> p h c", h=HA)
    Qhold = bfa.alloc(8 * SEQ).rearrange("p (c n) -> p c n", c=8)
    KVblk = bfa.alloc(8 * 130).rearrange("p (q c) -> p q c", q=8)
    o_sb = bfa.alloc(D)
    ybf = bfa.alloc(D)
    xgr = f32a_t[:, 0:4096].rearrange("p (m c) -> p m c", m=4)
    xgx = f32a_t[:, 4096:8192].rearrange("p (m c) -> p m c", m=4)
    junk = f32a_t[:, 8192:9216]
    xbt = f32a_t[:, 9216:10240]
    st2 = per_t[:, 0:0]

    def group_ln(xg, base_col, n=NSUB, mu_arr=None, iv_arr=None):
        mu = mu_arr[:, base_col:base_col + n]
        iv = iv_arr[:, base_col:base_col + n]
        s_ = tiny[:, 0:n]; q_ = tiny[:, 4:4 + n]; m2 = tiny[:, 8:8 + n]
        for m in range(n):
            vec(lambda e, m=m: e.reduce_sum(s_[:, m:m + 1], xg[:, m, :], axis=AX.X), inc=(m == n - 1))
        b.wait("scalar", DV)
        for m in range(n):
            act(lambda e, m=m: e.activation(junk, xg[:, m, :], AF.Square, accum_out=q_[:, m:m + 1]), inc=(m == n - 1))
        b.wait("vector", AC)
        vtiny(lambda e: e.tensor_scalar(mu, s_, 1.0 / D, None, op0=ALU.mult))
        vtiny(lambda e: e.tensor_mul(m2, mu, mu))
        vtiny(lambda e: e.scalar_tensor_tensor(iv, q_, 1.0 / D, m2, op0=ALU.mult, op1=ALU.subtract))
        vtiny(lambda e: e.tensor_scalar(iv, iv, LN_EPS, None, op0=ALU.add))
        vtiny(lambda e: e.reciprocal(iv, iv))
        b.wait("scalar", DV)
        atiny(lambda e: e.activation(iv, iv, AF.Sqrt))
        b.wait("vector", AC)

    def kvcol(h):
        return (h // 7) * 512 + (h % 7) * 65 if h < 14 else 1024 + (h - 14) * 65

    if n_sublayers >= 1:
        xgr_guard = [DV.v]
        xgx_guard = [DV.v]
        yfm_guard = [0, 0]
        ofm_guard = [0, 0]
        ybf_guard = [0]
        osb_guard = [0]
        xbt_guard = [0]
        for s in range(NSEQ):
            ydone_st = [0] * GPS
            for gi in range(GPS + 1):
                if gi >= 1:
                    g = gi - 1; t0 = (s * GPS + g) * G
                    yf = yfm[g % 2]
                    b.wait("sync", ST, ydone_st[g])
                    b.wait("sync", PE, yfm_guard[g % 2])
                    for dch in range(8):
                        dmaT("sync", yf[:, dch, :], YB[t0:t0 + G, 128 * dch:128 * (dch + 1)], LT)
                    b.wait("tensor", LT)
                    b.wait("tensor", LW, p1_lw)
                    if s == 0 and g == 0:
                        b.wait("vector", LD, 0)
                        vec(lambda e: e.memset(Vext[:, :, 64:65], 1.0))
                        b.wait("tensor", DV)
                    for m in range(NSUB):
                        for wname, bank in (("wk", 0), ("wv", 2)):
                            for n in range(2):
                                for dch in range(8):
                                    pe(lambda e, wname=wname, bank=bank, n=n, dch=dch, m=m, yf=yf: e.matmul(
                                        pb[bank + n][:, :],
                                        yf[:, dch, m * 128:(m + 1) * 128],
                                        w_at[wname][:, dch, n * 512:(n + 1) * 512],
                                        start=(dch == 0), stop=(dch == 7)),
                                        inc=(1 if dch == 7 else 0))
                        kv_pe = PE.v
                        b.wait("scalar", PE, kv_pe)
                        for n in range(2):
                            act(lambda e, n=n: e.activation(Khold[:, n * 512:(n + 1) * 512], pb[n][:, :], AF.Sigmoid))
                        k_ac = AC.v
                        b.wait("vector", PE, kv_pe)
                        for n in range(2):
                            vec(lambda e, n=n: e.tensor_copy(
                                Vext[:, 8 * n:8 * (n + 1), 0:64],
                                pb[2 + n][:, :]))
                        v_dv = DV.v
                        for qch in range(8):
                            if qch == 0:
                                bank = 4
                            else:
                                bank = (qch - 1) % 2
                                b.wait("tensor", AC, k_ac)
                            if qch >= 2:
                                b.wait("tensor", AC, AC.v - 2)
                            for dch in range(8):
                                pe(lambda e, bank=bank, qch=qch, dch=dch, m=m, yf=yf: e.matmul(
                                    pb[bank][:, 0:128],
                                    w_at["wq"][:, dch, qch * 128:(qch + 1) * 128],
                                    yf[:, dch, m * 128:(m + 1) * 128],
                                    start=(dch == 0), stop=(dch == 7)),
                                    inc=(1 if dch == 7 else 0))
                            b.wait("scalar", PE)
                            act(lambda e, bank=bank, qch=qch, g=g, m=m: e.activation(
                                Qhold[:, qch, (g * NSUB + m) * 128:(g * NSUB + m + 1) * 128],
                                pb[bank][:, 0:128], AF.Sigmoid))
                        b.wait("tensor", AC)
                        b.wait("tensor", DV, v_dv)
                        firstmn = (g == 0 and m == 0)
                        lastmn = (g == GPS - 1 and m == NSUB - 1)
                        for h in range(HA):
                            first = firstmn and h in (0, 7, 14)
                            last = lastmn and h in (6, 13, 15)
                            pe(lambda e, h=h, first=first, last=last: e.matmul(
                                kvps[0:64, kvcol(h):kvcol(h) + 65],
                                Khold[:, 64 * h:64 * (h + 1)],
                                Vext[:, h, :],
                                start=first, stop=last),
                                inc=(1 if (lastmn and h == HA - 1) else 0))
                    yfm_guard[g % 2] = PE.v
                if gi < GPS:
                    g = gi; t0 = (s * GPS + g) * G
                    b.wait("sync", DV, xgr_guard[0])
                    dma("sync", xgr, ap["xin"][t0:t0 + G, :].rearrange("(m p) c -> p m c", p=128), LD)
                    b.wait("vector", LD)
                    gcol = (s * GPS + g) * NSUB
                    group_ln(xgr, gcol, NSUB, stat_mu, stat_iv)
                    for m in range(NSUB):
                        st_ = t0 + m * 128
                        col = gcol + m
                        b.wait("vector", ST, ybf_guard[0])
                        vec(lambda e, m=m, col=col: e.tensor_scalar(
                            ybf, xgr[:, m, :], stat_mu[:, col:col + 1], stat_iv[:, col:col + 1],
                            op0=ALU.subtract, op1=ALU.mult))
                        b.wait("gpsimd", DV)
                        dma("gpsimd", YB[st_:st_ + 128, :], ybf, ST)
                        ybf_guard[0] = ST.v
                    xgr_guard[0] = DV.v
                    ydone_st[g] = ST.v
            b.wait("vector", PE)
            vec(lambda e: e.memset(KVblk[:, :, :], 0.0), inc=0)
            for h in range(HA):
                q_, r_ = h // 2, h % 2
                vec(lambda e, h=h, q_=q_, r_=r_: e.tensor_copy(
                    KVblk[64 * r_:64 * (r_ + 1), q_, 65 * r_:65 * r_ + 65],
                    kvps[0:64, kvcol(h):kvcol(h) + 65]), inc=(1 if h == HA - 1 else 0))
            kv_dv = DV.v

            odone_st = [0] * GPS
            evdv = [DV.v, DV.v]
            nbuf_ac = [AC.v]
            for gi in range(GPS + 1):
                if gi < GPS:
                    g = gi; t0 = (s * GPS + g) * G
                    b.wait("tensor", DV, kv_dv)
                    for m in range(NSUB):
                        st_ = t0 + m * 128
                        b.wait("vector", AC, nbuf_ac[0])
                        for p_ in range(8):
                            bank = p_ % 2
                            b.wait("tensor", DV, evdv[bank])
                            pe(lambda e, p_=p_, bank=bank, g=g, m=m: e.matmul(
                                pb[bank][:, 0:130],
                                Qhold[:, p_, (g * NSUB + m) * 128:(g * NSUB + m + 1) * 128],
                                KVblk[:, p_, :],
                                start=True, stop=True))
                            b.wait("vector", PE)
                            vec(lambda e, bank=bank, p_=p_: e.tensor_copy(
                                numbuf[:, 130 * p_:130 * p_ + 130], pb[bank][:, 0:130]))
                            evdv[bank] = DV.v
                        nb3 = numbuf[:, 0:1040].rearrange("p (q c) -> p q c", c=130)
                        for j in range(2):
                            vec(lambda e, j=j, nb3=nb3: e.tensor_scalar(
                                den8[:, 8 * j:8 * j + 8], nb3[:, :, 64 + 65 * j:65 + 65 * j],
                                1e-8, None, op0=ALU.add), inc=(j == 1))
                        b.wait("vector", DV)
                        vtiny(lambda e: e.reciprocal(den8[:, 0:16], den8[:, 0:16]))
                        b.wait("scalar", DV)
                        b.wait("scalar", ST, osb_guard[0])
                        for p_ in range(8):
                            for j in range(2):
                                act(lambda e, p_=p_, j=j: e.activation(
                                    o_sb[:, 128 * p_ + 64 * j:128 * p_ + 64 * (j + 1)],
                                    numbuf[:, 130 * p_ + 65 * j:130 * p_ + 65 * j + 64],
                                    AF.Copy, scale=den8[:, 8 * j + p_:8 * j + p_ + 1]),
                                    inc=(1 if (p_ == 7 and j == 1) else 0))
                        nbuf_ac[0] = AC.v
                        b.wait("gpsimd", AC)
                        dma("gpsimd", OB[st_:st_ + 128, :], o_sb, ST)
                        osb_guard[0] = ST.v
                    odone_st[g] = ST.v
                if gi >= 1:
                    g = gi - 1; t0 = (s * GPS + g) * G
                    of = ofm[g % 2]
                    b.wait("sync", ST, odone_st[g])
                    b.wait("sync", PE, ofm_guard[g % 2])
                    for dch in range(8):
                        dmaT("sync", of[:, dch, :], OB[t0:t0 + G, 128 * dch:128 * (dch + 1)], LT)
                    b.wait("sync", DV, xgx_guard[0])
                    dma("sync", xgx, ap["xin"][t0:t0 + G, :].rearrange("(m p) c -> p m c", p=128), LD)
                    b.wait("tensor", LT)
                    gcol = (s * GPS + g) * NSUB
                    for m in range(NSUB):
                        col = gcol + m
                        b.wait("vector", LD)
                        b.wait("vector", ST, xbt_guard[0])
                        vec(lambda e, m=m, col=col: e.tensor_scalar(
                            xbt, xgx[:, m, :], stat_mu[:, col:col + 1], stat_iv[:, col:col + 1],
                            op0=ALU.subtract, op1=ALU.mult))
                        fcdv = DV.v
                        b.wait("tensor", DV, fcdv)
                        for n in range(2):
                            for dch in range(8):
                                pe(lambda e, n=n, dch=dch, m=m, of=of: e.matmul(
                                    pb[2 + n][:, :],
                                    of[:, dch, m * 128:(m + 1) * 128],
                                    w_at["wfc"][:, dch, n * 512:(n + 1) * 512],
                                    start=(dch == 0), stop=(dch == 7)),
                                    inc=(1 if dch == 7 else 0))
                        b.wait("vector", PE)
                        for n in range(2):
                            vec(lambda e, n=n, m=m: e.tensor_add(
                                xgr[:, m, n * 512:(n + 1) * 512], pb[2 + n][:, :],
                                xbt[:, n * 512:(n + 1) * 512]), inc=(1 if n == 1 else 0))
                        xbt_guard[0] = 0
                    group_ln(xgr, 0, NSUB, stat_mu2, stat_iv2)
                    for m in range(NSUB):
                        st_ = t0 + m * 128
                        vec(lambda e, m=m: e.tensor_scalar(
                            junk, xgr[:, m, :], stat_mu2[:, m:m + 1], stat_iv2[:, m:m + 1],
                            op0=ALU.subtract, op1=ALU.mult), inc=0)
                        vec(lambda e: e.tensor_mul(junk, junk, gamma[:, :]), inc=0)
                        b.wait("vector", ST, xbt_guard[0])
                        vec(lambda e, m=m: e.tensor_add(xbt, xgx[:, m, :], beta[:, :]), inc=0)
                        vec(lambda e: e.tensor_add(xbt, xbt, junk))
                        b.wait("gpsimd", DV)
                        dma("gpsimd", XS[st_:st_ + 128, :], xbt, ST)
                        xbt_guard[0] = ST.v
                    xgx_guard[0] = DV.v
                    xgr_guard[0] = DV.v
                    ofm_guard[g % 2] = PE.v

    f_xg   = f32a_t[:, 0:4096].rearrange("p (m c) -> p m c", m=4)
    f_xg2  = f32a_t[:, 4096:8192].rearrange("p (m c) -> p m c", m=4)
    f_hs   = [f32a_t[:, 4096:4612], f32a_t[:, 4612:5128]]
    f_cva  = f32a_t[:, 5128:5644]
    f_h2s  = [f32a_t[:, 5644:6157], f32a_t[:, 6157:6670]]
    f_m1x  = [f32a_t[:, 6670:7183], f32a_t[:, 7183:7696]]
    f_m1n  = [f32a_t[:, 7696:8192][:, 0:496], None]
    f_junk = f32a_t[:, 8192:9216]
    f_xbt  = f32a_t[:, 9216:10240]
    del f_hs, f_cva, f_h2s, f_m1x, f_m1n
    f_hs  = [f32a_t[:, 4096:4612], f32a_t[:, 4612:5128]]
    f_cva = f32a_t[:, 5128:5644]
    f_cva2 = [f32a_t[:, 5128:5644], f32b_t[:, 0:516]]
    f_h2s = f32a_t[:, 5644:6157]
    f_h2s2 = [f32a_t[:, 5644:6157], f32b_t[:, 516:1029]]
    f_m1x = f32a_t[:, 6157:6670]
    f_m1n = f32a_t[:, 6670:7183]
    f_k3x = f32a_t[:, 7183:7311]
    f_k3n = f32a_t[:, 7311:7439]
    f_bc  = f32a_t[:, 7439:7951]
    f_dg  = f32a_t[:, 5128:5640]
    f_h2i = [f32a_t[:, 8192:8704], f32a_t[:, 8704:9216]]
    f_mq  = f32a_t[:, 9216:9728]
    f_bcB = f32a_t[:, 9728:10240]

    halo = per.alloc(64)

    def emit_ffn(phase, src_XS, dst_XS, dbg=None):
        b.wait("gpsimd", PE)
        b.wait("gpsimd", ST)
        w1q = wta_t[:, 0:32768].rearrange("p (c n) -> p c n", c=8)
        for dch in range(8):
            dma("gpsimd", w1q[:, dch, :], WQ1[128 * dch:128 * (dch + 1), :], LW)
        lw_a = LW.v
        xs_st = ST.v
        h1bank_guard = [DV.v] * 6
        xg_guard = [DV.v]
        xg_guard_ac = [AC.v]
        yfm_guard2 = [PE.v, PE.v]
        ybf_guard2 = [ST.v, ST.v]
        ybf2 = bfa_t[:, 17408:18432]
        prevk3 = [None]
        h2st_done = [0]
        h2s_guard = [ST.v, ST.v]
        xbt_st_guard = [ST.v]

        def finish_amax_batch(col0, srcs):
            n = len(srcs)
            b.wait("tensor", DV)
            b.wait("tensor", GP)
            for i, (sx, sn) in enumerate(srcs):
                pe(lambda e, sx=sx, i=i: e.transpose(pb[3][:, 0:128] if False else pb[3][:, 0:128], sx, ident), inc=0) if False else None
            waves = [srcs[i:i + 2] for i in range(0, n, 2)]
            base = 0
            for w_i, wave in enumerate(waves):
                for i, (sx, sn) in enumerate(wave):
                    pe(lambda e, sx=sx, o=(2 * i) * 128: e.transpose(pb[3][:, o:o + 128], sx, ident), inc=0)
                    pe(lambda e, sn=sn, o=(2 * i + 1) * 128: e.transpose(pb[3][:, o:o + 128], sn, ident),
                       inc=(1 if i == len(wave) - 1 else 0))
                b.wait("vector", PE)
                for i in range(len(wave)):
                    j = w_i * 2 + i
                    vec(lambda e, i=i, j=j: e.reduce_max(tiny[:, 16 + j:17 + j], pb[3][:, (2 * i) * 128:(2 * i) * 128 + 128], axis=AX.X), inc=0)
                    vec(lambda e, i=i, j=j: e.tensor_reduce(tiny[:, 21 + j:22 + j], pb[3][:, (2 * i + 1) * 128:(2 * i + 1) * 128 + 128], axis=AX.X, op=ALU.min),
                        inc=(1 if i == len(wave) - 1 else 0))
                if w_i + 1 < len(waves):
                    b.wait("tensor", DV)
            b.wait("vector", DV)
            xs_ = tiny[:, 16:16 + n]; ns_ = tiny[:, 21:21 + n]
            vtiny(lambda e, xs_=xs_, ns_=ns_: e.scalar_tensor_tensor(xs_, ns_, -1.0, xs_, op0=ALU.mult, op1=ALU.max))
            vtiny(lambda e, xs_=xs_: e.tensor_scalar(xs_, xs_, 1e-5, None, op0=ALU.max))
            vtiny(lambda e, col0=col0, n=n, xs_=xs_: e.tensor_scalar(rAq[:, col0:col0 + n], xs_, invws2[:, 0:1], 1.0 / 127.0, op0=ALU.mult, op1=ALU.mult))
            vtiny(lambda e, ns_=ns_, xs_=xs_: e.reciprocal(ns_, xs_))
            vtiny(lambda e, col0=col0, n=n, ns_=ns_: e.tensor_scalar(Aq[:, col0:col0 + n], ns_, 127.0, None, op0=ALU.mult))

        for gi in range(NG + 1):
            if gi >= 1:
                g = gi - 1; t0 = g * G
                gcol = g * NSUB
                seq_start = (g % GPS == 0)
                seq_end = (g % GPS == GPS - 1)
                w = 511 if seq_start else (513 if seq_end else 512)
                wstart_off = 0 if seq_start else 1
                yf = yfm[g % 2]
                b.wait("sync", ST)
                b.wait("sync", PE, yfm_guard2[g % 2])
                for dch in range(8):
                    dmaT("sync", yf[:, dch, :], YB[t0:t0 + G, 128 * dch:128 * (dch + 1)], LT)
                b.wait("vector", PE)
                b.wait("vector", AC)
                for m in range(NSUB):
                    vec(lambda e, m=m, gcol=gcol: e.tensor_scalar(
                        f_dg[:, 128 * m:128 * (m + 1)], ident[:, :], rabq[:, gcol + m:gcol + m + 1], None,
                        op0=ALU.mult), inc=(m == NSUB - 1))
                b.wait("tensor", DV)
                b.wait("tensor", LT)
                b.wait("tensor", LW, lw_a)
                for m in range(NSUB):
                    pe(lambda e, m=m: e.matmul(pb[2][:, 128 * m:128 * (m + 1)], onesf[:, :], f_dg[:, 128 * m:128 * (m + 1)], start=True, stop=True), inc=(m == NSUB - 1))
                b.wait("vector", PE)
                vec(lambda e: e.tensor_copy(f_bc, pb[2][:, :]))
                if seq_start:
                    vec(lambda e: e.memset(halo[:, :], 0.0))
                b.wait("gpsimd", PE)
                b.wait("gpsimd", DV)
                gp(lambda e: e.memset(f_m1x, -3.0e38), inc=0)
                gp(lambda e: e.memset(f_m1n, 3.0e38))
                b.wait("vector", GP)
                silu_ac = [AC.v, AC.v]
                m1_dv = [DV.v, DV.v]
                h1banks = [pb[0][:, :], pb[1][:, :], pb[4][:, :],
                           kvps[:, 0:512], kvps[:, 512:1024], kvps[:, 1024:1536]]
                m1_pend = [None]
                halo_pend = [None]
                gp_done = {}
                gsilu_ac = [AC.v]
                for c in range(32):
                    is_gp = False
                    hbk = h1banks[c % 6]
                    hsb = f_hs[c % 2]
                    b.wait("tensor", DV, h1bank_guard[c % 6])
                    for dch in range(8):
                        pe(lambda e, hbk=hbk, c=c, dch=dch, yf=yf: e.matmul(
                            hbk,
                            w1q[:, dch, 128 * c:128 * (c + 1)],
                            yf[:, dch, :],
                            start=(dch == 0), stop=(dch == 7)),
                            inc=(1 if dch == 7 else 0))
                    if m1_pend[0] is not None:
                        m1_pend[0](); m1_pend[0] = None
                    if halo_pend[0] is not None:
                        halo_pend[0](); halo_pend[0] = None
                    if (c - 2) in gp_done:
                        b.wait("vector", GP, gp_done.pop(c - 2))
                    vec(lambda e, hsb=hsb, c=c: e.tensor_copy(hsb[:, 0:2], halo[:, 2 * c:2 * c + 2]), inc=0)
                    if seq_end:
                        vec(lambda e, hsb=hsb: e.memset(hsb[:, 514:516], 0.0), inc=0)
                    b.wait("vector", PE)
                    vec(lambda e, hsb=hsb, hbk=hbk: e.tensor_mul(hsb[:, 2:514], hbk, f_bc))
                    h1bank_guard[c % 6] = DV.v
                    halo_pend[0] = (lambda hsb=hsb, c=c: vec(
                        lambda e: e.tensor_copy(halo[:, 2 * c:2 * c + 2], hsb[:, 512:514])))
                    h2s = f_h2s2[c % 2]
                    lo = 1 if seq_start else 0
                    if is_gp:
                        gA = gcv_t[:, 0:516]; gB = gcv_t[:, 516:1032]
                        b.wait("gpsimd", DV)
                        b.wait("gpsimd", AC, gsilu_ac[0])
                        gp(lambda e, hsb=hsb, c=c, lo=lo, w=w, gA=gA: e.tensor_scalar(
                            gA[:, 0:w], hsb[:, lo + 1:lo + 1 + w], cw[1][:, c:c + 1], None, op0=ALU.mult), inc=0)
                        gp(lambda e, hsb=hsb, c=c, lo=lo, w=w, gB=gB: e.tensor_scalar(
                            gB[:, 0:w], hsb[:, lo:lo + w], cw[0][:, c:c + 1], None, op0=ALU.mult), inc=0)
                        gp(lambda e, w=w, gA=gA, gB=gB: e.tensor_tensor(gA[:, 0:w], gA[:, 0:w], gB[:, 0:w], op=ALU.add), inc=0)
                        gp(lambda e, hsb=hsb, c=c, lo=lo, w=w, gB=gB: e.tensor_scalar(
                            gB[:, 0:w], hsb[:, lo + 2:lo + 2 + w], cw[2][:, c:c + 1], None, op0=ALU.mult), inc=0)
                        gp(lambda e, w=w, gA=gA, gB=gB: e.tensor_tensor(gA[:, 0:w], gA[:, 0:w], gB[:, 0:w], op=ALU.add))
                        gp_done[c] = GP.v
                        b.wait("scalar", GP)
                        b.wait("scalar", ST, h2s_guard[c % 2])
                        b.wait("scalar", DV, m1_dv[c % 2])
                        act(lambda e, h2s=h2s, c=c, w=w, gA=gA: e.activation(h2s[:, 0:w], gA[:, 0:w], AF.Silu, bias=cbv[:, c:c + 1]))
                        gsilu_ac[0] = AC.v
                    else:
                        cva = f_cva2[c % 2]
                        b.wait("vector", AC, silu_ac[c % 2])
                        vec(lambda e, hsb=hsb, cva=cva, c=c, lo=lo, w=w: e.tensor_scalar(
                            cva[:, 0:w], hsb[:, lo + 1:lo + 1 + w], cw[1][:, c:c + 1], None, op0=ALU.mult), inc=0)
                        vec(lambda e, hsb=hsb, cva=cva, c=c, lo=lo, w=w: e.scalar_tensor_tensor(
                            cva[:, 0:w], hsb[:, lo:lo + w], cw[0][:, c:c + 1], cva[:, 0:w],
                            op0=ALU.mult, op1=ALU.add), inc=0)
                        vec(lambda e, hsb=hsb, cva=cva, c=c, lo=lo, w=w: e.scalar_tensor_tensor(
                            cva[:, 0:w], hsb[:, lo + 2:lo + 2 + w], cw[2][:, c:c + 1], cva[:, 0:w],
                            op0=ALU.mult, op1=ALU.add))
                        b.wait("scalar", DV)
                        b.wait("scalar", ST, h2s_guard[c % 2])
                        b.wait("scalar", DV, m1_dv[c % 2])
                        act(lambda e, cva=cva, h2s=h2s, c=c, w=w: e.activation(h2s[:, 0:w], cva[:, 0:w], AF.Silu, bias=cbv[:, c:c + 1]))
                        silu_ac[c % 2] = AC.v
                    def m1_emit(h2s=h2s, w=w, cc=c, ac_at=AC.v, gp_at=GP.v):
                        b.wait("vector", AC, ac_at)
                        b.wait("vector", GP, gp_at)
                        vec(lambda e: e.tensor_tensor(f_m1x[:, 0:w], f_m1x[:, 0:w], h2s[:, 0:w], op=ALU.max), inc=0)
                        vec(lambda e: e.tensor_tensor(f_m1n[:, 0:w], f_m1n[:, 0:w], h2s[:, 0:w], op=ALU.min))
                        m1_dv[cc % 2] = DV.v
                    m1_pend[0] = m1_emit
                    b.wait("sync", AC)
                    tkn0 = t0 - wstart_off
                    dma("sync", H2[128 * c:128 * (c + 1), tkn0:tkn0 + w], h2s[:, 0:w], ST)
                    h2s_guard[c % 2] = ST.v
                if m1_pend[0] is not None:
                    m1_pend[0](); m1_pend[0] = None
                if halo_pend[0] is not None:
                    halo_pend[0](); halo_pend[0] = None
                h2st_done[0] = ST.v
                batch = []
                col0 = gcol
                b.wait("vector", GP)
                if prevk3[0] is not None:
                    if not seq_start:
                        vtiny(lambda e: e.tensor_copy(f_k3x[:, 127:128], f_m1x[:, 0:1]))
                        vtiny(lambda e: e.tensor_copy(f_k3n[:, 127:128], f_m1n[:, 0:1]))
                    col0 = prevk3[0][0]
                    batch.append((f_k3x, f_k3n))
                    prevk3[0] = None
                for k in range(3):
                    sl = (128 * k + (0 if seq_start else 1))
                    batch.append((f_m1x[:, sl:sl + 128], f_m1n[:, sl:sl + 128]))
                sl3 = 128 * 3 + (0 if seq_start else 1)
                if seq_end:
                    batch.append((f_m1x[:, sl3:sl3 + 128], f_m1n[:, sl3:sl3 + 128]))
                else:
                    vtiny(lambda e, sl3=sl3: e.tensor_copy(f_k3x[:, 0:127], f_m1x[:, sl3:sl3 + 127]))
                    vtiny(lambda e, sl3=sl3: e.tensor_copy(f_k3n[:, 0:127], f_m1n[:, sl3:sl3 + 127]))
                    prevk3[0] = (gcol + 3,)
                finish_amax_batch(col0, batch)
                yfm_guard2[g % 2] = PE.v

            if gi < NG:
                g = gi; t0 = g * G
                gcol = g * NSUB
                b.wait("sync", DV, xg_guard[0])
                b.wait("sync", AC, xg_guard_ac[0] if isinstance(xg_guard_ac, list) else 0)
                b.wait("sync", ST, xs_st)
                dma("sync", f_xg[:, :, :], src_XS[t0:t0 + G, :].rearrange("(m p) c -> p m c", p=128), LD)
                b.wait("vector", LD)
                group_ln(f_xg, gcol, NSUB, stat_mu, stat_iv)
                for m in range(NSUB):
                    col = gcol + m
                    b.wait("vector", ST, xbt_st_guard[0])
                    vec(lambda e, m=m, col=col: e.tensor_scalar(
                        f_xbt, f_xg[:, m, :], stat_mu[:, col:col + 1], None, op0=ALU.subtract), inc=0)
                    vec(lambda e, m=m: e.reduce_max(tiny[:, m:m + 1], f_xbt, axis=AX.X, apply_absolute_value=True))
                    b.wait("vector", DV)
                iv4 = stat_iv[:, gcol:gcol + 4]
                vtiny(lambda e, iv4=iv4: e.tensor_mul(tiny[:, 4:8], tiny[:, 0:4], iv4))
                vtiny(lambda e: e.tensor_scalar(tiny[:, 4:8], tiny[:, 4:8], 1e-5, None, op0=ALU.max))
                vtiny(lambda e, gcol=gcol: e.tensor_scalar(rabq[:, gcol:gcol + 4], tiny[:, 4:8], invws1[:, 0:1], 1.0 / 127.0, op0=ALU.mult, op1=ALU.mult))
                vtiny(lambda e: e.reciprocal(tiny[:, 8:12], tiny[:, 4:8]))
                vtiny(lambda e: e.tensor_scalar(tiny[:, 8:12], tiny[:, 8:12], 127.0, None, op0=ALU.mult))
                vtiny(lambda e, gcol=gcol, iv4=iv4: e.tensor_mul(aq[:, gcol:gcol + 4], tiny[:, 8:12], iv4))
                vtiny(lambda e, gcol=gcol: e.tensor_mul(tiny[:, 0:4], stat_mu[:, gcol:gcol + 4], aq[:, gcol:gcol + 4]))
                vtiny(lambda e: e.tensor_scalar(tiny[:, 0:4], tiny[:, 0:4], -1.0, None, op0=ALU.mult))
                b.wait("scalar", DV)
                for m in range(NSUB):
                    st_ = t0 + m * 128
                    col = gcol + m
                    yb2 = ybf if m % 2 == 0 else ybf2
                    b.wait("scalar", ST, ybf_guard2[m % 2])
                    act(lambda e, m=m, col=col: e.activation(
                        i32_t[:, 0:D], f_xg[:, m, :], AF.Identity,
                        scale=aq[:, col:col + 1], bias=tiny[:, m:m + 1]), inc=0)
                    act(lambda e, yb2=yb2: e.activation(yb2, i32_t[:, 0:D], AF.Copy))
                    b.wait("gpsimd", AC)
                    dma("gpsimd", YB[st_:st_ + 128, :], yb2, ST)
                    ybf_guard2[m % 2] = ST.v
                xg_guard[0] = DV.v
                xg_guard_ac[0] = AC.v
        b.wait("gpsimd", PE)
        w2q = wta_t[:, 0:32768].rearrange("p (c n) -> p c n", c=32)
        for c2 in range(0, 32, 4):
            dma("gpsimd", w2q[:, c2:c2 + 4, :], WQ2[128 * c2:128 * (c2 + 4), :].rearrange("(c p) n -> p c n", p=128), LW)
        lw_b = LW.v
        hq = bfa_t[:, 16384:32768].rearrange("p (c n) -> p c n", c=32)
        if dbg == "W2Q":
            b.wait("gpsimd", LW)
            dma("gpsimd", H2[0:4096, 0:1024].rearrange("(x p) c -> p x c", p=128), wta_t[:, :].rearrange("p (x c) -> p x c", x=32), ST)
        b.wait("sync", ST, h2st_done[0])
        xo_guard = [ST.v]
        x2_guard = [DV.v]
        for g in range(NG):
            t0 = g * G
            gcol = g * NSUB
            b.wait("vector", PE)
            b.wait("vector", AC)
            for m in range(NSUB):
                vec(lambda e, m=m, gcol=gcol: e.tensor_scalar(
                    f_mq[:, 128 * m:128 * (m + 1)], ident[:, :], Aq[:, gcol + m:gcol + m + 1], None,
                    op0=ALU.mult), inc=(m == NSUB - 1))
            b.wait("tensor", DV)
            b.wait("tensor", ST, h2st_done[0])
            b.wait("tensor", LW, lw_b)
            for m in range(NSUB):
                pe(lambda e, m=m: e.matmul(pb[2][:, 128 * m:128 * (m + 1)], onesf[:, :], f_mq[:, 128 * m:128 * (m + 1)], start=True, stop=True), inc=(m == NSUB - 1))
            b.wait("vector", PE)
            vec(lambda e: e.tensor_copy(f_bcB, pb[2][:, :]))
            b.wait("sync", DV, x2_guard[0])
            b.wait("sync", ST, xs_st)
            dma("sync", f_xg2[:, :, :], src_XS[t0:t0 + G, :].rearrange("(m p) c -> p m c", p=128), LD)
            hq_gp = [0] * 32
            b.wait("gpsimd", DV)
            mq2 = [f_mq, f32b_t[:, 0:512]]
            mq_ac = [AC.v, AC.v]
            for c in range(32):
                hb = f_h2i[c % 2]
                mq = mq2[c % 2]
                b.wait("sync", GP, hq_gp[c - 2] if c >= 2 else 0)
                dma("sync", hb, H2[128 * c:128 * (c + 1), t0:t0 + G], LD)
                b.wait("gpsimd", LD)
                b.wait("gpsimd", AC, mq_ac[c % 2])
                gp(lambda e, hb=hb, mq=mq: e.tensor_tensor(mq, hb, f_bcB, op=ALU.mult))
                hq_gp[c] = GP.v
                b.wait("scalar", GP)
                act(lambda e, mq=mq: e.activation(i32_t[:, 0:G], mq, AF.Copy), inc=0)
                act(lambda e, c=c: e.activation(hq[:, c, :], i32_t[:, 0:G], AF.Copy))
                mq_ac[c % 2] = AC.v
            if dbg == "HQ" and g == 0:
                b.wait("gpsimd", DV)
                dma("gpsimd", OB[0:2048, :].rearrange("(x p) c -> p x c", p=128), hq[:, :, :].rearrange("p c t -> p (c t)").rearrange("p (x c) -> p x c", x=16), ST)
            bankmap = [0, 1, 3, 4]
            b.wait("vector", LD)
            for n in range(2):
                for m in range(NSUB):
                    bk = bankmap[m]
                    b.wait("tensor", AC)
                    b.wait("tensor", DV)
                    for c in range(32):
                        pe(lambda e, n=n, m=m, c=c, bk=bk: e.matmul(
                            pb[bk][:, :],
                            hq[:, c, 128 * m:128 * (m + 1)],
                            w2q[:, c, 512 * n:512 * (n + 1)],
                            start=(c == 0), stop=(c == 31)),
                            inc=(1 if c == 31 else 0))
                    b.wait("vector", PE)
                    b.wait("vector", ST, xo_guard[0])
                    vec(lambda e, n=n, m=m, bk=bk, gcol=gcol: e.scalar_tensor_tensor(
                        f_xg[:, m, 512 * n:512 * (n + 1)], pb[bk][:, :], rAq[:, gcol + m:gcol + m + 1],
                        f_xg2[:, m, 512 * n:512 * (n + 1)], op0=ALU.mult, op1=ALU.add))
            x2_guard[0] = DV.v
            b.wait("gpsimd", DV)
            for m in range(NSUB):
                st_ = t0 + m * 128
                dma("gpsimd", dst_XS[st_:st_ + 128, :], f_xg[:, m, :], ST)
            xo_guard[0] = ST.v

    if n_sublayers >= 2:
        emit_ffn(2, XS, XS if n_sublayers > 2 else XS, dbg=dbg)

    HB, DKB = 32, 32
    if n_sublayers >= 3:
        b.wait("gpsimd", PE)
        wta3 = Arena(wta_t, 32768)
        wqkv_sb = wta3.alloc(8 * 3 * D).rearrange("p (c n) -> p c n", c=8)
        wproj_sb = wta3.alloc(8 * D).rearrange("p (c n) -> p c n", c=8)
        for dch in range(8):
            dma("gpsimd", wqkv_sb[:, dch, :], ap["wqkvT"][128 * dch:128 * (dch + 1), :], LW)
            dma("gpsimd", wproj_sb[:, dch, :], ap["wprojT"][128 * dch:128 * (dch + 1), :], LW)
        p3_lw = LW.v
        bfa3 = Arena(bfa_t, 38400)
        yfm3 = [bfa3.alloc(8 * G).rearrange("p (c n) -> p c n", c=8) for _ in range(2)]
        ofm3 = [bfa3.alloc(8 * G).rearrange("p (c n) -> p c n", c=8) for _ in range(2)]
        k2_sb = bfa3.alloc(D)
        v2ext = bfa3.alloc(HB * 33).rearrange("p (h c) -> p h c", h=HB)
        Q2hold = bfa3.alloc(8 * SEQ).rearrange("p (c n) -> p c n", c=8)
        vkblk = bfa3.alloc(8 * 132).rearrange("p (q c) -> p q c", q=8)
        o2_sb = bfa3.alloc(D)
        ybf3 = bfa3.alloc(D)

        def vk_col(h):
            b_ = min(h // 15, 2)
            return 512 * b_ + 33 * (h - 15 * b_)

        xs_st3 = ST.v
        xgr_guard = [DV.v]
        xgx_guard = [DV.v]
        yfm_guard = [PE.v, PE.v]
        ofm_guard = [PE.v, PE.v]
        ybf_guard = [ST.v]
        osb_guard = [ST.v]
        xbt_guard = [ST.v]
        first_phase = [True]
        for s in range(NSEQ):
            ydone_st = [0] * GPS
            for gi in range(GPS + 1):
                if gi >= 1:
                    g = gi - 1; t0 = (s * GPS + g) * G
                    yf = yfm3[g % 2]
                    b.wait("sync", ST, ydone_st[g])
                    b.wait("sync", PE, yfm_guard[g % 2])
                    for dch in range(8):
                        dmaT("sync", yf[:, dch, :], YB[t0:t0 + G, 128 * dch:128 * (dch + 1)], LT)
                    b.wait("tensor", LT)
                    b.wait("tensor", LW, p3_lw)
                    if first_phase[0] and g == 0:
                        b.wait("vector", PE)
                        vec(lambda e: e.memset(v2ext[:, :, 32:33], 1.0))
                        b.wait("tensor", DV)
                        first_phase[0] = False
                    for m in range(NSUB):
                        for off, bank in ((D, 0), (2 * D, 2)):
                            for n in range(2):
                                for dch in range(8):
                                    pe(lambda e, off=off, bank=bank, n=n, dch=dch, m=m, yf=yf: e.matmul(
                                        pb[bank + n][:, :],
                                        yf[:, dch, m * 128:(m + 1) * 128],
                                        wqkv_sb[:, dch, off + n * 512:off + (n + 1) * 512],
                                        start=(dch == 0), stop=(dch == 7)),
                                        inc=(1 if dch == 7 else 0))
                        kv_pe = PE.v
                        b.wait("scalar", PE, kv_pe)
                        for n in range(2):
                            act(lambda e, n=n: e.activation(k2_sb[:, n * 512:(n + 1) * 512], pb[n][:, :], AF.Relu))
                        k_ac = AC.v
                        b.wait("vector", PE, kv_pe)
                        for n in range(2):
                            vec(lambda e, n=n: e.tensor_copy(
                                v2ext[:, 16 * n:16 * (n + 1), 0:32],
                                pb[2 + n][:, :]))
                        v_dv = DV.v
                        for qch in range(8):
                            if qch == 0:
                                bank = 4
                            else:
                                bank = (qch - 1) % 2
                                b.wait("tensor", AC, k_ac)
                            if qch >= 2:
                                b.wait("tensor", AC, AC.v - 2)
                            for dch in range(8):
                                pe(lambda e, bank=bank, qch=qch, dch=dch, m=m, yf=yf: e.matmul(
                                    pb[bank][:, 0:128],
                                    wqkv_sb[:, dch, qch * 128:(qch + 1) * 128],
                                    yf[:, dch, m * 128:(m + 1) * 128],
                                    start=(dch == 0), stop=(dch == 7)),
                                    inc=(1 if dch == 7 else 0))
                            b.wait("scalar", PE)
                            act(lambda e, bank=bank, qch=qch, g=g, m=m: e.activation(
                                Q2hold[:, qch, (g * NSUB + m) * 128:(g * NSUB + m + 1) * 128],
                                pb[bank][:, 0:128], AF.Relu))
                        b.wait("tensor", AC)
                        b.wait("tensor", DV, v_dv)
                        firstmn = (g == 0 and m == 0)
                        lastmn = (g == GPS - 1 and m == NSUB - 1)
                        for h in range(HB):
                            first = firstmn and h in (0, 15, 30)
                            last = lastmn and h in (14, 29, 31)
                            pe(lambda e, h=h, first=first, last=last: e.matmul(
                                kvps[0:32, vk_col(h):vk_col(h) + 33],
                                k2_sb[:, 32 * h:32 * (h + 1)],
                                v2ext[:, h, :],
                                start=first, stop=last),
                                inc=(1 if (lastmn and h == HB - 1) else 0))
                    yfm_guard[g % 2] = PE.v
                if gi < GPS:
                    g = gi; t0 = (s * GPS + g) * G
                    b.wait("sync", DV, xgr_guard[0])
                    b.wait("sync", ST, xs_st3)
                    dma("sync", xgr, XS[t0:t0 + G, :].rearrange("(m p) c -> p m c", p=128), LD)
                    b.wait("vector", LD)
                    gcol = (s * GPS + g) * NSUB
                    group_ln(xgr, gcol, NSUB, stat_mu, stat_iv)
                    for m in range(NSUB):
                        st_ = t0 + m * 128
                        col = gcol + m
                        b.wait("vector", ST, ybf_guard[0])
                        vec(lambda e, m=m, col=col: e.tensor_scalar(
                            ybf3, xgr[:, m, :], stat_mu[:, col:col + 1], stat_iv[:, col:col + 1],
                            op0=ALU.subtract, op1=ALU.mult))
                        b.wait("gpsimd", DV)
                        dma("gpsimd", YB[st_:st_ + 128, :], ybf3, ST)
                        ybf_guard[0] = ST.v
                    xgr_guard[0] = DV.v
                    ydone_st[g] = ST.v
            b.wait("vector", PE)
            vec(lambda e: e.memset(vkblk[:, :, :], 0.0), inc=0)
            for h in range(HB):
                q_, r_ = h // 4, h % 4
                vec(lambda e, h=h, q_=q_, r_=r_: e.tensor_copy(
                    vkblk[32 * r_:32 * (r_ + 1), q_, 33 * r_:33 * r_ + 33],
                    kvps[0:32, vk_col(h):vk_col(h) + 33]), inc=(1 if h == HB - 1 else 0))
            kv_dv = DV.v

            odone_st = [0] * GPS
            evdv = [DV.v, DV.v]
            nbuf_ac = [AC.v]
            for gi in range(GPS + 1):
                if gi < GPS:
                    g = gi; t0 = (s * GPS + g) * G
                    b.wait("tensor", DV, kv_dv)
                    for m in range(NSUB):
                        st_ = t0 + m * 128
                        b.wait("vector", AC, nbuf_ac[0])
                        for q_ in range(8):
                            bank = q_ % 2
                            b.wait("tensor", DV, evdv[bank])
                            pe(lambda e, q_=q_, bank=bank, g=g, m=m: e.matmul(
                                pb[bank][:, 0:132],
                                Q2hold[:, q_, (g * NSUB + m) * 128:(g * NSUB + m + 1) * 128],
                                vkblk[:, q_, :],
                                start=True, stop=True))
                            b.wait("vector", PE)
                            vec(lambda e, bank=bank, q_=q_: e.tensor_copy(
                                numbuf[:, 132 * q_:132 * q_ + 132], pb[bank][:, 0:132]))
                            evdv[bank] = DV.v
                        nb3 = numbuf[:, 0:1056].rearrange("p (q c) -> p q c", c=132)
                        for j in range(4):
                            vec(lambda e, j=j, nb3=nb3: e.tensor_scalar(
                                den8[:, 8 * j:8 * j + 8], nb3[:, :, 33 * j + 32:33 * j + 33],
                                1e-15, None, op0=ALU.add), inc=(j == 3))
                        b.wait("vector", DV)
                        vtiny(lambda e: e.reciprocal(den8[:, 0:32], den8[:, 0:32]))
                        b.wait("scalar", DV)
                        b.wait("scalar", ST, osb_guard[0])
                        for q_ in range(8):
                            for j in range(4):
                                act(lambda e, q_=q_, j=j: e.activation(
                                    o2_sb[:, 128 * q_ + 32 * j:128 * q_ + 32 * (j + 1)],
                                    numbuf[:, 132 * q_ + 33 * j:132 * q_ + 33 * j + 32],
                                    AF.Copy, scale=den8[:, 8 * j + q_:8 * j + q_ + 1]),
                                    inc=(1 if (q_ == 7 and j == 3) else 0))
                        nbuf_ac[0] = AC.v
                        b.wait("gpsimd", AC)
                        dma("gpsimd", OB[st_:st_ + 128, :], o2_sb, ST)
                        osb_guard[0] = ST.v
                    odone_st[g] = ST.v
                if gi >= 1:
                    g = gi - 1; t0 = (s * GPS + g) * G
                    of = ofm3[g % 2]
                    b.wait("sync", ST, odone_st[g])
                    b.wait("sync", PE, ofm_guard[g % 2])
                    for dch in range(8):
                        dmaT("sync", of[:, dch, :], OB[t0:t0 + G, 128 * dch:128 * (dch + 1)], LT)
                    b.wait("sync", DV, xgx_guard[0])
                    b.wait("sync", ST, xs_st3)
                    dma("sync", xgx, XS[t0:t0 + G, :].rearrange("(m p) c -> p m c", p=128), LD)
                    b.wait("tensor", LT)
                    for m in range(NSUB):
                        st_ = t0 + m * 128
                        fcdv = DV.v
                        b.wait("tensor", DV, fcdv)
                        for n in range(2):
                            for dch in range(8):
                                pe(lambda e, n=n, dch=dch, m=m, of=of: e.matmul(
                                    pb[2 + n][:, :],
                                    of[:, dch, m * 128:(m + 1) * 128],
                                    wproj_sb[:, dch, n * 512:(n + 1) * 512],
                                    start=(dch == 0), stop=(dch == 7)),
                                    inc=(1 if dch == 7 else 0))
                        b.wait("vector", LD)
                        b.wait("vector", PE)
                        b.wait("vector", ST, xbt_guard[0])
                        for n in range(2):
                            vec(lambda e, n=n, m=m: e.tensor_add(
                                xbt[:, n * 512:(n + 1) * 512], pb[2 + n][:, :],
                                bproj[:, n * 512:(n + 1) * 512]), inc=0)
                        vec(lambda e, m=m: e.tensor_add(xbt, xbt, xgx[:, m, :]))
                        b.wait("gpsimd", DV)
                        dma("gpsimd", XS[st_:st_ + 128, :], xbt, ST)
                        xbt_guard[0] = ST.v
                    xgx_guard[0] = DV.v
                    ofm_guard[g % 2] = PE.v
            xgr_guard[0] = DV.v

    if n_sublayers >= 4:
        emit_ffn(4, XS, out, dbg=None)

    if n_sublayers < 4 or dbg is not None:
        b.wait("gpsimd", ST)
        b.wait("gpsimd", DV)
        b.wait("gpsimd", AC)
        if dbg == "PER":
            for i in range(3):
                dma("gpsimd", out[128 * i:128 * (i + 1), :], per_t[:, 1024 * i:1024 * (i + 1)], ST)
            dma("gpsimd", out[384:512, 0:3840 - 3072], per_t[:, 3072:3840], ST)
            b.wait("gpsimd", ST)
            srcT = None
        else:
            srcT = {None: XS, "XS": XS, "YB": YB, "OB": OB, "HQ": OB, "H2": H2, "W2Q": H2, "WQ1": WQ1, "WQ2": WQ2}[dbg]
        nrow = srcT.tensor.shape[0] if srcT is not None else 0
        for i in range(nrow // 128):
            dma("gpsimd", wld[i % 2][:, 0:D], srcT[128 * i:128 * (i + 1), 0:D], LD)
            b.wait("gpsimd", LD)
            dma("gpsimd", out[128 * i:128 * (i + 1), :], wld[i % 2][:, 0:D], ST)
        b.wait("gpsimd", ST)
    else:
        b.wait("gpsimd", ST)

    with nc.Block() as blk:
        for eng in B.ENGS:
            ops = b.q[eng]
            getattr(blk, eng)(lambda e, ops=ops: [f(e) for f in ops])
    es.close()
    return nc


def kernel(**inputs):
    per_core = host_prep(inputs)
    nc = build_program(n_sublayers=4)
    res = run_bass_kernel_spmd(nc, per_core, core_ids=list(range(NCORES)))
    return np.concatenate([r["out"] for r in res.results], axis=0).reshape(16, 2048, 1024)


# version 09
# speedup vs baseline: 1.0255x; 1.0255x over previous
import numpy as np
import ml_dtypes
import concourse.bass as bass
import concourse.mybir as mybir
from concourse.bass_utils import run_bass_kernel_spmd
from contextlib import ExitStack

F32 = mybir.dt.float32
BF16 = mybir.dt.bfloat16
I32 = mybir.dt.int32
AF = mybir.ActivationFunctionType
ALU = mybir.AluOpType
AX = mybir.AxisListType
bf16 = ml_dtypes.bfloat16

NCORES = 8
SEQ = 2048
NSEQ = 2
T = SEQ * NSEQ
D = 1024
DH = 4096
G = 512
NG = T // G
GPS = SEQ // G
NSUB = 4
HA, DKA = 16, 64
LN_EPS = 1e-5

class Ctr:
    def __init__(self, handle):
        self.h = handle; self.v = 0

class B:
    ENGS = ("sync", "gpsimd", "scalar", "vector", "tensor")
    def __init__(self):
        self.q = {e: [] for e in self.ENGS}
    def wait(self, eng, ctr, val=None):
        v = ctr.v if val is None else val
        if v <= 0: return
        h = ctr.h
        self.q[eng].append(lambda e: e.wait_ge(h, v))
    def _inc(self, eng, fn, ctr, amt):
        if ctr is None:
            self.q[eng].append(fn)
        else:
            h = ctr.h
            self.q[eng].append(lambda e: fn(e).then_inc(h, amt))
            ctr.v += amt

class Arena:
    def __init__(self, tens, width):
        self.t = tens; self.width = width; self.off = 0
    def alloc(self, n):
        assert self.off + n <= self.width, f"arena overflow {self.off}+{n}>{self.width}"
        v = self.t[:, self.off:self.off + n]
        self.off += n
        return v

def host_prep(inputs):
    x = np.asarray(inputs["L0_attn_a_inputs"])
    d = {}
    d["wq"] = np.ascontiguousarray(inputs["L0_attn_a_W_Q"])
    d["wk"] = np.ascontiguousarray(inputs["L0_attn_a_W_K"])
    d["wv"] = np.ascontiguousarray(inputs["L0_attn_a_W_V"])
    d["wfc"] = np.ascontiguousarray(inputs["L0_attn_a_W_fc"])
    d["gamma_bc"] = np.ascontiguousarray(np.broadcast_to(np.asarray(inputs["L0_attn_a_gamma"])[None, :], (128, D)))
    d["beta_bc"] = np.ascontiguousarray(np.broadcast_to(np.asarray(inputs["L0_attn_a_beta"])[None, :], (128, D)))
    d["wqkvT"] = np.ascontiguousarray(np.asarray(inputs["L0_attn_b_W_qkv"]).T)
    d["wprojT"] = np.ascontiguousarray(np.asarray(inputs["L0_attn_b_W_proj"]).T)
    d["bproj_bc"] = np.ascontiguousarray(np.broadcast_to(np.asarray(inputs["L0_attn_b_b_proj"])[None, :], (128, D)))
    d["w1T"] = np.ascontiguousarray(np.asarray(inputs["L0_ffn_w1"]).T)
    d["w2T"] = np.ascontiguousarray(np.asarray(inputs["L0_ffn_w2"]).T)
    cwa = np.asarray(inputs["L0_ffn_conv_w"])
    for j in range(3):
        d[f"cw{j}"] = np.ascontiguousarray(cwa[:, 0, j].reshape(32, 128).T)
    d["cb"] = np.ascontiguousarray(np.asarray(inputs["L0_ffn_conv_b"]).reshape(32, 128).T)
    d["ident"] = np.eye(128, dtype=np.float32)
    per_core = []
    for c in range(NCORES):
        m = dict(d)
        m["xin"] = np.ascontiguousarray(x[2 * c:2 * c + 2].reshape(T, D))
        per_core.append(m)
    return per_core


def build_program(n_sublayers=4, dbg=None):
    nc = bass.Bass("TRN2", target_bir_lowering=False)
    es = ExitStack()
    ap = {}
    def din(name, shape, dt=F32):
        ap[name] = nc.dram_tensor(name, list(shape), dt, kind="ExternalInput").ap()
    din("xin", (T, D))
    for w in ("wq", "wk", "wv", "wfc"):
        din(w, (D, D))
    din("gamma_bc", (128, D)); din("beta_bc", (128, D))
    din("wqkvT", (D, 3 * D)); din("wprojT", (D, D)); din("bproj_bc", (128, D))
    din("w1T", (D, DH)); din("w2T", (DH, D))
    for j in range(3):
        din(f"cw{j}", (128, 32))
    din("cb", (128, 32))
    din("ident", (128, 128))
    out = nc.dram_tensor("out", [T, D], F32, kind="ExternalOutput").ap()
    XS = nc.dram_tensor("XS", [T, D], F32, kind="Internal").ap()
    YB = nc.dram_tensor("YB", [T, D], BF16, kind="Internal").ap()
    OB = nc.dram_tensor("OB", [T, D], BF16, kind="Internal").ap()
    H2 = nc.dram_tensor("H2", [DH, T], F32, kind="Internal").ap()
    WQ1 = nc.dram_tensor("WQ1", [D, DH], BF16, kind="Internal").ap()
    WQ2 = nc.dram_tensor("WQ2", [DH, D], BF16, kind="Internal").ap()

    bfa_t = es.enter_context(nc.sbuf_tensor("bfa", [128, 38400], BF16))
    wta_t = es.enter_context(nc.sbuf_tensor("wta", [128, 32768], BF16))
    f32a_t = es.enter_context(nc.sbuf_tensor("f32a", [128, 10240], F32))
    per_t = es.enter_context(nc.sbuf_tensor("per", [128, 3904], F32))
    i32_t = es.enter_context(nc.sbuf_tensor("i32b", [128, 1024], I32))
    f32b_t = es.enter_context(nc.sbuf_tensor("f32b", [128, 1152], F32))
    gcv_t = es.enter_context(nc.sbuf_tensor("gcv", [128, 1032], F32))
    per = Arena(per_t, 3904)
    numbuf = f32b_t[:, 0:1088]

    pb = [es.enter_context(nc.psum_tensor(f"pb{i}", [128, 512], F32)) for i in range(5)]
    kvps = es.enter_context(nc.psum_tensor("kvps", [128, 1536], F32))

    ident = per.alloc(128)
    gamma = per.alloc(D); beta = per.alloc(D); bproj = per.alloc(D)
    cw = [per.alloc(32) for _ in range(3)]
    cbv = per.alloc(32)
    onesf = per.alloc(128)
    invws1 = per.alloc(1); invws2 = per.alloc(1)
    stat_mu = per.alloc(32); stat_iv = per.alloc(32)
    stat_mu2 = per.alloc(4); stat_iv2 = per.alloc(4)
    aq = per.alloc(32)
    rabq = per.alloc(32)
    Aq = per.alloc(32)
    rAq = per.alloc(32)
    scr = per.alloc(64)
    den8 = per.alloc(32)
    tiny = per.alloc(32)

    b = B()
    sems = {k: Ctr(es.enter_context(nc.semaphore(k))) for k in ("LD", "LW", "LT", "ST", "PE", "DV", "AC", "GP")}
    LD, LW, LT, ST, PE, DV, AC, GP = (sems[k] for k in ("LD", "LW", "LT", "ST", "PE", "DV", "AC", "GP"))

    def dma(qeng, outv, inv, ctr):
        b._inc(qeng, lambda e, o=outv, i=inv: e.dma_start(out=o, in_=i), ctr, 16)
    def dmaT(qeng, outv, inv, ctr):
        b._inc(qeng, lambda e, o=outv, i=inv: e.dma_start_transpose(o, i), ctr, 16)
    def vec(fn, inc=1):
        b._inc("vector", fn, DV if inc else None, 1)
    def act(fn, inc=1):
        b._inc("scalar", fn, AC if inc else None, 1)
    def gp(fn, inc=1):
        b._inc("gpsimd", fn, GP if inc else None, 1)
    def pe(fn, inc=1):
        b._inc("tensor", fn, PE if inc else None, 1)

    dma("sync", ident[:], ap["ident"][:, :], LD)
    dma("sync", gamma[:], ap["gamma_bc"][:, :], LD)
    dma("sync", beta[:], ap["beta_bc"][:, :], LD)
    dma("sync", bproj[:], ap["bproj_bc"][:, :], LD)
    for j in range(3):
        dma("sync", cw[j][:], ap[f"cw{j}"][:, :], LD)
    dma("sync", cbv[:], ap["cb"][:, :], LD)
    b.wait("vector", LD)
    vec(lambda e: e.memset(onesf[:], 1.0), inc=0)
    vec(lambda e: e.memset(scr[:], 0.0))

    def vtiny(fn):
        vec(fn, inc=1)
        for _ in range(16):
            b.q["vector"].append(lambda e: e.engine_nop())
    def atiny(fn):
        act(fn, inc=1)
        b.wait("scalar", AC)

    f32a = Arena(f32a_t, 10240)
    wld = [f32a.alloc(4096), f32a.alloc(4096)]
    wbf_a = Arena(bfa_t, 38400)
    wbf = wbf_a.alloc(4096)

    def quant_weight(wapT, nchunk, wqap, invws_col):
        cols = DH if nchunk == 8 else D
        guard = [DV.v, DV.v]
        accv = tiny[:, 0:nchunk] if nchunk <= 16 else None
        acc32 = per
        accview = tiny[:, 0:8] if nchunk == 8 else scr[:, 0:32]
        for c in range(nchunk):
            buf = wld[c % 2][:, :cols]
            b.wait("sync", DV, guard[c % 2])
            dma("sync", buf, wapT[128 * c:128 * (c + 1), :], LD)
            b.wait("vector", LD)
            vec(lambda e, buf=buf, c=c: e.reduce_sum(accview[:, c:c + 1], buf, axis=AX.X, apply_absolute_value=True))
            guard[c % 2] = DV.v
        b.wait("vector", DV)
        vec(lambda e: e.reduce_sum(scr[:, 40:41], accview, axis=AX.X))
        b.wait("vector", DV)
        b.wait("tensor", DV)
        pe(lambda e: e.matmul(pb[0][0:1, 0:1], onesf[:, 0:1], scr[:, 40:41], start=True, stop=True))
        nelem = float(128 * nchunk * cols)
        b.wait("vector", PE)
        vtiny(lambda e: e.tensor_scalar(scr[0:1, 41:42], pb[0][0:1, 0:1], 1.0 / nelem, 1e-5, op0=ALU.mult, op1=ALU.max))
        b.wait("tensor", DV)
        pe(lambda e: e.matmul(pb[0][:, 4:5], onesf[0:1, :], scr[0:1, 41:42], start=True, stop=True))
        b.wait("vector", PE)
        vtiny(lambda e: e.tensor_copy(invws_col[:, 0:1], pb[0][:, 4:5]))
        vtiny(lambda e: e.reciprocal(scr[:, 42:43], invws_col[:, 0:1]))
        wsv = scr[:, 42:43]
        st_guard = ST.v
        for c in range(nchunk):
            buf = wld[c % 2][:, :cols]
            b.wait("sync", DV, guard[c % 2])
            dma("sync", buf, wapT[128 * c:128 * (c + 1), :], LD)
            b.wait("vector", LD)
            b.wait("vector", ST, st_guard)
            for h0 in range(0, cols, 1024):
                hw = min(1024, cols - h0)
                vec(lambda e, buf=buf, h0=h0, hw=hw: e.tensor_scalar(i32_t[:, :hw], buf[:, h0:h0 + hw], wsv, None, op0=ALU.mult), inc=0)
                vec(lambda e, h0=h0, hw=hw: e.tensor_scalar(wbf[:, h0:h0 + hw], i32_t[:, :hw], -1.0, 1.0, op0=ALU.max, op1=ALU.min), inc=(1 if h0 + hw >= cols else 0))
            guard[c % 2] = DV.v
            b.wait("gpsimd", DV)
            dma("gpsimd", wqap[128 * c:128 * (c + 1), :], wbf[:, :cols], ST)
            st_guard = ST.v
    quant_weight(ap["w1T"], 8, WQ1, invws1)
    quant_weight(ap["w2T"], 32, WQ2, invws2)

    wta = Arena(wta_t, 32768)
    w_at = {}
    for w in ("wk", "wv", "wq", "wfc"):
        w_at[w] = wta.alloc(8 * D).rearrange("p (c n) -> p c n", c=8)
        for dch in range(8):
            dma("gpsimd", w_at[w][:, dch, :], ap[w][128 * dch:128 * (dch + 1), :], LW)
    p1_lw = LW.v

    bfa = Arena(bfa_t, 38400)
    yfm = [bfa.alloc(8 * G).rearrange("p (c n) -> p c n", c=8) for _ in range(2)]
    ofm = [bfa.alloc(8 * G).rearrange("p (c n) -> p c n", c=8) for _ in range(2)]
    Khold = bfa.alloc(D)
    Vext = bfa.alloc(HA * 65).rearrange("p (h c) -> p h c", h=HA)
    Qhold = bfa.alloc(8 * SEQ).rearrange("p (c n) -> p c n", c=8)
    KVblk = bfa.alloc(8 * 130).rearrange("p (q c) -> p q c", q=8)
    o_sb = bfa.alloc(D)
    ybf = bfa.alloc(D)
    xgr = f32a_t[:, 0:4096].rearrange("p (m c) -> p m c", m=4)
    xgx = f32a_t[:, 4096:8192].rearrange("p (m c) -> p m c", m=4)
    junk = f32a_t[:, 8192:9216]
    xbt = f32a_t[:, 9216:10240]
    st2 = per_t[:, 0:0]

    def group_ln(xg, base_col, n=NSUB, mu_arr=None, iv_arr=None):
        mu = mu_arr[:, base_col:base_col + n]
        iv = iv_arr[:, base_col:base_col + n]
        s_ = tiny[:, 0:n]; q_ = tiny[:, 4:4 + n]; m2 = tiny[:, 8:8 + n]
        for m in range(n):
            vec(lambda e, m=m: e.reduce_sum(s_[:, m:m + 1], xg[:, m, :], axis=AX.X), inc=(m == n - 1))
        b.wait("scalar", DV)
        for m in range(n):
            act(lambda e, m=m: e.activation(junk, xg[:, m, :], AF.Square, accum_out=q_[:, m:m + 1]), inc=(m == n - 1))
        b.wait("vector", AC)
        vtiny(lambda e: e.tensor_scalar(mu, s_, 1.0 / D, None, op0=ALU.mult))
        vtiny(lambda e: e.tensor_mul(m2, mu, mu))
        vtiny(lambda e: e.scalar_tensor_tensor(iv, q_, 1.0 / D, m2, op0=ALU.mult, op1=ALU.subtract))
        vtiny(lambda e: e.tensor_scalar(iv, iv, LN_EPS, None, op0=ALU.add))
        vtiny(lambda e: e.reciprocal(iv, iv))
        b.wait("scalar", DV)
        atiny(lambda e: e.activation(iv, iv, AF.Sqrt))
        b.wait("vector", AC)

    def kvcol(h):
        return (h // 7) * 512 + (h % 7) * 65 if h < 14 else 1024 + (h - 14) * 65

    if n_sublayers >= 1:
        xgr_guard = [DV.v]
        xgx_guard = [DV.v]
        yfm_guard = [0, 0]
        ofm_guard = [0, 0]
        ybf_guard = [0]
        osb_guard = [0]
        xbt_guard = [0]
        for s in range(NSEQ):
            ydone_st = [0] * GPS
            for gi in range(GPS + 2):
                if gi >= 2:
                    g = gi - 2; t0 = (s * GPS + g) * G
                    yf = yfm[g % 2]
                    b.wait("sync", ST, ydone_st[g])
                    b.wait("sync", PE, yfm_guard[g % 2])
                    for dch in range(8):
                        dmaT("sync", yf[:, dch, :], YB[t0:t0 + G, 128 * dch:128 * (dch + 1)], LT)
                    b.wait("tensor", LT)
                    b.wait("tensor", LW, p1_lw)
                    if s == 0 and g == 0:
                        b.wait("vector", LD, 0)
                        vec(lambda e: e.memset(Vext[:, :, 64:65], 1.0))
                        b.wait("tensor", DV)
                    for m in range(NSUB):
                        for wname, bank in (("wk", 0), ("wv", 2)):
                            for n in range(2):
                                for dch in range(8):
                                    pe(lambda e, wname=wname, bank=bank, n=n, dch=dch, m=m, yf=yf: e.matmul(
                                        pb[bank + n][:, :],
                                        yf[:, dch, m * 128:(m + 1) * 128],
                                        w_at[wname][:, dch, n * 512:(n + 1) * 512],
                                        start=(dch == 0), stop=(dch == 7)),
                                        inc=(1 if dch == 7 else 0))
                        kv_pe = PE.v
                        b.wait("scalar", PE, kv_pe)
                        for n in range(2):
                            act(lambda e, n=n: e.activation(Khold[:, n * 512:(n + 1) * 512], pb[n][:, :], AF.Sigmoid))
                        k_ac = AC.v
                        b.wait("vector", PE, kv_pe)
                        for n in range(2):
                            vec(lambda e, n=n: e.tensor_copy(
                                Vext[:, 8 * n:8 * (n + 1), 0:64],
                                pb[2 + n][:, :]))
                        v_dv = DV.v
                        for qch in range(8):
                            if qch == 0:
                                bank = 4
                            else:
                                bank = (qch - 1) % 2
                                b.wait("tensor", AC, k_ac)
                            if qch >= 2:
                                b.wait("tensor", AC, AC.v - 2)
                            for dch in range(8):
                                pe(lambda e, bank=bank, qch=qch, dch=dch, m=m, yf=yf: e.matmul(
                                    pb[bank][:, 0:128],
                                    w_at["wq"][:, dch, qch * 128:(qch + 1) * 128],
                                    yf[:, dch, m * 128:(m + 1) * 128],
                                    start=(dch == 0), stop=(dch == 7)),
                                    inc=(1 if dch == 7 else 0))
                            b.wait("scalar", PE)
                            act(lambda e, bank=bank, qch=qch, g=g, m=m: e.activation(
                                Qhold[:, qch, (g * NSUB + m) * 128:(g * NSUB + m + 1) * 128],
                                pb[bank][:, 0:128], AF.Sigmoid))
                        b.wait("tensor", AC)
                        b.wait("tensor", DV, v_dv)
                        firstmn = (g == 0 and m == 0)
                        lastmn = (g == GPS - 1 and m == NSUB - 1)
                        for h in range(HA):
                            first = firstmn and h in (0, 7, 14)
                            last = lastmn and h in (6, 13, 15)
                            pe(lambda e, h=h, first=first, last=last: e.matmul(
                                kvps[0:64, kvcol(h):kvcol(h) + 65],
                                Khold[:, 64 * h:64 * (h + 1)],
                                Vext[:, h, :],
                                start=first, stop=last),
                                inc=(1 if (lastmn and h == HA - 1) else 0))
                    yfm_guard[g % 2] = PE.v
                if gi < GPS:
                    g = gi; t0 = (s * GPS + g) * G
                    b.wait("sync", DV, xgr_guard[0])
                    dma("sync", xgr, ap["xin"][t0:t0 + G, :].rearrange("(m p) c -> p m c", p=128), LD)
                    b.wait("vector", LD)
                    gcol = (s * GPS + g) * NSUB
                    group_ln(xgr, gcol, NSUB, stat_mu, stat_iv)
                    for m in range(NSUB):
                        st_ = t0 + m * 128
                        col = gcol + m
                        b.wait("vector", ST, ybf_guard[0])
                        vec(lambda e, m=m, col=col: e.tensor_scalar(
                            ybf, xgr[:, m, :], stat_mu[:, col:col + 1], stat_iv[:, col:col + 1],
                            op0=ALU.subtract, op1=ALU.mult))
                        b.wait("gpsimd", DV)
                        dma("gpsimd", YB[st_:st_ + 128, :], ybf, ST)
                        ybf_guard[0] = ST.v
                    xgr_guard[0] = DV.v
                    ydone_st[g] = ST.v
            b.wait("vector", PE)
            vec(lambda e: e.memset(KVblk[:, :, :], 0.0), inc=0)
            for h in range(HA):
                q_, r_ = h // 2, h % 2
                vec(lambda e, h=h, q_=q_, r_=r_: e.tensor_copy(
                    KVblk[64 * r_:64 * (r_ + 1), q_, 65 * r_:65 * r_ + 65],
                    kvps[0:64, kvcol(h):kvcol(h) + 65]), inc=(1 if h == HA - 1 else 0))
            kv_dv = DV.v

            odone_st = [0] * GPS
            evdv = [DV.v, DV.v]
            nbuf_ac = [AC.v]
            for gi in range(GPS + 2):
                if gi < GPS:
                    g = gi; t0 = (s * GPS + g) * G
                    b.wait("tensor", DV, kv_dv)
                    for m in range(NSUB):
                        st_ = t0 + m * 128
                        b.wait("vector", AC, nbuf_ac[0])
                        for p_ in range(8):
                            bank = p_ % 2
                            b.wait("tensor", DV, evdv[bank])
                            pe(lambda e, p_=p_, bank=bank, g=g, m=m: e.matmul(
                                pb[bank][:, 0:130],
                                Qhold[:, p_, (g * NSUB + m) * 128:(g * NSUB + m + 1) * 128],
                                KVblk[:, p_, :],
                                start=True, stop=True))
                            b.wait("vector", PE)
                            vec(lambda e, bank=bank, p_=p_: e.tensor_copy(
                                numbuf[:, 130 * p_:130 * p_ + 130], pb[bank][:, 0:130]))
                            evdv[bank] = DV.v
                        nb3 = numbuf[:, 0:1040].rearrange("p (q c) -> p q c", c=130)
                        for j in range(2):
                            vec(lambda e, j=j, nb3=nb3: e.tensor_scalar(
                                den8[:, 8 * j:8 * j + 8], nb3[:, :, 64 + 65 * j:65 + 65 * j],
                                1e-8, None, op0=ALU.add), inc=(j == 1))
                        b.wait("vector", DV)
                        vtiny(lambda e: e.reciprocal(den8[:, 0:16], den8[:, 0:16]))
                        b.wait("scalar", DV)
                        b.wait("scalar", ST, osb_guard[0])
                        for p_ in range(8):
                            for j in range(2):
                                act(lambda e, p_=p_, j=j: e.activation(
                                    o_sb[:, 128 * p_ + 64 * j:128 * p_ + 64 * (j + 1)],
                                    numbuf[:, 130 * p_ + 65 * j:130 * p_ + 65 * j + 64],
                                    AF.Copy, scale=den8[:, 8 * j + p_:8 * j + p_ + 1]),
                                    inc=(1 if (p_ == 7 and j == 1) else 0))
                        nbuf_ac[0] = AC.v
                        b.wait("gpsimd", AC)
                        dma("gpsimd", OB[st_:st_ + 128, :], o_sb, ST)
                        osb_guard[0] = ST.v
                    odone_st[g] = ST.v
                if gi >= 2:
                    g = gi - 2; t0 = (s * GPS + g) * G
                    of = ofm[g % 2]
                    b.wait("sync", ST, odone_st[g])
                    b.wait("sync", PE, ofm_guard[g % 2])
                    for dch in range(8):
                        dmaT("sync", of[:, dch, :], OB[t0:t0 + G, 128 * dch:128 * (dch + 1)], LT)
                    b.wait("sync", DV, xgx_guard[0])
                    dma("sync", xgx, ap["xin"][t0:t0 + G, :].rearrange("(m p) c -> p m c", p=128), LD)
                    b.wait("tensor", LT)
                    gcol = (s * GPS + g) * NSUB
                    for m in range(NSUB):
                        col = gcol + m
                        b.wait("vector", LD)
                        b.wait("vector", ST, xbt_guard[0])
                        vec(lambda e, m=m, col=col: e.tensor_scalar(
                            xbt, xgx[:, m, :], stat_mu[:, col:col + 1], stat_iv[:, col:col + 1],
                            op0=ALU.subtract, op1=ALU.mult))
                        fcdv = DV.v
                        b.wait("tensor", DV, fcdv)
                        for n in range(2):
                            for dch in range(8):
                                pe(lambda e, n=n, dch=dch, m=m, of=of: e.matmul(
                                    pb[2 + n][:, :],
                                    of[:, dch, m * 128:(m + 1) * 128],
                                    w_at["wfc"][:, dch, n * 512:(n + 1) * 512],
                                    start=(dch == 0), stop=(dch == 7)),
                                    inc=(1 if dch == 7 else 0))
                        b.wait("vector", PE)
                        for n in range(2):
                            vec(lambda e, n=n, m=m: e.tensor_add(
                                xgr[:, m, n * 512:(n + 1) * 512], pb[2 + n][:, :],
                                xbt[:, n * 512:(n + 1) * 512]), inc=(1 if n == 1 else 0))
                        xbt_guard[0] = 0
                    group_ln(xgr, 0, NSUB, stat_mu2, stat_iv2)
                    for m in range(NSUB):
                        st_ = t0 + m * 128
                        vec(lambda e, m=m: e.tensor_scalar(
                            junk, xgr[:, m, :], stat_mu2[:, m:m + 1], stat_iv2[:, m:m + 1],
                            op0=ALU.subtract, op1=ALU.mult), inc=0)
                        vec(lambda e: e.tensor_mul(junk, junk, gamma[:, :]), inc=0)
                        b.wait("vector", ST, xbt_guard[0])
                        vec(lambda e, m=m: e.tensor_add(xbt, xgx[:, m, :], beta[:, :]), inc=0)
                        vec(lambda e: e.tensor_add(xbt, xbt, junk))
                        b.wait("gpsimd", DV)
                        dma("gpsimd", XS[st_:st_ + 128, :], xbt, ST)
                        xbt_guard[0] = ST.v
                    xgx_guard[0] = DV.v
                    xgr_guard[0] = DV.v
                    ofm_guard[g % 2] = PE.v

    f_xg   = f32a_t[:, 0:4096].rearrange("p (m c) -> p m c", m=4)
    f_xg2  = f32a_t[:, 4096:8192].rearrange("p (m c) -> p m c", m=4)
    f_hs   = [f32a_t[:, 4096:4612], f32a_t[:, 4612:5128]]
    f_cva  = f32a_t[:, 5128:5644]
    f_h2s  = [f32a_t[:, 5644:6157], f32a_t[:, 6157:6670]]
    f_m1x  = [f32a_t[:, 6670:7183], f32a_t[:, 7183:7696]]
    f_m1n  = [f32a_t[:, 7696:8192][:, 0:496], None]
    f_junk = f32a_t[:, 8192:9216]
    f_xbt  = f32a_t[:, 9216:10240]
    del f_hs, f_cva, f_h2s, f_m1x, f_m1n
    f_hs  = [f32a_t[:, 4096:4612], f32a_t[:, 4612:5128]]
    f_cva = f32a_t[:, 5128:5644]
    f_cva2 = [f32a_t[:, 5128:5644], f32b_t[:, 0:516]]
    f_h2s = f32a_t[:, 5644:6157]
    f_h2s2 = [f32a_t[:, 5644:6157], f32b_t[:, 516:1029]]
    f_m1x = f32a_t[:, 6157:6670]
    f_m1n = f32a_t[:, 6670:7183]
    f_k3x = f32a_t[:, 7183:7311]
    f_k3n = f32a_t[:, 7311:7439]
    f_bc  = f32a_t[:, 7439:7951]
    f_dg  = f32a_t[:, 5128:5640]
    f_h2i = [f32a_t[:, 8192:8704], f32a_t[:, 8704:9216]]
    f_mq  = f32a_t[:, 9216:9728]
    f_bcB = f32a_t[:, 9728:10240]

    halo = per.alloc(64)

    def emit_ffn(phase, src_XS, dst_XS, dbg=None):
        b.wait("gpsimd", PE)
        b.wait("gpsimd", ST)
        w1q = wta_t[:, 0:32768].rearrange("p (c n) -> p c n", c=8)
        for dch in range(8):
            dma("gpsimd", w1q[:, dch, :], WQ1[128 * dch:128 * (dch + 1), :], LW)
        lw_a = LW.v
        xs_st = ST.v
        h1bank_guard = [DV.v] * 6
        xg_guard = [DV.v]
        xg_guard_ac = [AC.v]
        yfm_guard2 = [PE.v, PE.v]
        ybf_guard2 = [ST.v, ST.v]
        ybf2 = bfa_t[:, 17408:18432]
        prevk3 = [None]
        h2st_done = [0]
        h2s_guard = [ST.v, ST.v]
        xbt_st_guard = [ST.v]

        def finish_amax_batch(col0, srcs):
            n = len(srcs)
            b.wait("tensor", DV)
            b.wait("tensor", GP)
            for i, (sx, sn) in enumerate(srcs):
                pe(lambda e, sx=sx, i=i: e.transpose(pb[3][:, 0:128] if False else pb[3][:, 0:128], sx, ident), inc=0) if False else None
            waves = [srcs[i:i + 2] for i in range(0, n, 2)]
            base = 0
            for w_i, wave in enumerate(waves):
                for i, (sx, sn) in enumerate(wave):
                    pe(lambda e, sx=sx, o=(2 * i) * 128: e.transpose(pb[3][:, o:o + 128], sx, ident), inc=0)
                    pe(lambda e, sn=sn, o=(2 * i + 1) * 128: e.transpose(pb[3][:, o:o + 128], sn, ident),
                       inc=(1 if i == len(wave) - 1 else 0))
                b.wait("vector", PE)
                for i in range(len(wave)):
                    j = w_i * 2 + i
                    vec(lambda e, i=i, j=j: e.reduce_max(tiny[:, 16 + j:17 + j], pb[3][:, (2 * i) * 128:(2 * i) * 128 + 128], axis=AX.X), inc=0)
                    vec(lambda e, i=i, j=j: e.tensor_reduce(tiny[:, 21 + j:22 + j], pb[3][:, (2 * i + 1) * 128:(2 * i + 1) * 128 + 128], axis=AX.X, op=ALU.min),
                        inc=(1 if i == len(wave) - 1 else 0))
                if w_i + 1 < len(waves):
                    b.wait("tensor", DV)
            b.wait("vector", DV)
            xs_ = tiny[:, 16:16 + n]; ns_ = tiny[:, 21:21 + n]
            vtiny(lambda e, xs_=xs_, ns_=ns_: e.scalar_tensor_tensor(xs_, ns_, -1.0, xs_, op0=ALU.mult, op1=ALU.max))
            vtiny(lambda e, xs_=xs_: e.tensor_scalar(xs_, xs_, 1e-5, None, op0=ALU.max))
            vtiny(lambda e, col0=col0, n=n, xs_=xs_: e.tensor_scalar(rAq[:, col0:col0 + n], xs_, invws2[:, 0:1], 1.0 / 127.0, op0=ALU.mult, op1=ALU.mult))
            vtiny(lambda e, ns_=ns_, xs_=xs_: e.reciprocal(ns_, xs_))
            vtiny(lambda e, col0=col0, n=n, ns_=ns_: e.tensor_scalar(Aq[:, col0:col0 + n], ns_, 127.0, None, op0=ALU.mult))

        for gi in range(NG + 2):
            if gi >= 2:
                g = gi - 2; t0 = g * G
                gcol = g * NSUB
                seq_start = (g % GPS == 0)
                seq_end = (g % GPS == GPS - 1)
                w = 511 if seq_start else (513 if seq_end else 512)
                wstart_off = 0 if seq_start else 1
                yf = yfm[g % 2]
                b.wait("sync", ST)
                b.wait("sync", PE, yfm_guard2[g % 2])
                for dch in range(8):
                    dmaT("sync", yf[:, dch, :], YB[t0:t0 + G, 128 * dch:128 * (dch + 1)], LT)
                b.wait("vector", PE)
                b.wait("vector", AC)
                for m in range(NSUB):
                    vec(lambda e, m=m, gcol=gcol: e.tensor_scalar(
                        f_dg[:, 128 * m:128 * (m + 1)], ident[:, :], rabq[:, gcol + m:gcol + m + 1], None,
                        op0=ALU.mult), inc=(m == NSUB - 1))
                b.wait("tensor", DV)
                b.wait("tensor", LT)
                b.wait("tensor", LW, lw_a)
                for m in range(NSUB):
                    pe(lambda e, m=m: e.matmul(pb[2][:, 128 * m:128 * (m + 1)], onesf[:, :], f_dg[:, 128 * m:128 * (m + 1)], start=True, stop=True), inc=(m == NSUB - 1))
                b.wait("vector", PE)
                vec(lambda e: e.tensor_copy(f_bc, pb[2][:, :]))
                if seq_start:
                    vec(lambda e: e.memset(halo[:, :], 0.0))
                b.wait("gpsimd", PE)
                b.wait("gpsimd", DV)
                gp(lambda e: e.memset(f_m1x, -3.0e38), inc=0)
                gp(lambda e: e.memset(f_m1n, 3.0e38))
                b.wait("vector", GP)
                silu_ac = [AC.v, AC.v]
                m1_dv = [DV.v, DV.v]
                h1banks = [pb[0][:, :], pb[1][:, :], pb[4][:, :],
                           kvps[:, 0:512], kvps[:, 512:1024], kvps[:, 1024:1536]]
                m1_pend = [None]
                halo_pend = [None]
                gp_done = {}
                gsilu_ac = [AC.v]
                for c in range(32):
                    is_gp = False
                    hbk = h1banks[c % 6]
                    hsb = f_hs[c % 2]
                    b.wait("tensor", DV, h1bank_guard[c % 6])
                    for dch in range(8):
                        pe(lambda e, hbk=hbk, c=c, dch=dch, yf=yf: e.matmul(
                            hbk,
                            w1q[:, dch, 128 * c:128 * (c + 1)],
                            yf[:, dch, :],
                            start=(dch == 0), stop=(dch == 7)),
                            inc=(1 if dch == 7 else 0))
                    if m1_pend[0] is not None:
                        m1_pend[0](); m1_pend[0] = None
                    if halo_pend[0] is not None:
                        halo_pend[0](); halo_pend[0] = None
                    if (c - 2) in gp_done:
                        b.wait("vector", GP, gp_done.pop(c - 2))
                    vec(lambda e, hsb=hsb, c=c: e.tensor_copy(hsb[:, 0:2], halo[:, 2 * c:2 * c + 2]), inc=0)
                    if seq_end:
                        vec(lambda e, hsb=hsb: e.memset(hsb[:, 514:516], 0.0), inc=0)
                    b.wait("vector", PE)
                    vec(lambda e, hsb=hsb, hbk=hbk: e.tensor_mul(hsb[:, 2:514], hbk, f_bc))
                    h1bank_guard[c % 6] = DV.v
                    halo_pend[0] = (lambda hsb=hsb, c=c: vec(
                        lambda e: e.tensor_copy(halo[:, 2 * c:2 * c + 2], hsb[:, 512:514])))
                    h2s = f_h2s2[c % 2]
                    lo = 1 if seq_start else 0
                    if is_gp:
                        gA = gcv_t[:, 0:516]; gB = gcv_t[:, 516:1032]
                        b.wait("gpsimd", DV)
                        b.wait("gpsimd", AC, gsilu_ac[0])
                        gp(lambda e, hsb=hsb, c=c, lo=lo, w=w, gA=gA: e.tensor_scalar(
                            gA[:, 0:w], hsb[:, lo + 1:lo + 1 + w], cw[1][:, c:c + 1], None, op0=ALU.mult), inc=0)
                        gp(lambda e, hsb=hsb, c=c, lo=lo, w=w, gB=gB: e.tensor_scalar(
                            gB[:, 0:w], hsb[:, lo:lo + w], cw[0][:, c:c + 1], None, op0=ALU.mult), inc=0)
                        gp(lambda e, w=w, gA=gA, gB=gB: e.tensor_tensor(gA[:, 0:w], gA[:, 0:w], gB[:, 0:w], op=ALU.add), inc=0)
                        gp(lambda e, hsb=hsb, c=c, lo=lo, w=w, gB=gB: e.tensor_scalar(
                            gB[:, 0:w], hsb[:, lo + 2:lo + 2 + w], cw[2][:, c:c + 1], None, op0=ALU.mult), inc=0)
                        gp(lambda e, w=w, gA=gA, gB=gB: e.tensor_tensor(gA[:, 0:w], gA[:, 0:w], gB[:, 0:w], op=ALU.add))
                        gp_done[c] = GP.v
                        b.wait("scalar", GP)
                        b.wait("scalar", ST, h2s_guard[c % 2])
                        b.wait("scalar", DV, m1_dv[c % 2])
                        act(lambda e, h2s=h2s, c=c, w=w, gA=gA: e.activation(h2s[:, 0:w], gA[:, 0:w], AF.Silu, bias=cbv[:, c:c + 1]))
                        gsilu_ac[0] = AC.v
                    else:
                        cva = f_cva2[c % 2]
                        b.wait("vector", AC, silu_ac[c % 2])
                        vec(lambda e, hsb=hsb, cva=cva, c=c, lo=lo, w=w: e.tensor_scalar(
                            cva[:, 0:w], hsb[:, lo + 1:lo + 1 + w], cw[1][:, c:c + 1], None, op0=ALU.mult), inc=0)
                        vec(lambda e, hsb=hsb, cva=cva, c=c, lo=lo, w=w: e.scalar_tensor_tensor(
                            cva[:, 0:w], hsb[:, lo:lo + w], cw[0][:, c:c + 1], cva[:, 0:w],
                            op0=ALU.mult, op1=ALU.add), inc=0)
                        vec(lambda e, hsb=hsb, cva=cva, c=c, lo=lo, w=w: e.scalar_tensor_tensor(
                            cva[:, 0:w], hsb[:, lo + 2:lo + 2 + w], cw[2][:, c:c + 1], cva[:, 0:w],
                            op0=ALU.mult, op1=ALU.add))
                        b.wait("scalar", DV)
                        b.wait("scalar", ST, h2s_guard[c % 2])
                        b.wait("scalar", DV, m1_dv[c % 2])
                        act(lambda e, cva=cva, h2s=h2s, c=c, w=w: e.activation(h2s[:, 0:w], cva[:, 0:w], AF.Silu, bias=cbv[:, c:c + 1]))
                        silu_ac[c % 2] = AC.v
                    def m1_emit(h2s=h2s, w=w, cc=c, ac_at=AC.v, gp_at=GP.v):
                        b.wait("vector", AC, ac_at)
                        b.wait("vector", GP, gp_at)
                        vec(lambda e: e.tensor_tensor(f_m1x[:, 0:w], f_m1x[:, 0:w], h2s[:, 0:w], op=ALU.max), inc=0)
                        vec(lambda e: e.tensor_tensor(f_m1n[:, 0:w], f_m1n[:, 0:w], h2s[:, 0:w], op=ALU.min))
                        m1_dv[cc % 2] = DV.v
                    m1_pend[0] = m1_emit
                    b.wait("sync", AC)
                    tkn0 = t0 - wstart_off
                    dma("sync", H2[128 * c:128 * (c + 1), tkn0:tkn0 + w], h2s[:, 0:w], ST)
                    h2s_guard[c % 2] = ST.v
                if m1_pend[0] is not None:
                    m1_pend[0](); m1_pend[0] = None
                if halo_pend[0] is not None:
                    halo_pend[0](); halo_pend[0] = None
                h2st_done[0] = ST.v
                batch = []
                col0 = gcol
                b.wait("vector", GP)
                if prevk3[0] is not None:
                    if not seq_start:
                        vtiny(lambda e: e.tensor_copy(f_k3x[:, 127:128], f_m1x[:, 0:1]))
                        vtiny(lambda e: e.tensor_copy(f_k3n[:, 127:128], f_m1n[:, 0:1]))
                    col0 = prevk3[0][0]
                    batch.append((f_k3x, f_k3n))
                    prevk3[0] = None
                for k in range(3):
                    sl = (128 * k + (0 if seq_start else 1))
                    batch.append((f_m1x[:, sl:sl + 128], f_m1n[:, sl:sl + 128]))
                sl3 = 128 * 3 + (0 if seq_start else 1)
                if seq_end:
                    batch.append((f_m1x[:, sl3:sl3 + 128], f_m1n[:, sl3:sl3 + 128]))
                else:
                    vtiny(lambda e, sl3=sl3: e.tensor_copy(f_k3x[:, 0:127], f_m1x[:, sl3:sl3 + 127]))
                    vtiny(lambda e, sl3=sl3: e.tensor_copy(f_k3n[:, 0:127], f_m1n[:, sl3:sl3 + 127]))
                    prevk3[0] = (gcol + 3,)
                finish_amax_batch(col0, batch)
                yfm_guard2[g % 2] = PE.v

            if gi < NG:
                g = gi; t0 = g * G
                gcol = g * NSUB
                b.wait("sync", DV, xg_guard[0])
                b.wait("sync", AC, xg_guard_ac[0] if isinstance(xg_guard_ac, list) else 0)
                b.wait("sync", ST, xs_st)
                dma("sync", f_xg[:, :, :], src_XS[t0:t0 + G, :].rearrange("(m p) c -> p m c", p=128), LD)
                b.wait("vector", LD)
                group_ln(f_xg, gcol, NSUB, stat_mu, stat_iv)
                for m in range(NSUB):
                    col = gcol + m
                    b.wait("vector", ST, xbt_st_guard[0])
                    vec(lambda e, m=m, col=col: e.tensor_scalar(
                        f_xbt, f_xg[:, m, :], stat_mu[:, col:col + 1], None, op0=ALU.subtract), inc=0)
                    vec(lambda e, m=m: e.reduce_max(tiny[:, m:m + 1], f_xbt, axis=AX.X, apply_absolute_value=True))
                    b.wait("vector", DV)
                iv4 = stat_iv[:, gcol:gcol + 4]
                vtiny(lambda e, iv4=iv4: e.tensor_mul(tiny[:, 4:8], tiny[:, 0:4], iv4))
                vtiny(lambda e: e.tensor_scalar(tiny[:, 4:8], tiny[:, 4:8], 1e-5, None, op0=ALU.max))
                vtiny(lambda e, gcol=gcol: e.tensor_scalar(rabq[:, gcol:gcol + 4], tiny[:, 4:8], invws1[:, 0:1], 1.0 / 127.0, op0=ALU.mult, op1=ALU.mult))
                vtiny(lambda e: e.reciprocal(tiny[:, 8:12], tiny[:, 4:8]))
                vtiny(lambda e: e.tensor_scalar(tiny[:, 8:12], tiny[:, 8:12], 127.0, None, op0=ALU.mult))
                vtiny(lambda e, gcol=gcol, iv4=iv4: e.tensor_mul(aq[:, gcol:gcol + 4], tiny[:, 8:12], iv4))
                vtiny(lambda e, gcol=gcol: e.tensor_mul(tiny[:, 0:4], stat_mu[:, gcol:gcol + 4], aq[:, gcol:gcol + 4]))
                vtiny(lambda e: e.tensor_scalar(tiny[:, 0:4], tiny[:, 0:4], -1.0, None, op0=ALU.mult))
                b.wait("scalar", DV)
                for m in range(NSUB):
                    st_ = t0 + m * 128
                    col = gcol + m
                    yb2 = ybf if m % 2 == 0 else ybf2
                    b.wait("scalar", ST, ybf_guard2[m % 2])
                    act(lambda e, m=m, col=col: e.activation(
                        i32_t[:, 0:D], f_xg[:, m, :], AF.Identity,
                        scale=aq[:, col:col + 1], bias=tiny[:, m:m + 1]), inc=0)
                    act(lambda e, yb2=yb2: e.activation(yb2, i32_t[:, 0:D], AF.Copy))
                    b.wait("gpsimd", AC)
                    dma("gpsimd", YB[st_:st_ + 128, :], yb2, ST)
                    ybf_guard2[m % 2] = ST.v
                xg_guard[0] = DV.v
                xg_guard_ac[0] = AC.v
        b.wait("gpsimd", PE)
        w2q = wta_t[:, 0:32768].rearrange("p (c n) -> p c n", c=32)
        for c2 in range(0, 32, 4):
            dma("gpsimd", w2q[:, c2:c2 + 4, :], WQ2[128 * c2:128 * (c2 + 4), :].rearrange("(c p) n -> p c n", p=128), LW)
        lw_b = LW.v
        hq = bfa_t[:, 16384:32768].rearrange("p (c n) -> p c n", c=32)
        if dbg == "W2Q":
            b.wait("gpsimd", LW)
            dma("gpsimd", H2[0:4096, 0:1024].rearrange("(x p) c -> p x c", p=128), wta_t[:, :].rearrange("p (x c) -> p x c", x=32), ST)
        b.wait("sync", ST, h2st_done[0])
        xo_guard = [ST.v]
        x2_guard = [DV.v]
        for g in range(NG):
            t0 = g * G
            gcol = g * NSUB
            b.wait("vector", PE)
            b.wait("vector", AC)
            for m in range(NSUB):
                vec(lambda e, m=m, gcol=gcol: e.tensor_scalar(
                    f_mq[:, 128 * m:128 * (m + 1)], ident[:, :], Aq[:, gcol + m:gcol + m + 1], None,
                    op0=ALU.mult), inc=(m == NSUB - 1))
            b.wait("tensor", DV)
            b.wait("tensor", ST, h2st_done[0])
            b.wait("tensor", LW, lw_b)
            for m in range(NSUB):
                pe(lambda e, m=m: e.matmul(pb[2][:, 128 * m:128 * (m + 1)], onesf[:, :], f_mq[:, 128 * m:128 * (m + 1)], start=True, stop=True), inc=(m == NSUB - 1))
            b.wait("vector", PE)
            vec(lambda e: e.tensor_copy(f_bcB, pb[2][:, :]))
            b.wait("sync", DV, x2_guard[0])
            b.wait("sync", ST, xs_st)
            dma("sync", f_xg2[:, :, :], src_XS[t0:t0 + G, :].rearrange("(m p) c -> p m c", p=128), LD)
            hq_gp = [0] * 32
            b.wait("gpsimd", DV)
            mq2 = [f_mq, f32b_t[:, 0:512]]
            mq_ac = [AC.v, AC.v]
            for c in range(32):
                hb = f_h2i[c % 2]
                mq = mq2[c % 2]
                b.wait("sync", GP, hq_gp[c - 2] if c >= 2 else 0)
                dma("sync", hb, H2[128 * c:128 * (c + 1), t0:t0 + G], LD)
                b.wait("gpsimd", LD)
                b.wait("gpsimd", AC, mq_ac[c % 2])
                gp(lambda e, hb=hb, mq=mq: e.tensor_tensor(mq, hb, f_bcB, op=ALU.mult))
                hq_gp[c] = GP.v
                b.wait("scalar", GP)
                act(lambda e, mq=mq: e.activation(i32_t[:, 0:G], mq, AF.Copy), inc=0)
                act(lambda e, c=c: e.activation(hq[:, c, :], i32_t[:, 0:G], AF.Copy))
                mq_ac[c % 2] = AC.v
            if dbg == "HQ" and g == 0:
                b.wait("gpsimd", DV)
                dma("gpsimd", OB[0:2048, :].rearrange("(x p) c -> p x c", p=128), hq[:, :, :].rearrange("p c t -> p (c t)").rearrange("p (x c) -> p x c", x=16), ST)
            bankmap = [0, 1, 3, 4]
            b.wait("vector", LD)
            for n in range(2):
                for m in range(NSUB):
                    bk = bankmap[m]
                    b.wait("tensor", AC)
                    b.wait("tensor", DV)
                    for c in range(32):
                        pe(lambda e, n=n, m=m, c=c, bk=bk: e.matmul(
                            pb[bk][:, :],
                            hq[:, c, 128 * m:128 * (m + 1)],
                            w2q[:, c, 512 * n:512 * (n + 1)],
                            start=(c == 0), stop=(c == 31)),
                            inc=(1 if c == 31 else 0))
                    b.wait("vector", PE)
                    b.wait("vector", ST, xo_guard[0])
                    vec(lambda e, n=n, m=m, bk=bk, gcol=gcol: e.scalar_tensor_tensor(
                        f_xg[:, m, 512 * n:512 * (n + 1)], pb[bk][:, :], rAq[:, gcol + m:gcol + m + 1],
                        f_xg2[:, m, 512 * n:512 * (n + 1)], op0=ALU.mult, op1=ALU.add))
            x2_guard[0] = DV.v
            b.wait("gpsimd", DV)
            for m in range(NSUB):
                st_ = t0 + m * 128
                dma("gpsimd", dst_XS[st_:st_ + 128, :], f_xg[:, m, :], ST)
            xo_guard[0] = ST.v

    if n_sublayers >= 2:
        emit_ffn(2, XS, XS if n_sublayers > 2 else XS, dbg=dbg)

    HB, DKB = 32, 32
    if n_sublayers >= 3:
        b.wait("gpsimd", PE)
        wta3 = Arena(wta_t, 32768)
        wqkv_sb = wta3.alloc(8 * 3 * D).rearrange("p (c n) -> p c n", c=8)
        wproj_sb = wta3.alloc(8 * D).rearrange("p (c n) -> p c n", c=8)
        for dch in range(8):
            dma("gpsimd", wqkv_sb[:, dch, :], ap["wqkvT"][128 * dch:128 * (dch + 1), :], LW)
            dma("gpsimd", wproj_sb[:, dch, :], ap["wprojT"][128 * dch:128 * (dch + 1), :], LW)
        p3_lw = LW.v
        bfa3 = Arena(bfa_t, 38400)
        yfm3 = [bfa3.alloc(8 * G).rearrange("p (c n) -> p c n", c=8) for _ in range(2)]
        ofm3 = [bfa3.alloc(8 * G).rearrange("p (c n) -> p c n", c=8) for _ in range(2)]
        k2_sb = bfa3.alloc(D)
        v2ext = bfa3.alloc(HB * 33).rearrange("p (h c) -> p h c", h=HB)
        Q2hold = bfa3.alloc(8 * SEQ).rearrange("p (c n) -> p c n", c=8)
        vkblk = bfa3.alloc(8 * 132).rearrange("p (q c) -> p q c", q=8)
        o2_sb = bfa3.alloc(D)
        ybf3 = bfa3.alloc(D)

        def vk_col(h):
            b_ = min(h // 15, 2)
            return 512 * b_ + 33 * (h - 15 * b_)

        xs_st3 = ST.v
        xgr_guard = [DV.v]
        xgx_guard = [DV.v]
        yfm_guard = [PE.v, PE.v]
        ofm_guard = [PE.v, PE.v]
        ybf_guard = [ST.v]
        osb_guard = [ST.v]
        xbt_guard = [ST.v]
        first_phase = [True]
        for s in range(NSEQ):
            ydone_st = [0] * GPS
            for gi in range(GPS + 2):
                if gi >= 2:
                    g = gi - 2; t0 = (s * GPS + g) * G
                    yf = yfm3[g % 2]
                    b.wait("sync", ST, ydone_st[g])
                    b.wait("sync", PE, yfm_guard[g % 2])
                    for dch in range(8):
                        dmaT("sync", yf[:, dch, :], YB[t0:t0 + G, 128 * dch:128 * (dch + 1)], LT)
                    b.wait("tensor", LT)
                    b.wait("tensor", LW, p3_lw)
                    if first_phase[0] and g == 0:
                        b.wait("vector", PE)
                        vec(lambda e: e.memset(v2ext[:, :, 32:33], 1.0))
                        b.wait("tensor", DV)
                        first_phase[0] = False
                    for m in range(NSUB):
                        for off, bank in ((D, 0), (2 * D, 2)):
                            for n in range(2):
                                for dch in range(8):
                                    pe(lambda e, off=off, bank=bank, n=n, dch=dch, m=m, yf=yf: e.matmul(
                                        pb[bank + n][:, :],
                                        yf[:, dch, m * 128:(m + 1) * 128],
                                        wqkv_sb[:, dch, off + n * 512:off + (n + 1) * 512],
                                        start=(dch == 0), stop=(dch == 7)),
                                        inc=(1 if dch == 7 else 0))
                        kv_pe = PE.v
                        b.wait("scalar", PE, kv_pe)
                        for n in range(2):
                            act(lambda e, n=n: e.activation(k2_sb[:, n * 512:(n + 1) * 512], pb[n][:, :], AF.Relu))
                        k_ac = AC.v
                        b.wait("vector", PE, kv_pe)
                        for n in range(2):
                            vec(lambda e, n=n: e.tensor_copy(
                                v2ext[:, 16 * n:16 * (n + 1), 0:32],
                                pb[2 + n][:, :]))
                        v_dv = DV.v
                        for qch in range(8):
                            if qch == 0:
                                bank = 4
                            else:
                                bank = (qch - 1) % 2
                                b.wait("tensor", AC, k_ac)
                            if qch >= 2:
                                b.wait("tensor", AC, AC.v - 2)
                            for dch in range(8):
                                pe(lambda e, bank=bank, qch=qch, dch=dch, m=m, yf=yf: e.matmul(
                                    pb[bank][:, 0:128],
                                    wqkv_sb[:, dch, qch * 128:(qch + 1) * 128],
                                    yf[:, dch, m * 128:(m + 1) * 128],
                                    start=(dch == 0), stop=(dch == 7)),
                                    inc=(1 if dch == 7 else 0))
                            b.wait("scalar", PE)
                            act(lambda e, bank=bank, qch=qch, g=g, m=m: e.activation(
                                Q2hold[:, qch, (g * NSUB + m) * 128:(g * NSUB + m + 1) * 128],
                                pb[bank][:, 0:128], AF.Relu))
                        b.wait("tensor", AC)
                        b.wait("tensor", DV, v_dv)
                        firstmn = (g == 0 and m == 0)
                        lastmn = (g == GPS - 1 and m == NSUB - 1)
                        for h in range(HB):
                            first = firstmn and h in (0, 15, 30)
                            last = lastmn and h in (14, 29, 31)
                            pe(lambda e, h=h, first=first, last=last: e.matmul(
                                kvps[0:32, vk_col(h):vk_col(h) + 33],
                                k2_sb[:, 32 * h:32 * (h + 1)],
                                v2ext[:, h, :],
                                start=first, stop=last),
                                inc=(1 if (lastmn and h == HB - 1) else 0))
                    yfm_guard[g % 2] = PE.v
                if gi < GPS:
                    g = gi; t0 = (s * GPS + g) * G
                    b.wait("sync", DV, xgr_guard[0])
                    b.wait("sync", ST, xs_st3)
                    dma("sync", xgr, XS[t0:t0 + G, :].rearrange("(m p) c -> p m c", p=128), LD)
                    b.wait("vector", LD)
                    gcol = (s * GPS + g) * NSUB
                    group_ln(xgr, gcol, NSUB, stat_mu, stat_iv)
                    for m in range(NSUB):
                        st_ = t0 + m * 128
                        col = gcol + m
                        b.wait("vector", ST, ybf_guard[0])
                        vec(lambda e, m=m, col=col: e.tensor_scalar(
                            ybf3, xgr[:, m, :], stat_mu[:, col:col + 1], stat_iv[:, col:col + 1],
                            op0=ALU.subtract, op1=ALU.mult))
                        b.wait("gpsimd", DV)
                        dma("gpsimd", YB[st_:st_ + 128, :], ybf3, ST)
                        ybf_guard[0] = ST.v
                    xgr_guard[0] = DV.v
                    ydone_st[g] = ST.v
            b.wait("vector", PE)
            vec(lambda e: e.memset(vkblk[:, :, :], 0.0), inc=0)
            for h in range(HB):
                q_, r_ = h // 4, h % 4
                vec(lambda e, h=h, q_=q_, r_=r_: e.tensor_copy(
                    vkblk[32 * r_:32 * (r_ + 1), q_, 33 * r_:33 * r_ + 33],
                    kvps[0:32, vk_col(h):vk_col(h) + 33]), inc=(1 if h == HB - 1 else 0))
            kv_dv = DV.v

            odone_st = [0] * GPS
            evdv = [DV.v, DV.v]
            nbuf_ac = [AC.v]
            for gi in range(GPS + 2):
                if gi < GPS:
                    g = gi; t0 = (s * GPS + g) * G
                    b.wait("tensor", DV, kv_dv)
                    for m in range(NSUB):
                        st_ = t0 + m * 128
                        b.wait("vector", AC, nbuf_ac[0])
                        for q_ in range(8):
                            bank = q_ % 2
                            b.wait("tensor", DV, evdv[bank])
                            pe(lambda e, q_=q_, bank=bank, g=g, m=m: e.matmul(
                                pb[bank][:, 0:132],
                                Q2hold[:, q_, (g * NSUB + m) * 128:(g * NSUB + m + 1) * 128],
                                vkblk[:, q_, :],
                                start=True, stop=True))
                            b.wait("vector", PE)
                            vec(lambda e, bank=bank, q_=q_: e.tensor_copy(
                                numbuf[:, 132 * q_:132 * q_ + 132], pb[bank][:, 0:132]))
                            evdv[bank] = DV.v
                        nb3 = numbuf[:, 0:1056].rearrange("p (q c) -> p q c", c=132)
                        for j in range(4):
                            vec(lambda e, j=j, nb3=nb3: e.tensor_scalar(
                                den8[:, 8 * j:8 * j + 8], nb3[:, :, 33 * j + 32:33 * j + 33],
                                1e-15, None, op0=ALU.add), inc=(j == 3))
                        b.wait("vector", DV)
                        vtiny(lambda e: e.reciprocal(den8[:, 0:32], den8[:, 0:32]))
                        b.wait("scalar", DV)
                        b.wait("scalar", ST, osb_guard[0])
                        for q_ in range(8):
                            for j in range(4):
                                act(lambda e, q_=q_, j=j: e.activation(
                                    o2_sb[:, 128 * q_ + 32 * j:128 * q_ + 32 * (j + 1)],
                                    numbuf[:, 132 * q_ + 33 * j:132 * q_ + 33 * j + 32],
                                    AF.Copy, scale=den8[:, 8 * j + q_:8 * j + q_ + 1]),
                                    inc=(1 if (q_ == 7 and j == 3) else 0))
                        nbuf_ac[0] = AC.v
                        b.wait("gpsimd", AC)
                        dma("gpsimd", OB[st_:st_ + 128, :], o2_sb, ST)
                        osb_guard[0] = ST.v
                    odone_st[g] = ST.v
                if gi >= 2:
                    g = gi - 2; t0 = (s * GPS + g) * G
                    of = ofm3[g % 2]
                    b.wait("sync", ST, odone_st[g])
                    b.wait("sync", PE, ofm_guard[g % 2])
                    for dch in range(8):
                        dmaT("sync", of[:, dch, :], OB[t0:t0 + G, 128 * dch:128 * (dch + 1)], LT)
                    b.wait("sync", DV, xgx_guard[0])
                    b.wait("sync", ST, xs_st3)
                    dma("sync", xgx, XS[t0:t0 + G, :].rearrange("(m p) c -> p m c", p=128), LD)
                    b.wait("tensor", LT)
                    for m in range(NSUB):
                        st_ = t0 + m * 128
                        fcdv = DV.v
                        b.wait("tensor", DV, fcdv)
                        for n in range(2):
                            for dch in range(8):
                                pe(lambda e, n=n, dch=dch, m=m, of=of: e.matmul(
                                    pb[2 + n][:, :],
                                    of[:, dch, m * 128:(m + 1) * 128],
                                    wproj_sb[:, dch, n * 512:(n + 1) * 512],
                                    start=(dch == 0), stop=(dch == 7)),
                                    inc=(1 if dch == 7 else 0))
                        b.wait("vector", LD)
                        b.wait("vector", PE)
                        b.wait("vector", ST, xbt_guard[0])
                        for n in range(2):
                            vec(lambda e, n=n, m=m: e.tensor_add(
                                xbt[:, n * 512:(n + 1) * 512], pb[2 + n][:, :],
                                bproj[:, n * 512:(n + 1) * 512]), inc=0)
                        vec(lambda e, m=m: e.tensor_add(xbt, xbt, xgx[:, m, :]))
                        b.wait("gpsimd", DV)
                        dma("gpsimd", XS[st_:st_ + 128, :], xbt, ST)
                        xbt_guard[0] = ST.v
                    xgx_guard[0] = DV.v
                    ofm_guard[g % 2] = PE.v
            xgr_guard[0] = DV.v

    if n_sublayers >= 4:
        emit_ffn(4, XS, out, dbg=None)

    if n_sublayers < 4 or dbg is not None:
        b.wait("gpsimd", ST)
        b.wait("gpsimd", DV)
        b.wait("gpsimd", AC)
        if dbg == "PER":
            for i in range(3):
                dma("gpsimd", out[128 * i:128 * (i + 1), :], per_t[:, 1024 * i:1024 * (i + 1)], ST)
            dma("gpsimd", out[384:512, 0:3840 - 3072], per_t[:, 3072:3840], ST)
            b.wait("gpsimd", ST)
            srcT = None
        else:
            srcT = {None: XS, "XS": XS, "YB": YB, "OB": OB, "HQ": OB, "H2": H2, "W2Q": H2, "WQ1": WQ1, "WQ2": WQ2}[dbg]
        nrow = srcT.tensor.shape[0] if srcT is not None else 0
        for i in range(nrow // 128):
            dma("gpsimd", wld[i % 2][:, 0:D], srcT[128 * i:128 * (i + 1), 0:D], LD)
            b.wait("gpsimd", LD)
            dma("gpsimd", out[128 * i:128 * (i + 1), :], wld[i % 2][:, 0:D], ST)
        b.wait("gpsimd", ST)
    else:
        b.wait("gpsimd", ST)

    with nc.Block() as blk:
        for eng in B.ENGS:
            ops = b.q[eng]
            getattr(blk, eng)(lambda e, ops=ops: [f(e) for f in ops])
    es.close()
    return nc


def kernel(**inputs):
    per_core = host_prep(inputs)
    nc = build_program(n_sublayers=4)
    res = run_bass_kernel_spmd(nc, per_core, core_ids=list(range(NCORES)))
    return np.concatenate([r["out"] for r in res.results], axis=0).reshape(16, 2048, 1024)


# version 10
# speedup vs baseline: 1.1267x; 1.0988x over previous
import numpy as np
import ml_dtypes
import concourse.bass as bass
import concourse.mybir as mybir
from concourse.bass_utils import run_bass_kernel_spmd
from contextlib import ExitStack

F32 = mybir.dt.float32
BF16 = mybir.dt.bfloat16
I32 = mybir.dt.int32
AF = mybir.ActivationFunctionType
ALU = mybir.AluOpType
AX = mybir.AxisListType
bf16 = ml_dtypes.bfloat16

NCORES = 8
SEQ = 2048
NSEQ = 2
T = SEQ * NSEQ
D = 1024
DH = 4096
G = 512
NG = T // G
GPS = SEQ // G
NSUB = 4
HA, DKA = 16, 64
LN_EPS = 1e-5

class Ctr:
    def __init__(self, handle):
        self.h = handle; self.v = 0

class B:
    ENGS = ("sync", "gpsimd", "scalar", "vector", "tensor")
    def __init__(self):
        self.q = {e: [] for e in self.ENGS}
    def wait(self, eng, ctr, val=None):
        v = ctr.v if val is None else val
        if v <= 0: return
        h = ctr.h
        self.q[eng].append(lambda e: e.wait_ge(h, v))
    def _inc(self, eng, fn, ctr, amt):
        if ctr is None:
            self.q[eng].append(fn)
        else:
            h = ctr.h
            self.q[eng].append(lambda e: fn(e).then_inc(h, amt))
            ctr.v += amt

class Arena:
    def __init__(self, tens, width):
        self.t = tens; self.width = width; self.off = 0
    def alloc(self, n):
        assert self.off + n <= self.width, f"arena overflow {self.off}+{n}>{self.width}"
        v = self.t[:, self.off:self.off + n]
        self.off += n
        return v

def host_prep(inputs):
    x = np.asarray(inputs["L0_attn_a_inputs"])
    d = {}
    d["wq"] = np.ascontiguousarray(inputs["L0_attn_a_W_Q"])
    d["wk"] = np.ascontiguousarray(inputs["L0_attn_a_W_K"])
    d["wv"] = np.ascontiguousarray(inputs["L0_attn_a_W_V"])
    d["wfc"] = np.ascontiguousarray(inputs["L0_attn_a_W_fc"])
    d["gamma_bc"] = np.ascontiguousarray(np.broadcast_to(np.asarray(inputs["L0_attn_a_gamma"])[None, :], (128, D)))
    d["beta_bc"] = np.ascontiguousarray(np.broadcast_to(np.asarray(inputs["L0_attn_a_beta"])[None, :], (128, D)))
    d["wqkvT"] = np.ascontiguousarray(np.asarray(inputs["L0_attn_b_W_qkv"]).T)
    d["wprojT"] = np.ascontiguousarray(np.asarray(inputs["L0_attn_b_W_proj"]).T)
    d["bproj_bc"] = np.ascontiguousarray(np.broadcast_to(np.asarray(inputs["L0_attn_b_b_proj"])[None, :], (128, D)))
    d["w1T"] = np.ascontiguousarray(np.asarray(inputs["L0_ffn_w1"]).T)
    d["w2T"] = np.ascontiguousarray(np.asarray(inputs["L0_ffn_w2"]).T)
    cwa = np.asarray(inputs["L0_ffn_conv_w"])
    for j in range(3):
        d[f"cw{j}"] = np.ascontiguousarray(cwa[:, 0, j].reshape(32, 128).T)
    d["cb"] = np.ascontiguousarray(np.asarray(inputs["L0_ffn_conv_b"]).reshape(32, 128).T)
    d["ident"] = np.eye(128, dtype=np.float32)
    per_core = []
    for c in range(NCORES):
        m = dict(d)
        m["xin"] = np.ascontiguousarray(x[2 * c:2 * c + 2].reshape(T, D))
        per_core.append(m)
    return per_core


def build_program(n_sublayers=4, dbg=None):
    nc = bass.Bass("TRN2", target_bir_lowering=False)
    es = ExitStack()
    ap = {}
    def din(name, shape, dt=F32):
        ap[name] = nc.dram_tensor(name, list(shape), dt, kind="ExternalInput").ap()
    din("xin", (T, D))
    for w in ("wq", "wk", "wv", "wfc"):
        din(w, (D, D))
    din("gamma_bc", (128, D)); din("beta_bc", (128, D))
    din("wqkvT", (D, 3 * D)); din("wprojT", (D, D)); din("bproj_bc", (128, D))
    din("w1T", (D, DH)); din("w2T", (DH, D))
    for j in range(3):
        din(f"cw{j}", (128, 32))
    din("cb", (128, 32))
    din("ident", (128, 128))
    out = nc.dram_tensor("out", [T, D], F32, kind="ExternalOutput").ap()
    XS = nc.dram_tensor("XS", [T, D], F32, kind="Internal").ap()
    YB = nc.dram_tensor("YB", [T, D], BF16, kind="Internal").ap()
    OB = nc.dram_tensor("OB", [T, D], BF16, kind="Internal").ap()
    H2 = nc.dram_tensor("H2", [DH, T], F32, kind="Internal").ap()
    WQ1 = nc.dram_tensor("WQ1", [D, DH], BF16, kind="Internal").ap()
    WQ2 = nc.dram_tensor("WQ2", [DH, D], BF16, kind="Internal").ap()

    bfa_t = es.enter_context(nc.sbuf_tensor("bfa", [128, 38400], BF16))
    wta_t = es.enter_context(nc.sbuf_tensor("wta", [128, 32768], BF16))
    f32a_t = es.enter_context(nc.sbuf_tensor("f32a", [128, 10240], F32))
    per_t = es.enter_context(nc.sbuf_tensor("per", [128, 3904], F32))
    i32_t = es.enter_context(nc.sbuf_tensor("i32b", [128, 1024], I32))
    f32b_t = es.enter_context(nc.sbuf_tensor("f32b", [128, 1152], F32))
    gcv_t = es.enter_context(nc.sbuf_tensor("gcv", [128, 1032], F32))
    per = Arena(per_t, 3904)
    numbuf = f32b_t[:, 0:1088]

    pb = [es.enter_context(nc.psum_tensor(f"pb{i}", [128, 512], F32)) for i in range(5)]
    kvps = es.enter_context(nc.psum_tensor("kvps", [128, 1536], F32))

    ident = per.alloc(128)
    gamma = per.alloc(D); beta = per.alloc(D); bproj = per.alloc(D)
    cw = [per.alloc(32) for _ in range(3)]
    cbv = per.alloc(32)
    onesf = per.alloc(128)
    invws1 = per.alloc(1); invws2 = per.alloc(1)
    stat_mu = per.alloc(32); stat_iv = per.alloc(32)
    stat_mu2 = per.alloc(4); stat_iv2 = per.alloc(4)
    aq = per.alloc(32)
    rabq = per.alloc(32)
    Aq = per.alloc(32)
    rAq = per.alloc(32)
    scr = per.alloc(64)
    den8 = per.alloc(32)
    tiny = per.alloc(32)

    b = B()
    sems = {k: Ctr(es.enter_context(nc.semaphore(k))) for k in ("LD", "LW", "LT", "ST", "PE", "DV", "AC", "GP")}
    LD, LW, LT, ST, PE, DV, AC, GP = (sems[k] for k in ("LD", "LW", "LT", "ST", "PE", "DV", "AC", "GP"))

    def dma(qeng, outv, inv, ctr):
        b._inc(qeng, lambda e, o=outv, i=inv: e.dma_start(out=o, in_=i), ctr, 16)
    def dmaT(qeng, outv, inv, ctr):
        b._inc(qeng, lambda e, o=outv, i=inv: e.dma_start_transpose(o, i), ctr, 16)
    def vec(fn, inc=1):
        b._inc("vector", fn, DV if inc else None, 1)
    def act(fn, inc=1):
        b._inc("scalar", fn, AC if inc else None, 1)
    def gp(fn, inc=1):
        b._inc("gpsimd", fn, GP if inc else None, 1)
    def pe(fn, inc=1):
        b._inc("tensor", fn, PE if inc else None, 1)

    dma("sync", ident[:], ap["ident"][:, :], LD)
    dma("sync", gamma[:], ap["gamma_bc"][:, :], LD)
    dma("sync", beta[:], ap["beta_bc"][:, :], LD)
    dma("sync", bproj[:], ap["bproj_bc"][:, :], LD)
    for j in range(3):
        dma("sync", cw[j][:], ap[f"cw{j}"][:, :], LD)
    dma("sync", cbv[:], ap["cb"][:, :], LD)
    b.wait("vector", LD)
    vec(lambda e: e.memset(onesf[:], 1.0), inc=0)
    vec(lambda e: e.memset(scr[:], 0.0))

    def vtiny(fn):
        vec(fn, inc=1)
        for _ in range(16):
            b.q["vector"].append(lambda e: e.engine_nop())
    def atiny(fn):
        act(fn, inc=1)
        b.wait("scalar", AC)

    f32a = Arena(f32a_t, 10240)
    wld = [f32a.alloc(4096), f32a.alloc(4096)]
    wbf_a = Arena(bfa_t, 38400)
    wbf = wbf_a.alloc(4096)

    def quant_weight(wapT, nchunk, wqap, invws_col):
        cols = DH if nchunk == 8 else D
        guard = [DV.v, DV.v]
        accv = tiny[:, 0:nchunk] if nchunk <= 16 else None
        acc32 = per
        accview = tiny[:, 0:8] if nchunk == 8 else scr[:, 0:32]
        for c in range(nchunk):
            buf = wld[c % 2][:, :cols]
            b.wait("sync", DV, guard[c % 2])
            dma("sync", buf, wapT[128 * c:128 * (c + 1), :], LD)
            b.wait("vector", LD)
            vec(lambda e, buf=buf, c=c: e.reduce_sum(accview[:, c:c + 1], buf, axis=AX.X, apply_absolute_value=True))
            guard[c % 2] = DV.v
        b.wait("vector", DV)
        vec(lambda e: e.reduce_sum(scr[:, 40:41], accview, axis=AX.X))
        b.wait("vector", DV)
        b.wait("tensor", DV)
        pe(lambda e: e.matmul(pb[0][0:1, 0:1], onesf[:, 0:1], scr[:, 40:41], start=True, stop=True))
        nelem = float(128 * nchunk * cols)
        b.wait("vector", PE)
        vtiny(lambda e: e.tensor_scalar(scr[0:1, 41:42], pb[0][0:1, 0:1], 1.0 / nelem, 1e-5, op0=ALU.mult, op1=ALU.max))
        b.wait("tensor", DV)
        pe(lambda e: e.matmul(pb[0][:, 4:5], onesf[0:1, :], scr[0:1, 41:42], start=True, stop=True))
        b.wait("vector", PE)
        vtiny(lambda e: e.tensor_copy(invws_col[:, 0:1], pb[0][:, 4:5]))
        vtiny(lambda e: e.reciprocal(scr[:, 42:43], invws_col[:, 0:1]))
        wsv = scr[:, 42:43]
        st_guard = ST.v
        for c in range(nchunk):
            buf = wld[c % 2][:, :cols]
            b.wait("sync", DV, guard[c % 2])
            dma("sync", buf, wapT[128 * c:128 * (c + 1), :], LD)
            b.wait("vector", LD)
            b.wait("vector", ST, st_guard)
            for h0 in range(0, cols, 1024):
                hw = min(1024, cols - h0)
                vec(lambda e, buf=buf, h0=h0, hw=hw: e.tensor_scalar(i32_t[:, :hw], buf[:, h0:h0 + hw], wsv, None, op0=ALU.mult), inc=0)
                vec(lambda e, h0=h0, hw=hw: e.tensor_scalar(wbf[:, h0:h0 + hw], i32_t[:, :hw], -1.0, 1.0, op0=ALU.max, op1=ALU.min), inc=(1 if h0 + hw >= cols else 0))
            guard[c % 2] = DV.v
            b.wait("gpsimd", DV)
            dma("gpsimd", wqap[128 * c:128 * (c + 1), :], wbf[:, :cols], ST)
            st_guard = ST.v
    quant_weight(ap["w1T"], 8, WQ1, invws1)
    quant_weight(ap["w2T"], 32, WQ2, invws2)

    wta = Arena(wta_t, 32768)
    w_at = {}
    for w in ("wk", "wv", "wq", "wfc"):
        w_at[w] = wta.alloc(8 * D).rearrange("p (c n) -> p c n", c=8)
        for dch in range(8):
            dma("gpsimd", w_at[w][:, dch, :], ap[w][128 * dch:128 * (dch + 1), :], LW)
    p1_lw = LW.v

    bfa = Arena(bfa_t, 38400)
    yfm = [bfa.alloc(8 * G).rearrange("p (c n) -> p c n", c=8) for _ in range(2)]
    ofm = [bfa.alloc(8 * G).rearrange("p (c n) -> p c n", c=8) for _ in range(2)]
    Khold = bfa.alloc(D)
    Vext = bfa.alloc(HA * 65).rearrange("p (h c) -> p h c", h=HA)
    Qhold = bfa.alloc(8 * SEQ).rearrange("p (c n) -> p c n", c=8)
    KVblk = bfa.alloc(8 * 130).rearrange("p (q c) -> p q c", q=8)
    o_sb = bfa.alloc(D)
    ybf = bfa.alloc(D)
    xgr = f32a_t[:, 0:4096].rearrange("p (m c) -> p m c", m=4)
    xgx = f32a_t[:, 4096:8192].rearrange("p (m c) -> p m c", m=4)
    junk = f32a_t[:, 8192:9216]
    xbt = f32a_t[:, 9216:10240]
    st2 = per_t[:, 0:0]

    def group_ln(xg, base_col, n=NSUB, mu_arr=None, iv_arr=None):
        mu = mu_arr[:, base_col:base_col + n]
        iv = iv_arr[:, base_col:base_col + n]
        s_ = tiny[:, 0:n]; q_ = tiny[:, 4:4 + n]; m2 = tiny[:, 8:8 + n]
        for m in range(n):
            vec(lambda e, m=m: e.reduce_sum(s_[:, m:m + 1], xg[:, m, :], axis=AX.X), inc=(m == n - 1))
        b.wait("scalar", DV)
        for m in range(n):
            act(lambda e, m=m: e.activation(junk, xg[:, m, :], AF.Square, accum_out=q_[:, m:m + 1]), inc=(m == n - 1))
        b.wait("vector", AC)
        vtiny(lambda e: e.tensor_scalar(mu, s_, 1.0 / D, None, op0=ALU.mult))
        vtiny(lambda e: e.tensor_mul(m2, mu, mu))
        vtiny(lambda e: e.scalar_tensor_tensor(iv, q_, 1.0 / D, m2, op0=ALU.mult, op1=ALU.subtract))
        vtiny(lambda e: e.tensor_scalar(iv, iv, LN_EPS, None, op0=ALU.add))
        vtiny(lambda e: e.reciprocal(iv, iv))
        b.wait("scalar", DV)
        atiny(lambda e: e.activation(iv, iv, AF.Sqrt))
        b.wait("vector", AC)

    def kvcol(h):
        return (h // 7) * 512 + (h % 7) * 65 if h < 14 else 1024 + (h - 14) * 65

    if n_sublayers >= 1:
        xgr_guard = [DV.v]
        xgx_guard = [DV.v]
        yfm_guard = [0, 0]
        ofm_guard = [0, 0]
        ybf_guard = [0]
        osb_guard = [0]
        xbt_guard = [0]
        for s in range(NSEQ):
            ydone_st = [0] * GPS
            for gi in range(GPS + 2):
                if gi >= 2:
                    g = gi - 2; t0 = (s * GPS + g) * G
                    yf = yfm[g % 2]
                    b.wait("sync", ST, ydone_st[g])
                    b.wait("sync", PE, yfm_guard[g % 2])
                    for dch in range(8):
                        dmaT("sync", yf[:, dch, :], YB[t0:t0 + G, 128 * dch:128 * (dch + 1)], LT)
                    b.wait("tensor", LT)
                    b.wait("tensor", LW, p1_lw)
                    if s == 0 and g == 0:
                        b.wait("vector", LD, 0)
                        vec(lambda e: e.memset(Vext[:, :, 64:65], 1.0))
                        b.wait("tensor", DV)
                    for m in range(NSUB):
                        for wname, bank in (("wk", 0), ("wv", 2)):
                            for n in range(2):
                                for dch in range(8):
                                    pe(lambda e, wname=wname, bank=bank, n=n, dch=dch, m=m, yf=yf: e.matmul(
                                        pb[bank + n][:, :],
                                        yf[:, dch, m * 128:(m + 1) * 128],
                                        w_at[wname][:, dch, n * 512:(n + 1) * 512],
                                        start=(dch == 0), stop=(dch == 7)),
                                        inc=(1 if dch == 7 else 0))
                        kv_pe = PE.v
                        b.wait("scalar", PE, kv_pe)
                        for n in range(2):
                            act(lambda e, n=n: e.activation(Khold[:, n * 512:(n + 1) * 512], pb[n][:, :], AF.Sigmoid))
                        k_ac = AC.v
                        b.wait("vector", PE, kv_pe)
                        for n in range(2):
                            vec(lambda e, n=n: e.tensor_copy(
                                Vext[:, 8 * n:8 * (n + 1), 0:64],
                                pb[2 + n][:, :]))
                        v_dv = DV.v
                        for qch in range(8):
                            if qch == 0:
                                bank = 4
                            else:
                                bank = (qch - 1) % 2
                                b.wait("tensor", AC, k_ac)
                            if qch >= 2:
                                b.wait("tensor", AC, AC.v - 2)
                            for dch in range(8):
                                pe(lambda e, bank=bank, qch=qch, dch=dch, m=m, yf=yf: e.matmul(
                                    pb[bank][:, 0:128],
                                    w_at["wq"][:, dch, qch * 128:(qch + 1) * 128],
                                    yf[:, dch, m * 128:(m + 1) * 128],
                                    start=(dch == 0), stop=(dch == 7)),
                                    inc=(1 if dch == 7 else 0))
                            b.wait("scalar", PE)
                            act(lambda e, bank=bank, qch=qch, g=g, m=m: e.activation(
                                Qhold[:, qch, (g * NSUB + m) * 128:(g * NSUB + m + 1) * 128],
                                pb[bank][:, 0:128], AF.Sigmoid))
                        b.wait("tensor", AC)
                        b.wait("tensor", DV, v_dv)
                        firstmn = (g == 0 and m == 0)
                        lastmn = (g == GPS - 1 and m == NSUB - 1)
                        for h in range(HA):
                            first = firstmn and h in (0, 7, 14)
                            last = lastmn and h in (6, 13, 15)
                            pe(lambda e, h=h, first=first, last=last: e.matmul(
                                kvps[0:64, kvcol(h):kvcol(h) + 65],
                                Khold[:, 64 * h:64 * (h + 1)],
                                Vext[:, h, :],
                                start=first, stop=last),
                                inc=(1 if (lastmn and h == HA - 1) else 0))
                    yfm_guard[g % 2] = PE.v
                if gi < GPS:
                    g = gi; t0 = (s * GPS + g) * G
                    b.wait("sync", DV, xgr_guard[0])
                    dma("sync", xgr, ap["xin"][t0:t0 + G, :].rearrange("(m p) c -> p m c", p=128), LD)
                    b.wait("vector", LD)
                    gcol = (s * GPS + g) * NSUB
                    group_ln(xgr, gcol, NSUB, stat_mu, stat_iv)
                    for m in range(NSUB):
                        st_ = t0 + m * 128
                        col = gcol + m
                        b.wait("vector", ST, ybf_guard[0])
                        vec(lambda e, m=m, col=col: e.tensor_scalar(
                            ybf, xgr[:, m, :], stat_mu[:, col:col + 1], stat_iv[:, col:col + 1],
                            op0=ALU.subtract, op1=ALU.mult))
                        b.wait("gpsimd", DV)
                        dma("gpsimd", YB[st_:st_ + 128, :], ybf, ST)
                        ybf_guard[0] = ST.v
                    xgr_guard[0] = DV.v
                    ydone_st[g] = ST.v
            b.wait("vector", PE)
            vec(lambda e: e.memset(KVblk[:, :, :], 0.0), inc=0)
            for h in range(HA):
                q_, r_ = h // 2, h % 2
                vec(lambda e, h=h, q_=q_, r_=r_: e.tensor_copy(
                    KVblk[64 * r_:64 * (r_ + 1), q_, 65 * r_:65 * r_ + 65],
                    kvps[0:64, kvcol(h):kvcol(h) + 65]), inc=(1 if h == HA - 1 else 0))
            kv_dv = DV.v

            odone_st = [0] * GPS
            evdv = [DV.v, DV.v]
            nbuf_ac = [AC.v]
            for gi in range(GPS + 2):
                if gi < GPS:
                    g = gi; t0 = (s * GPS + g) * G
                    b.wait("tensor", DV, kv_dv)
                    for m in range(NSUB):
                        st_ = t0 + m * 128
                        b.wait("vector", AC, nbuf_ac[0])
                        for p_ in range(8):
                            bank = p_ % 2
                            b.wait("tensor", DV, evdv[bank])
                            pe(lambda e, p_=p_, bank=bank, g=g, m=m: e.matmul(
                                pb[bank][:, 0:130],
                                Qhold[:, p_, (g * NSUB + m) * 128:(g * NSUB + m + 1) * 128],
                                KVblk[:, p_, :],
                                start=True, stop=True))
                            b.wait("vector", PE)
                            vec(lambda e, bank=bank, p_=p_: e.tensor_copy(
                                numbuf[:, 130 * p_:130 * p_ + 130], pb[bank][:, 0:130]))
                            evdv[bank] = DV.v
                        nb3 = numbuf[:, 0:1040].rearrange("p (q c) -> p q c", c=130)
                        for j in range(2):
                            vec(lambda e, j=j, nb3=nb3: e.tensor_scalar(
                                den8[:, 8 * j:8 * j + 8], nb3[:, :, 64 + 65 * j:65 + 65 * j],
                                1e-8, None, op0=ALU.add), inc=(j == 1))
                        b.wait("vector", DV)
                        vtiny(lambda e: e.reciprocal(den8[:, 0:16], den8[:, 0:16]))
                        b.wait("scalar", DV)
                        b.wait("scalar", ST, osb_guard[0])
                        for p_ in range(8):
                            for j in range(2):
                                act(lambda e, p_=p_, j=j: e.activation(
                                    o_sb[:, 128 * p_ + 64 * j:128 * p_ + 64 * (j + 1)],
                                    numbuf[:, 130 * p_ + 65 * j:130 * p_ + 65 * j + 64],
                                    AF.Copy, scale=den8[:, 8 * j + p_:8 * j + p_ + 1]),
                                    inc=(1 if (p_ == 7 and j == 1) else 0))
                        nbuf_ac[0] = AC.v
                        b.wait("gpsimd", AC)
                        dma("gpsimd", OB[st_:st_ + 128, :], o_sb, ST)
                        osb_guard[0] = ST.v
                    odone_st[g] = ST.v
                if gi >= 2:
                    g = gi - 2; t0 = (s * GPS + g) * G
                    of = ofm[g % 2]
                    b.wait("sync", ST, odone_st[g])
                    b.wait("sync", PE, ofm_guard[g % 2])
                    for dch in range(8):
                        dmaT("sync", of[:, dch, :], OB[t0:t0 + G, 128 * dch:128 * (dch + 1)], LT)
                    b.wait("sync", DV, xgx_guard[0])
                    dma("sync", xgx, ap["xin"][t0:t0 + G, :].rearrange("(m p) c -> p m c", p=128), LD)
                    b.wait("tensor", LT)
                    gcol = (s * GPS + g) * NSUB
                    for m in range(NSUB):
                        col = gcol + m
                        b.wait("vector", LD)
                        b.wait("vector", ST, xbt_guard[0])
                        vec(lambda e, m=m, col=col: e.tensor_scalar(
                            xbt, xgx[:, m, :], stat_mu[:, col:col + 1], stat_iv[:, col:col + 1],
                            op0=ALU.subtract, op1=ALU.mult))
                        fcdv = DV.v
                        b.wait("tensor", DV, fcdv)
                        for n in range(2):
                            for dch in range(8):
                                pe(lambda e, n=n, dch=dch, m=m, of=of: e.matmul(
                                    pb[2 + n][:, :],
                                    of[:, dch, m * 128:(m + 1) * 128],
                                    w_at["wfc"][:, dch, n * 512:(n + 1) * 512],
                                    start=(dch == 0), stop=(dch == 7)),
                                    inc=(1 if dch == 7 else 0))
                        b.wait("vector", PE)
                        for n in range(2):
                            vec(lambda e, n=n, m=m: e.tensor_add(
                                xgr[:, m, n * 512:(n + 1) * 512], pb[2 + n][:, :],
                                xbt[:, n * 512:(n + 1) * 512]), inc=(1 if n == 1 else 0))
                        xbt_guard[0] = 0
                    group_ln(xgr, 0, NSUB, stat_mu2, stat_iv2)
                    for m in range(NSUB):
                        st_ = t0 + m * 128
                        vec(lambda e, m=m: e.tensor_scalar(
                            junk, xgr[:, m, :], stat_mu2[:, m:m + 1], stat_iv2[:, m:m + 1],
                            op0=ALU.subtract, op1=ALU.mult), inc=0)
                        vec(lambda e: e.tensor_mul(junk, junk, gamma[:, :]), inc=0)
                        b.wait("vector", ST, xbt_guard[0])
                        vec(lambda e, m=m: e.tensor_add(xbt, xgx[:, m, :], beta[:, :]), inc=0)
                        vec(lambda e: e.tensor_add(xbt, xbt, junk))
                        b.wait("gpsimd", DV)
                        dma("gpsimd", XS[st_:st_ + 128, :], xbt, ST)
                        xbt_guard[0] = ST.v
                    xgx_guard[0] = DV.v
                    xgr_guard[0] = DV.v
                    ofm_guard[g % 2] = PE.v

    f_xg   = f32a_t[:, 0:4096].rearrange("p (m c) -> p m c", m=4)
    f_xg2  = f32a_t[:, 4096:8192].rearrange("p (m c) -> p m c", m=4)
    f_hs   = [f32a_t[:, 4096:4612], f32a_t[:, 4612:5128]]
    f_cva  = f32a_t[:, 5128:5644]
    f_h2s  = [f32a_t[:, 5644:6157], f32a_t[:, 6157:6670]]
    f_m1x  = [f32a_t[:, 6670:7183], f32a_t[:, 7183:7696]]
    f_m1n  = [f32a_t[:, 7696:8192][:, 0:496], None]
    f_junk = f32a_t[:, 8192:9216]
    f_xbt  = f32a_t[:, 9216:10240]
    del f_hs, f_cva, f_h2s, f_m1x, f_m1n
    f_hs  = [f32a_t[:, 4096:4612], f32a_t[:, 4612:5128]]
    f_cva = f32a_t[:, 5128:5644]
    f_cva2 = [f32a_t[:, 5128:5644], f32b_t[:, 0:516]]
    f_h2s = f32a_t[:, 5644:6157]
    f_h2s2 = [f32a_t[:, 5644:6157], f32b_t[:, 516:1029]]
    f_m1x = f32a_t[:, 6157:6670]
    f_m1n = f32a_t[:, 6670:7183]
    f_k3x = f32a_t[:, 7183:7311]
    f_k3n = f32a_t[:, 7311:7439]
    f_bc  = f32a_t[:, 7439:7951]
    f_dg  = f32a_t[:, 5128:5640]
    f_h2i = [f32a_t[:, 8192:8704], f32a_t[:, 8704:9216]]
    f_mq  = f32a_t[:, 9216:9728]
    f_bcB = f32a_t[:, 9728:10240]

    halo = per.alloc(64)

    def emit_ffn(phase, src_XS, dst_XS, dbg=None):
        b.wait("gpsimd", PE)
        b.wait("gpsimd", ST)
        w1q = wta_t[:, 0:32768].rearrange("p (c n) -> p c n", c=8)
        for dch in range(8):
            dma("gpsimd", w1q[:, dch, :], WQ1[128 * dch:128 * (dch + 1), :], LW)
        lw_a = LW.v
        xs_st = ST.v
        h1bank_guard = [DV.v] * 6
        xg_guard = [DV.v]
        xg_guard_ac = [AC.v]
        yfm_guard2 = [PE.v, PE.v]
        ybf_guard2 = [ST.v, ST.v]
        ybf2 = bfa_t[:, 17408:18432]
        prevk3 = [None]
        h2st_done = [0]
        h2s_guard = [ST.v, ST.v]
        xbt_st_guard = [ST.v]

        def finish_amax_batch(col0, srcs):
            n = len(srcs)
            b.wait("tensor", DV)
            b.wait("tensor", GP)
            for i, (sx, sn) in enumerate(srcs):
                pe(lambda e, sx=sx, i=i: e.transpose(pb[3][:, 0:128] if False else pb[3][:, 0:128], sx, ident), inc=0) if False else None
            waves = [srcs[i:i + 2] for i in range(0, n, 2)]
            base = 0
            for w_i, wave in enumerate(waves):
                for i, (sx, sn) in enumerate(wave):
                    pe(lambda e, sx=sx, o=(2 * i) * 128: e.transpose(pb[3][:, o:o + 128], sx, ident), inc=0)
                    pe(lambda e, sn=sn, o=(2 * i + 1) * 128: e.transpose(pb[3][:, o:o + 128], sn, ident),
                       inc=(1 if i == len(wave) - 1 else 0))
                b.wait("vector", PE)
                for i in range(len(wave)):
                    j = w_i * 2 + i
                    vec(lambda e, i=i, j=j: e.reduce_max(tiny[:, 16 + j:17 + j], pb[3][:, (2 * i) * 128:(2 * i) * 128 + 128], axis=AX.X), inc=0)
                    vec(lambda e, i=i, j=j: e.tensor_reduce(tiny[:, 21 + j:22 + j], pb[3][:, (2 * i + 1) * 128:(2 * i + 1) * 128 + 128], axis=AX.X, op=ALU.min),
                        inc=(1 if i == len(wave) - 1 else 0))
                if w_i + 1 < len(waves):
                    b.wait("tensor", DV)
            b.wait("vector", DV)
            xs_ = tiny[:, 16:16 + n]; ns_ = tiny[:, 21:21 + n]
            vtiny(lambda e, xs_=xs_, ns_=ns_: e.scalar_tensor_tensor(xs_, ns_, -1.0, xs_, op0=ALU.mult, op1=ALU.max))
            vtiny(lambda e, xs_=xs_: e.tensor_scalar(xs_, xs_, 1e-5, None, op0=ALU.max))
            vtiny(lambda e, col0=col0, n=n, xs_=xs_: e.tensor_scalar(rAq[:, col0:col0 + n], xs_, invws2[:, 0:1], 1.0 / 127.0, op0=ALU.mult, op1=ALU.mult))
            vtiny(lambda e, ns_=ns_, xs_=xs_: e.reciprocal(ns_, xs_))
            vtiny(lambda e, col0=col0, n=n, ns_=ns_: e.tensor_scalar(Aq[:, col0:col0 + n], ns_, 127.0, None, op0=ALU.mult))

        for gi in range(NG + 2):
            if gi >= 2:
                g = gi - 2; t0 = g * G
                gcol = g * NSUB
                seq_start = (g % GPS == 0)
                seq_end = (g % GPS == GPS - 1)
                w = 511 if seq_start else (513 if seq_end else 512)
                wstart_off = 0 if seq_start else 1
                yf = yfm[g % 2]
                b.wait("sync", ST)
                b.wait("sync", PE, yfm_guard2[g % 2])
                for dch in range(8):
                    dmaT("sync", yf[:, dch, :], YB[t0:t0 + G, 128 * dch:128 * (dch + 1)], LT)
                b.wait("vector", PE)
                b.wait("vector", AC)
                for m in range(NSUB):
                    vec(lambda e, m=m, gcol=gcol: e.tensor_scalar(
                        f_dg[:, 128 * m:128 * (m + 1)], ident[:, :], rabq[:, gcol + m:gcol + m + 1], None,
                        op0=ALU.mult), inc=(m == NSUB - 1))
                b.wait("tensor", DV)
                b.wait("tensor", LT)
                b.wait("tensor", LW, lw_a)
                for m in range(NSUB):
                    pe(lambda e, m=m: e.matmul(pb[2][:, 128 * m:128 * (m + 1)], onesf[:, :], f_dg[:, 128 * m:128 * (m + 1)], start=True, stop=True), inc=(m == NSUB - 1))
                b.wait("vector", PE)
                vec(lambda e: e.tensor_copy(f_bc, pb[2][:, :]))
                if seq_start:
                    vec(lambda e: e.memset(halo[:, :], 0.0))
                b.wait("gpsimd", PE)
                b.wait("gpsimd", DV)
                gp(lambda e: e.memset(f_m1x, -3.0e38), inc=0)
                gp(lambda e: e.memset(f_m1n, 3.0e38))
                b.wait("vector", GP)
                silu_ac = [AC.v, AC.v]
                m1_dv = [DV.v, DV.v]
                h1banks = [pb[0][:, :], pb[1][:, :], pb[4][:, :],
                           kvps[:, 0:512], kvps[:, 512:1024], kvps[:, 1024:1536]]
                m1_pend = [None]
                halo_pend = [None]
                gp_done = {}
                gsilu_ac = [AC.v]
                for c in range(32):
                    is_gp = False
                    hbk = h1banks[c % 6]
                    hsb = f_hs[c % 2]
                    b.wait("tensor", DV, h1bank_guard[c % 6])
                    for dch in range(8):
                        pe(lambda e, hbk=hbk, c=c, dch=dch, yf=yf: e.matmul(
                            hbk,
                            w1q[:, dch, 128 * c:128 * (c + 1)],
                            yf[:, dch, :],
                            start=(dch == 0), stop=(dch == 7)),
                            inc=(1 if dch == 7 else 0))
                    if m1_pend[0] is not None:
                        m1_pend[0](); m1_pend[0] = None
                    if halo_pend[0] is not None:
                        halo_pend[0](); halo_pend[0] = None
                    if (c - 2) in gp_done:
                        b.wait("vector", GP, gp_done.pop(c - 2))
                    vec(lambda e, hsb=hsb, c=c: e.tensor_copy(hsb[:, 0:2], halo[:, 2 * c:2 * c + 2]), inc=0)
                    if seq_end:
                        vec(lambda e, hsb=hsb: e.memset(hsb[:, 514:516], 0.0), inc=0)
                    b.wait("vector", PE)
                    vec(lambda e, hsb=hsb, hbk=hbk: e.tensor_mul(hsb[:, 2:514], hbk, f_bc))
                    h1bank_guard[c % 6] = DV.v
                    halo_pend[0] = (lambda hsb=hsb, c=c: vec(
                        lambda e: e.tensor_copy(halo[:, 2 * c:2 * c + 2], hsb[:, 512:514])))
                    h2s = f_h2s2[c % 2]
                    lo = 1 if seq_start else 0
                    if is_gp:
                        gA = gcv_t[:, 0:516]; gB = gcv_t[:, 516:1032]
                        b.wait("gpsimd", DV)
                        b.wait("gpsimd", AC, gsilu_ac[0])
                        gp(lambda e, hsb=hsb, c=c, lo=lo, w=w, gA=gA: e.tensor_scalar(
                            gA[:, 0:w], hsb[:, lo + 1:lo + 1 + w], cw[1][:, c:c + 1], None, op0=ALU.mult), inc=0)
                        gp(lambda e, hsb=hsb, c=c, lo=lo, w=w, gB=gB: e.tensor_scalar(
                            gB[:, 0:w], hsb[:, lo:lo + w], cw[0][:, c:c + 1], None, op0=ALU.mult), inc=0)
                        gp(lambda e, w=w, gA=gA, gB=gB: e.tensor_tensor(gA[:, 0:w], gA[:, 0:w], gB[:, 0:w], op=ALU.add), inc=0)
                        gp(lambda e, hsb=hsb, c=c, lo=lo, w=w, gB=gB: e.tensor_scalar(
                            gB[:, 0:w], hsb[:, lo + 2:lo + 2 + w], cw[2][:, c:c + 1], None, op0=ALU.mult), inc=0)
                        gp(lambda e, w=w, gA=gA, gB=gB: e.tensor_tensor(gA[:, 0:w], gA[:, 0:w], gB[:, 0:w], op=ALU.add))
                        gp_done[c] = GP.v
                        b.wait("scalar", GP)
                        b.wait("scalar", ST, h2s_guard[c % 2])
                        b.wait("scalar", DV, m1_dv[c % 2])
                        act(lambda e, h2s=h2s, c=c, w=w, gA=gA: e.activation(h2s[:, 0:w], gA[:, 0:w], AF.Silu, bias=cbv[:, c:c + 1]))
                        gsilu_ac[0] = AC.v
                    else:
                        cva = f_cva2[c % 2]
                        b.wait("vector", AC, silu_ac[c % 2])
                        vec(lambda e, hsb=hsb, cva=cva, c=c, lo=lo, w=w: e.tensor_scalar(
                            cva[:, 0:w], hsb[:, lo + 1:lo + 1 + w], cw[1][:, c:c + 1], None, op0=ALU.mult), inc=0)
                        vec(lambda e, hsb=hsb, cva=cva, c=c, lo=lo, w=w: e.scalar_tensor_tensor(
                            cva[:, 0:w], hsb[:, lo:lo + w], cw[0][:, c:c + 1], cva[:, 0:w],
                            op0=ALU.mult, op1=ALU.add), inc=0)
                        vec(lambda e, hsb=hsb, cva=cva, c=c, lo=lo, w=w: e.scalar_tensor_tensor(
                            cva[:, 0:w], hsb[:, lo + 2:lo + 2 + w], cw[2][:, c:c + 1], cva[:, 0:w],
                            op0=ALU.mult, op1=ALU.add))
                        b.wait("scalar", DV)
                        b.wait("scalar", ST, h2s_guard[c % 2])
                        b.wait("scalar", DV, m1_dv[c % 2])
                        act(lambda e, cva=cva, h2s=h2s, c=c, w=w: e.activation(h2s[:, 0:w], cva[:, 0:w], AF.Silu, bias=cbv[:, c:c + 1]))
                        silu_ac[c % 2] = AC.v
                    def m1_emit(h2s=h2s, w=w, cc=c, ac_at=AC.v, gp_at=GP.v):
                        b.wait("vector", AC, ac_at)
                        b.wait("vector", GP, gp_at)
                        vec(lambda e: e.tensor_tensor(f_m1x[:, 0:w], f_m1x[:, 0:w], h2s[:, 0:w], op=ALU.max), inc=0)
                        vec(lambda e: e.tensor_tensor(f_m1n[:, 0:w], f_m1n[:, 0:w], h2s[:, 0:w], op=ALU.min))
                        m1_dv[cc % 2] = DV.v
                    m1_pend[0] = m1_emit
                    b.wait("sync", AC)
                    tkn0 = t0 - wstart_off
                    dma("sync", H2[128 * c:128 * (c + 1), tkn0:tkn0 + w], h2s[:, 0:w], ST)
                    h2s_guard[c % 2] = ST.v
                if m1_pend[0] is not None:
                    m1_pend[0](); m1_pend[0] = None
                if halo_pend[0] is not None:
                    halo_pend[0](); halo_pend[0] = None
                h2st_done[0] = ST.v
                batch = []
                col0 = gcol
                b.wait("vector", GP)
                if prevk3[0] is not None:
                    if not seq_start:
                        vtiny(lambda e: e.tensor_copy(f_k3x[:, 127:128], f_m1x[:, 0:1]))
                        vtiny(lambda e: e.tensor_copy(f_k3n[:, 127:128], f_m1n[:, 0:1]))
                    col0 = prevk3[0][0]
                    batch.append((f_k3x, f_k3n))
                    prevk3[0] = None
                for k in range(3):
                    sl = (128 * k + (0 if seq_start else 1))
                    batch.append((f_m1x[:, sl:sl + 128], f_m1n[:, sl:sl + 128]))
                sl3 = 128 * 3 + (0 if seq_start else 1)
                if seq_end:
                    batch.append((f_m1x[:, sl3:sl3 + 128], f_m1n[:, sl3:sl3 + 128]))
                else:
                    vtiny(lambda e, sl3=sl3: e.tensor_copy(f_k3x[:, 0:127], f_m1x[:, sl3:sl3 + 127]))
                    vtiny(lambda e, sl3=sl3: e.tensor_copy(f_k3n[:, 0:127], f_m1n[:, sl3:sl3 + 127]))
                    prevk3[0] = (gcol + 3,)
                finish_amax_batch(col0, batch)
                yfm_guard2[g % 2] = PE.v

            if gi < NG:
                g = gi; t0 = g * G
                gcol = g * NSUB
                b.wait("sync", DV, xg_guard[0])
                b.wait("sync", AC, xg_guard_ac[0] if isinstance(xg_guard_ac, list) else 0)
                b.wait("sync", ST, xs_st)
                dma("sync", f_xg[:, :, :], src_XS[t0:t0 + G, :].rearrange("(m p) c -> p m c", p=128), LD)
                b.wait("vector", LD)
                group_ln(f_xg, gcol, NSUB, stat_mu, stat_iv)
                for m in range(NSUB):
                    col = gcol + m
                    b.wait("vector", ST, xbt_st_guard[0])
                    vec(lambda e, m=m, col=col: e.tensor_scalar(
                        f_xbt, f_xg[:, m, :], stat_mu[:, col:col + 1], None, op0=ALU.subtract), inc=0)
                    vec(lambda e, m=m: e.reduce_max(tiny[:, m:m + 1], f_xbt, axis=AX.X, apply_absolute_value=True))
                    b.wait("vector", DV)
                iv4 = stat_iv[:, gcol:gcol + 4]
                vtiny(lambda e, iv4=iv4: e.tensor_mul(tiny[:, 4:8], tiny[:, 0:4], iv4))
                vtiny(lambda e: e.tensor_scalar(tiny[:, 4:8], tiny[:, 4:8], 1e-5, None, op0=ALU.max))
                vtiny(lambda e, gcol=gcol: e.tensor_scalar(rabq[:, gcol:gcol + 4], tiny[:, 4:8], invws1[:, 0:1], 1.0 / 127.0, op0=ALU.mult, op1=ALU.mult))
                vtiny(lambda e: e.reciprocal(tiny[:, 8:12], tiny[:, 4:8]))
                vtiny(lambda e: e.tensor_scalar(tiny[:, 8:12], tiny[:, 8:12], 127.0, None, op0=ALU.mult))
                vtiny(lambda e, gcol=gcol, iv4=iv4: e.tensor_mul(aq[:, gcol:gcol + 4], tiny[:, 8:12], iv4))
                vtiny(lambda e, gcol=gcol: e.tensor_mul(tiny[:, 0:4], stat_mu[:, gcol:gcol + 4], aq[:, gcol:gcol + 4]))
                vtiny(lambda e: e.tensor_scalar(tiny[:, 0:4], tiny[:, 0:4], -1.0, None, op0=ALU.mult))
                b.wait("scalar", DV)
                for m in range(NSUB):
                    st_ = t0 + m * 128
                    col = gcol + m
                    yb2 = ybf if m % 2 == 0 else ybf2
                    b.wait("scalar", ST, ybf_guard2[m % 2])
                    act(lambda e, m=m, col=col: e.activation(
                        i32_t[:, 0:D], f_xg[:, m, :], AF.Identity,
                        scale=aq[:, col:col + 1], bias=tiny[:, m:m + 1]), inc=0)
                    act(lambda e, yb2=yb2: e.activation(yb2, i32_t[:, 0:D], AF.Copy))
                    b.wait("gpsimd", AC)
                    dma("gpsimd", YB[st_:st_ + 128, :], yb2, ST)
                    ybf_guard2[m % 2] = ST.v
                xg_guard[0] = DV.v
                xg_guard_ac[0] = AC.v
        b.wait("gpsimd", PE)
        w2q = wta_t[:, 0:32768].rearrange("p (c n) -> p c n", c=32)
        for c2 in range(0, 32, 4):
            dma("gpsimd", w2q[:, c2:c2 + 4, :], WQ2[128 * c2:128 * (c2 + 4), :].rearrange("(c p) n -> p c n", p=128), LW)
        lw_b = LW.v
        hq = bfa_t[:, 16384:32768].rearrange("p (c n) -> p c n", c=32)
        if dbg == "W2Q":
            b.wait("gpsimd", LW)
            dma("gpsimd", H2[0:4096, 0:1024].rearrange("(x p) c -> p x c", p=128), wta_t[:, :].rearrange("p (x c) -> p x c", x=32), ST)
        b.wait("sync", ST, h2st_done[0])
        xo_guard = [ST.v]
        x2_guard = [DV.v]
        wevac_dv = [DV.v]
        for g in range(NG):
            t0 = g * G
            gcol = g * NSUB
            b.wait("vector", PE)
            b.wait("vector", AC)
            for m in range(NSUB):
                vec(lambda e, m=m, gcol=gcol: e.tensor_scalar(
                    f_mq[:, 128 * m:128 * (m + 1)], ident[:, :], Aq[:, gcol + m:gcol + m + 1], None,
                    op0=ALU.mult), inc=(m == NSUB - 1))
            b.wait("tensor", DV)
            b.wait("tensor", ST, h2st_done[0])
            b.wait("tensor", LW, lw_b)
            for m in range(NSUB):
                pe(lambda e, m=m: e.matmul(pb[2][:, 128 * m:128 * (m + 1)], onesf[:, :], f_mq[:, 128 * m:128 * (m + 1)], start=True, stop=True), inc=(m == NSUB - 1))
            b.wait("vector", PE)
            vec(lambda e: e.tensor_copy(f_bcB, pb[2][:, :]))
            b.wait("sync", DV, x2_guard[0])
            b.wait("sync", ST, xs_st)
            dma("sync", f_xg2[:, :, :], src_XS[t0:t0 + G, :].rearrange("(m p) c -> p m c", p=128), LD)
            hq_gp = [0] * 32
            b.wait("gpsimd", DV)
            mq2 = [f_mq, f32b_t[:, 0:512]]
            mq_ac = [AC.v, AC.v]
            cast_ac = [0] * 32
            wbanks = [pb[0][:, :], pb[1][:, :], pb[3][:, :], pb[4][:, :],
                      kvps[:, 0:512], kvps[:, 512:1024], kvps[:, 1024:1536], pb[2][:, :]]
            b.wait("tensor", DV, wevac_dv[0])
            b.wait("vector", LD)
            for cc in range(34):
                if cc < 32:
                    c = cc
                    hb = f_h2i[c % 2]
                    mq = mq2[c % 2]
                    b.wait("sync", GP, hq_gp[c - 2] if c >= 2 else 0)
                    dma("sync", hb, H2[128 * c:128 * (c + 1), t0:t0 + G], LD)
                    b.wait("gpsimd", LD)
                    b.wait("gpsimd", AC, mq_ac[c % 2])
                    gp(lambda e, hb=hb, mq=mq: e.tensor_tensor(mq, hb, f_bcB, op=ALU.mult))
                    hq_gp[c] = GP.v
                    b.wait("scalar", GP)
                    act(lambda e, mq=mq: e.activation(i32_t[:, 0:G], mq, AF.Copy), inc=0)
                    act(lambda e, c=c: e.activation(hq[:, c, :], i32_t[:, 0:G], AF.Copy))
                    mq_ac[c % 2] = AC.v
                    cast_ac[c] = AC.v
                if cc >= 2:
                    c = cc - 2
                    b.wait("tensor", AC, cast_ac[c])
                    for k8 in range(8):
                        n, m = k8 // 4, k8 % 4
                        pe(lambda e, n=n, m=m, c=c, k8=k8, wbanks=wbanks: e.matmul(
                            wbanks[k8],
                            hq[:, c, 128 * m:128 * (m + 1)],
                            w2q[:, c, 512 * n:512 * (n + 1)],
                            start=(c == 0), stop=(c == 31)),
                            inc=(1 if (c == 31 and k8 == 7) else 0))
            if dbg == "HQ" and g == 0:
                b.wait("gpsimd", DV)
                dma("gpsimd", OB[0:2048, :].rearrange("(x p) c -> p x c", p=128), hq[:, :, :].rearrange("p c t -> p (c t)").rearrange("p (x c) -> p x c", x=16), ST)
            b.wait("vector", PE)
            b.wait("vector", ST, xo_guard[0])
            for k8 in range(8):
                n, m = k8 // 4, k8 % 4
                vec(lambda e, n=n, m=m, k8=k8, gcol=gcol, wbanks=wbanks: e.scalar_tensor_tensor(
                    f_xg[:, m, 512 * n:512 * (n + 1)], wbanks[k8], rAq[:, gcol + m:gcol + m + 1],
                    f_xg2[:, m, 512 * n:512 * (n + 1)], op0=ALU.mult, op1=ALU.add),
                    inc=(1 if k8 == 7 else 0))
            wevac_dv[0] = DV.v
            x2_guard[0] = DV.v
            b.wait("gpsimd", DV)
            for m in range(NSUB):
                st_ = t0 + m * 128
                dma("gpsimd", dst_XS[st_:st_ + 128, :], f_xg[:, m, :], ST)
            xo_guard[0] = ST.v

    if n_sublayers >= 2:
        emit_ffn(2, XS, XS if n_sublayers > 2 else XS, dbg=dbg)

    HB, DKB = 32, 32
    if n_sublayers >= 3:
        b.wait("gpsimd", PE)
        wta3 = Arena(wta_t, 32768)
        wqkv_sb = wta3.alloc(8 * 3 * D).rearrange("p (c n) -> p c n", c=8)
        wproj_sb = wta3.alloc(8 * D).rearrange("p (c n) -> p c n", c=8)
        for dch in range(8):
            dma("gpsimd", wqkv_sb[:, dch, :], ap["wqkvT"][128 * dch:128 * (dch + 1), :], LW)
            dma("gpsimd", wproj_sb[:, dch, :], ap["wprojT"][128 * dch:128 * (dch + 1), :], LW)
        p3_lw = LW.v
        bfa3 = Arena(bfa_t, 38400)
        yfm3 = [bfa3.alloc(8 * G).rearrange("p (c n) -> p c n", c=8) for _ in range(2)]
        ofm3 = [bfa3.alloc(8 * G).rearrange("p (c n) -> p c n", c=8) for _ in range(2)]
        k2_sb = bfa3.alloc(D)
        v2ext = bfa3.alloc(HB * 33).rearrange("p (h c) -> p h c", h=HB)
        Q2hold = bfa3.alloc(8 * SEQ).rearrange("p (c n) -> p c n", c=8)
        vkblk = bfa3.alloc(8 * 132).rearrange("p (q c) -> p q c", q=8)
        o2_sb = bfa3.alloc(D)
        ybf3 = bfa3.alloc(D)

        def vk_col(h):
            b_ = min(h // 15, 2)
            return 512 * b_ + 33 * (h - 15 * b_)

        xs_st3 = ST.v
        xgr_guard = [DV.v]
        xgx_guard = [DV.v]
        yfm_guard = [PE.v, PE.v]
        ofm_guard = [PE.v, PE.v]
        ybf_guard = [ST.v]
        osb_guard = [ST.v]
        xbt_guard = [ST.v]
        first_phase = [True]
        for s in range(NSEQ):
            ydone_st = [0] * GPS
            for gi in range(GPS + 2):
                if gi >= 2:
                    g = gi - 2; t0 = (s * GPS + g) * G
                    yf = yfm3[g % 2]
                    b.wait("sync", ST, ydone_st[g])
                    b.wait("sync", PE, yfm_guard[g % 2])
                    for dch in range(8):
                        dmaT("sync", yf[:, dch, :], YB[t0:t0 + G, 128 * dch:128 * (dch + 1)], LT)
                    b.wait("tensor", LT)
                    b.wait("tensor", LW, p3_lw)
                    if first_phase[0] and g == 0:
                        b.wait("vector", PE)
                        vec(lambda e: e.memset(v2ext[:, :, 32:33], 1.0))
                        b.wait("tensor", DV)
                        first_phase[0] = False
                    for m in range(NSUB):
                        for off, bank in ((D, 0), (2 * D, 2)):
                            for n in range(2):
                                for dch in range(8):
                                    pe(lambda e, off=off, bank=bank, n=n, dch=dch, m=m, yf=yf: e.matmul(
                                        pb[bank + n][:, :],
                                        yf[:, dch, m * 128:(m + 1) * 128],
                                        wqkv_sb[:, dch, off + n * 512:off + (n + 1) * 512],
                                        start=(dch == 0), stop=(dch == 7)),
                                        inc=(1 if dch == 7 else 0))
                        kv_pe = PE.v
                        b.wait("scalar", PE, kv_pe)
                        for n in range(2):
                            act(lambda e, n=n: e.activation(k2_sb[:, n * 512:(n + 1) * 512], pb[n][:, :], AF.Relu))
                        k_ac = AC.v
                        b.wait("vector", PE, kv_pe)
                        for n in range(2):
                            vec(lambda e, n=n: e.tensor_copy(
                                v2ext[:, 16 * n:16 * (n + 1), 0:32],
                                pb[2 + n][:, :]))
                        v_dv = DV.v
                        for qch in range(8):
                            if qch == 0:
                                bank = 4
                            else:
                                bank = (qch - 1) % 2
                                b.wait("tensor", AC, k_ac)
                            if qch >= 2:
                                b.wait("tensor", AC, AC.v - 2)
                            for dch in range(8):
                                pe(lambda e, bank=bank, qch=qch, dch=dch, m=m, yf=yf: e.matmul(
                                    pb[bank][:, 0:128],
                                    wqkv_sb[:, dch, qch * 128:(qch + 1) * 128],
                                    yf[:, dch, m * 128:(m + 1) * 128],
                                    start=(dch == 0), stop=(dch == 7)),
                                    inc=(1 if dch == 7 else 0))
                            b.wait("scalar", PE)
                            act(lambda e, bank=bank, qch=qch, g=g, m=m: e.activation(
                                Q2hold[:, qch, (g * NSUB + m) * 128:(g * NSUB + m + 1) * 128],
                                pb[bank][:, 0:128], AF.Relu))
                        b.wait("tensor", AC)
                        b.wait("tensor", DV, v_dv)
                        firstmn = (g == 0 and m == 0)
                        lastmn = (g == GPS - 1 and m == NSUB - 1)
                        for h in range(HB):
                            first = firstmn and h in (0, 15, 30)
                            last = lastmn and h in (14, 29, 31)
                            pe(lambda e, h=h, first=first, last=last: e.matmul(
                                kvps[0:32, vk_col(h):vk_col(h) + 33],
                                k2_sb[:, 32 * h:32 * (h + 1)],
                                v2ext[:, h, :],
                                start=first, stop=last),
                                inc=(1 if (lastmn and h == HB - 1) else 0))
                    yfm_guard[g % 2] = PE.v
                if gi < GPS:
                    g = gi; t0 = (s * GPS + g) * G
                    b.wait("sync", DV, xgr_guard[0])
                    b.wait("sync", ST, xs_st3)
                    dma("sync", xgr, XS[t0:t0 + G, :].rearrange("(m p) c -> p m c", p=128), LD)
                    b.wait("vector", LD)
                    gcol = (s * GPS + g) * NSUB
                    group_ln(xgr, gcol, NSUB, stat_mu, stat_iv)
                    for m in range(NSUB):
                        st_ = t0 + m * 128
                        col = gcol + m
                        b.wait("vector", ST, ybf_guard[0])
                        vec(lambda e, m=m, col=col: e.tensor_scalar(
                            ybf3, xgr[:, m, :], stat_mu[:, col:col + 1], stat_iv[:, col:col + 1],
                            op0=ALU.subtract, op1=ALU.mult))
                        b.wait("gpsimd", DV)
                        dma("gpsimd", YB[st_:st_ + 128, :], ybf3, ST)
                        ybf_guard[0] = ST.v
                    xgr_guard[0] = DV.v
                    ydone_st[g] = ST.v
            b.wait("vector", PE)
            vec(lambda e: e.memset(vkblk[:, :, :], 0.0), inc=0)
            for h in range(HB):
                q_, r_ = h // 4, h % 4
                vec(lambda e, h=h, q_=q_, r_=r_: e.tensor_copy(
                    vkblk[32 * r_:32 * (r_ + 1), q_, 33 * r_:33 * r_ + 33],
                    kvps[0:32, vk_col(h):vk_col(h) + 33]), inc=(1 if h == HB - 1 else 0))
            kv_dv = DV.v

            odone_st = [0] * GPS
            evdv = [DV.v, DV.v]
            nbuf_ac = [AC.v]
            for gi in range(GPS + 2):
                if gi < GPS:
                    g = gi; t0 = (s * GPS + g) * G
                    b.wait("tensor", DV, kv_dv)
                    for m in range(NSUB):
                        st_ = t0 + m * 128
                        b.wait("vector", AC, nbuf_ac[0])
                        for q_ in range(8):
                            bank = q_ % 2
                            b.wait("tensor", DV, evdv[bank])
                            pe(lambda e, q_=q_, bank=bank, g=g, m=m: e.matmul(
                                pb[bank][:, 0:132],
                                Q2hold[:, q_, (g * NSUB + m) * 128:(g * NSUB + m + 1) * 128],
                                vkblk[:, q_, :],
                                start=True, stop=True))
                            b.wait("vector", PE)
                            vec(lambda e, bank=bank, q_=q_: e.tensor_copy(
                                numbuf[:, 132 * q_:132 * q_ + 132], pb[bank][:, 0:132]))
                            evdv[bank] = DV.v
                        nb3 = numbuf[:, 0:1056].rearrange("p (q c) -> p q c", c=132)
                        for j in range(4):
                            vec(lambda e, j=j, nb3=nb3: e.tensor_scalar(
                                den8[:, 8 * j:8 * j + 8], nb3[:, :, 33 * j + 32:33 * j + 33],
                                1e-15, None, op0=ALU.add), inc=(j == 3))
                        b.wait("vector", DV)
                        vtiny(lambda e: e.reciprocal(den8[:, 0:32], den8[:, 0:32]))
                        b.wait("scalar", DV)
                        b.wait("scalar", ST, osb_guard[0])
                        for q_ in range(8):
                            for j in range(4):
                                act(lambda e, q_=q_, j=j: e.activation(
                                    o2_sb[:, 128 * q_ + 32 * j:128 * q_ + 32 * (j + 1)],
                                    numbuf[:, 132 * q_ + 33 * j:132 * q_ + 33 * j + 32],
                                    AF.Copy, scale=den8[:, 8 * j + q_:8 * j + q_ + 1]),
                                    inc=(1 if (q_ == 7 and j == 3) else 0))
                        nbuf_ac[0] = AC.v
                        b.wait("gpsimd", AC)
                        dma("gpsimd", OB[st_:st_ + 128, :], o2_sb, ST)
                        osb_guard[0] = ST.v
                    odone_st[g] = ST.v
                if gi >= 2:
                    g = gi - 2; t0 = (s * GPS + g) * G
                    of = ofm3[g % 2]
                    b.wait("sync", ST, odone_st[g])
                    b.wait("sync", PE, ofm_guard[g % 2])
                    for dch in range(8):
                        dmaT("sync", of[:, dch, :], OB[t0:t0 + G, 128 * dch:128 * (dch + 1)], LT)
                    b.wait("sync", DV, xgx_guard[0])
                    b.wait("sync", ST, xs_st3)
                    dma("sync", xgx, XS[t0:t0 + G, :].rearrange("(m p) c -> p m c", p=128), LD)
                    b.wait("tensor", LT)
                    for m in range(NSUB):
                        st_ = t0 + m * 128
                        fcdv = DV.v
                        b.wait("tensor", DV, fcdv)
                        for n in range(2):
                            for dch in range(8):
                                pe(lambda e, n=n, dch=dch, m=m, of=of: e.matmul(
                                    pb[2 + n][:, :],
                                    of[:, dch, m * 128:(m + 1) * 128],
                                    wproj_sb[:, dch, n * 512:(n + 1) * 512],
                                    start=(dch == 0), stop=(dch == 7)),
                                    inc=(1 if dch == 7 else 0))
                        b.wait("vector", LD)
                        b.wait("vector", PE)
                        b.wait("vector", ST, xbt_guard[0])
                        for n in range(2):
                            vec(lambda e, n=n, m=m: e.tensor_add(
                                xbt[:, n * 512:(n + 1) * 512], pb[2 + n][:, :],
                                bproj[:, n * 512:(n + 1) * 512]), inc=0)
                        vec(lambda e, m=m: e.tensor_add(xbt, xbt, xgx[:, m, :]))
                        b.wait("gpsimd", DV)
                        dma("gpsimd", XS[st_:st_ + 128, :], xbt, ST)
                        xbt_guard[0] = ST.v
                    xgx_guard[0] = DV.v
                    ofm_guard[g % 2] = PE.v
            xgr_guard[0] = DV.v

    if n_sublayers >= 4:
        emit_ffn(4, XS, out, dbg=None)

    if n_sublayers < 4 or dbg is not None:
        b.wait("gpsimd", ST)
        b.wait("gpsimd", DV)
        b.wait("gpsimd", AC)
        if dbg == "PER":
            for i in range(3):
                dma("gpsimd", out[128 * i:128 * (i + 1), :], per_t[:, 1024 * i:1024 * (i + 1)], ST)
            dma("gpsimd", out[384:512, 0:3840 - 3072], per_t[:, 3072:3840], ST)
            b.wait("gpsimd", ST)
            srcT = None
        else:
            srcT = {None: XS, "XS": XS, "YB": YB, "OB": OB, "HQ": OB, "H2": H2, "W2Q": H2, "WQ1": WQ1, "WQ2": WQ2}[dbg]
        nrow = srcT.tensor.shape[0] if srcT is not None else 0
        for i in range(nrow // 128):
            dma("gpsimd", wld[i % 2][:, 0:D], srcT[128 * i:128 * (i + 1), 0:D], LD)
            b.wait("gpsimd", LD)
            dma("gpsimd", out[128 * i:128 * (i + 1), :], wld[i % 2][:, 0:D], ST)
        b.wait("gpsimd", ST)
    else:
        b.wait("gpsimd", ST)

    with nc.Block() as blk:
        for eng in B.ENGS:
            ops = b.q[eng]
            getattr(blk, eng)(lambda e, ops=ops: [f(e) for f in ops])
    es.close()
    return nc


def kernel(**inputs):
    per_core = host_prep(inputs)
    nc = build_program(n_sublayers=4)
    res = run_bass_kernel_spmd(nc, per_core, core_ids=list(range(NCORES)))
    return np.concatenate([r["out"] for r in res.results], axis=0).reshape(16, 2048, 1024)


# version 11
# speedup vs baseline: 1.2376x; 1.0984x over previous
import numpy as np
import ml_dtypes
import concourse.bass as bass
import concourse.mybir as mybir
from concourse.bass_utils import run_bass_kernel_spmd
from contextlib import ExitStack

F32 = mybir.dt.float32
BF16 = mybir.dt.bfloat16
I32 = mybir.dt.int32
AF = mybir.ActivationFunctionType
ALU = mybir.AluOpType
AX = mybir.AxisListType
bf16 = ml_dtypes.bfloat16

NCORES = 8
SEQ = 2048
NSEQ = 2
T = SEQ * NSEQ
D = 1024
DH = 4096
G = 512
NG = T // G
GPS = SEQ // G
NSUB = 4
HA, DKA = 16, 64
LN_EPS = 1e-5

class Ctr:
    def __init__(self, handle):
        self.h = handle; self.v = 0

class B:
    ENGS = ("sync", "gpsimd", "scalar", "vector", "tensor")
    def __init__(self):
        self.q = {e: [] for e in self.ENGS}
    def wait(self, eng, ctr, val=None):
        v = ctr.v if val is None else val
        if v <= 0: return
        h = ctr.h
        self.q[eng].append(lambda e: e.wait_ge(h, v))
    def _inc(self, eng, fn, ctr, amt):
        if ctr is None:
            self.q[eng].append(fn)
        else:
            h = ctr.h
            self.q[eng].append(lambda e: fn(e).then_inc(h, amt))
            ctr.v += amt

class Arena:
    def __init__(self, tens, width):
        self.t = tens; self.width = width; self.off = 0
    def alloc(self, n):
        assert self.off + n <= self.width, f"arena overflow {self.off}+{n}>{self.width}"
        v = self.t[:, self.off:self.off + n]
        self.off += n
        return v

def host_prep(inputs):
    x = np.asarray(inputs["L0_attn_a_inputs"])
    d = {}
    d["wq"] = np.ascontiguousarray(inputs["L0_attn_a_W_Q"])
    d["wk"] = np.ascontiguousarray(inputs["L0_attn_a_W_K"])
    d["wv"] = np.ascontiguousarray(inputs["L0_attn_a_W_V"])
    d["wfc"] = np.ascontiguousarray(inputs["L0_attn_a_W_fc"])
    d["gamma_bc"] = np.ascontiguousarray(np.broadcast_to(np.asarray(inputs["L0_attn_a_gamma"])[None, :], (128, D)))
    d["beta_bc"] = np.ascontiguousarray(np.broadcast_to(np.asarray(inputs["L0_attn_a_beta"])[None, :], (128, D)))
    d["wqkvT"] = np.ascontiguousarray(np.asarray(inputs["L0_attn_b_W_qkv"]).T)
    d["wprojT"] = np.ascontiguousarray(np.asarray(inputs["L0_attn_b_W_proj"]).T)
    d["bproj_bc"] = np.ascontiguousarray(np.broadcast_to(np.asarray(inputs["L0_attn_b_b_proj"])[None, :], (128, D)))
    d["w1T"] = np.ascontiguousarray(np.asarray(inputs["L0_ffn_w1"]).T)
    d["w2T"] = np.ascontiguousarray(np.asarray(inputs["L0_ffn_w2"]).T)
    cwa = np.asarray(inputs["L0_ffn_conv_w"])
    for j in range(3):
        d[f"cw{j}"] = np.ascontiguousarray(cwa[:, 0, j].reshape(32, 128).T)
    d["cb"] = np.ascontiguousarray(np.asarray(inputs["L0_ffn_conv_b"]).reshape(32, 128).T)
    d["ident"] = np.eye(128, dtype=np.float32)
    per_core = []
    for c in range(NCORES):
        m = dict(d)
        m["xin"] = np.ascontiguousarray(x[2 * c:2 * c + 2].reshape(T, D))
        per_core.append(m)
    return per_core


def build_program(n_sublayers=4, dbg=None):
    nc = bass.Bass("TRN2", target_bir_lowering=False)
    es = ExitStack()
    ap = {}
    def din(name, shape, dt=F32):
        ap[name] = nc.dram_tensor(name, list(shape), dt, kind="ExternalInput").ap()
    din("xin", (T, D))
    for w in ("wq", "wk", "wv", "wfc"):
        din(w, (D, D))
    din("gamma_bc", (128, D)); din("beta_bc", (128, D))
    din("wqkvT", (D, 3 * D)); din("wprojT", (D, D)); din("bproj_bc", (128, D))
    din("w1T", (D, DH)); din("w2T", (DH, D))
    for j in range(3):
        din(f"cw{j}", (128, 32))
    din("cb", (128, 32))
    din("ident", (128, 128))
    out = nc.dram_tensor("out", [T, D], F32, kind="ExternalOutput").ap()
    XS = nc.dram_tensor("XS", [T, D], F32, kind="Internal").ap()
    YB = nc.dram_tensor("YB", [T, D], BF16, kind="Internal").ap()
    OB = nc.dram_tensor("OB", [T, D], BF16, kind="Internal").ap()
    H2 = nc.dram_tensor("H2", [DH, T], F32, kind="Internal").ap()
    WQ1 = nc.dram_tensor("WQ1", [D, DH], BF16, kind="Internal").ap()
    WQ2 = nc.dram_tensor("WQ2", [DH, D], BF16, kind="Internal").ap()

    bfa_t = es.enter_context(nc.sbuf_tensor("bfa", [128, 38400], BF16))
    wta_t = es.enter_context(nc.sbuf_tensor("wta", [128, 32768], BF16))
    f32a_t = es.enter_context(nc.sbuf_tensor("f32a", [128, 10240], F32))
    per_t = es.enter_context(nc.sbuf_tensor("per", [128, 3904], F32))
    i32_t = es.enter_context(nc.sbuf_tensor("i32b", [128, 1024], I32))
    f32b_t = es.enter_context(nc.sbuf_tensor("f32b", [128, 1152], F32))
    gcv_t = es.enter_context(nc.sbuf_tensor("gcv", [128, 1032], F32))
    per = Arena(per_t, 3904)
    numbuf = f32b_t[:, 0:1088]

    pb = [es.enter_context(nc.psum_tensor(f"pb{i}", [128, 512], F32)) for i in range(5)]
    kvps = es.enter_context(nc.psum_tensor("kvps", [128, 1536], F32))

    ident = per.alloc(128)
    gamma = per.alloc(D); beta = per.alloc(D); bproj = per.alloc(D)
    cw = [per.alloc(32) for _ in range(3)]
    cbv = per.alloc(32)
    onesf = per.alloc(128)
    invws1 = per.alloc(1); invws2 = per.alloc(1)
    stat_mu = per.alloc(32); stat_iv = per.alloc(32)
    stat_mu2 = per.alloc(4); stat_iv2 = per.alloc(4)
    aq = per.alloc(32)
    rabq = per.alloc(32)
    Aq = per.alloc(32)
    rAq = per.alloc(32)
    scr = per.alloc(64)
    den8 = per.alloc(32)
    tiny = per.alloc(32)

    b = B()
    sems = {k: Ctr(es.enter_context(nc.semaphore(k))) for k in ("LD", "LW", "LT", "ST", "PE", "DV", "AC", "GP")}
    LD, LW, LT, ST, PE, DV, AC, GP = (sems[k] for k in ("LD", "LW", "LT", "ST", "PE", "DV", "AC", "GP"))

    def dma(qeng, outv, inv, ctr):
        b._inc(qeng, lambda e, o=outv, i=inv: e.dma_start(out=o, in_=i), ctr, 16)
    def dmaT(qeng, outv, inv, ctr):
        b._inc(qeng, lambda e, o=outv, i=inv: e.dma_start_transpose(o, i), ctr, 16)
    def vec(fn, inc=1):
        b._inc("vector", fn, DV if inc else None, 1)
    def act(fn, inc=1):
        b._inc("scalar", fn, AC if inc else None, 1)
    def gp(fn, inc=1):
        b._inc("gpsimd", fn, GP if inc else None, 1)
    def pe(fn, inc=1):
        b._inc("tensor", fn, PE if inc else None, 1)

    dma("sync", ident[:], ap["ident"][:, :], LD)
    dma("sync", gamma[:], ap["gamma_bc"][:, :], LD)
    dma("sync", beta[:], ap["beta_bc"][:, :], LD)
    dma("sync", bproj[:], ap["bproj_bc"][:, :], LD)
    for j in range(3):
        dma("sync", cw[j][:], ap[f"cw{j}"][:, :], LD)
    dma("sync", cbv[:], ap["cb"][:, :], LD)
    b.wait("vector", LD)
    vec(lambda e: e.memset(onesf[:], 1.0), inc=0)
    vec(lambda e: e.memset(scr[:], 0.0))

    def vtiny(fn):
        vec(fn, inc=1)
        for _ in range(16):
            b.q["vector"].append(lambda e: e.engine_nop())
    def atiny(fn):
        act(fn, inc=1)
        b.wait("scalar", AC)

    f32a = Arena(f32a_t, 10240)
    wld = [f32a.alloc(4096), f32a.alloc(4096)]
    wbf_a = Arena(bfa_t, 38400)
    wbf = wbf_a.alloc(4096)

    def quant_weight(wapT, nchunk, wqap, invws_col):
        cols = DH if nchunk == 8 else D
        guard = [DV.v, DV.v]
        accv = tiny[:, 0:nchunk] if nchunk <= 16 else None
        acc32 = per
        accview = tiny[:, 0:8] if nchunk == 8 else scr[:, 0:32]
        for c in range(nchunk):
            buf = wld[c % 2][:, :cols]
            b.wait("sync", DV, guard[c % 2])
            dma("sync", buf, wapT[128 * c:128 * (c + 1), :], LD)
            b.wait("vector", LD)
            vec(lambda e, buf=buf, c=c: e.reduce_sum(accview[:, c:c + 1], buf, axis=AX.X, apply_absolute_value=True))
            guard[c % 2] = DV.v
        b.wait("vector", DV)
        vec(lambda e: e.reduce_sum(scr[:, 40:41], accview, axis=AX.X))
        b.wait("vector", DV)
        b.wait("tensor", DV)
        pe(lambda e: e.matmul(pb[0][0:1, 0:1], onesf[:, 0:1], scr[:, 40:41], start=True, stop=True))
        nelem = float(128 * nchunk * cols)
        b.wait("vector", PE)
        vtiny(lambda e: e.tensor_scalar(scr[0:1, 41:42], pb[0][0:1, 0:1], 1.0 / nelem, 1e-5, op0=ALU.mult, op1=ALU.max))
        b.wait("tensor", DV)
        pe(lambda e: e.matmul(pb[0][:, 4:5], onesf[0:1, :], scr[0:1, 41:42], start=True, stop=True))
        b.wait("vector", PE)
        vtiny(lambda e: e.tensor_copy(invws_col[:, 0:1], pb[0][:, 4:5]))
        vtiny(lambda e: e.reciprocal(scr[:, 42:43], invws_col[:, 0:1]))
        wsv = scr[:, 42:43]
        st_guard = ST.v
        for c in range(nchunk):
            buf = wld[c % 2][:, :cols]
            b.wait("sync", DV, guard[c % 2])
            dma("sync", buf, wapT[128 * c:128 * (c + 1), :], LD)
            b.wait("vector", LD)
            b.wait("vector", ST, st_guard)
            for h0 in range(0, cols, 1024):
                hw = min(1024, cols - h0)
                vec(lambda e, buf=buf, h0=h0, hw=hw: e.tensor_scalar(i32_t[:, :hw], buf[:, h0:h0 + hw], wsv, None, op0=ALU.mult), inc=0)
                vec(lambda e, h0=h0, hw=hw: e.tensor_scalar(wbf[:, h0:h0 + hw], i32_t[:, :hw], -1.0, 1.0, op0=ALU.max, op1=ALU.min), inc=(1 if h0 + hw >= cols else 0))
            guard[c % 2] = DV.v
            b.wait("gpsimd", DV)
            dma("gpsimd", wqap[128 * c:128 * (c + 1), :], wbf[:, :cols], ST)
            st_guard = ST.v
    quant_weight(ap["w1T"], 8, WQ1, invws1)
    quant_weight(ap["w2T"], 32, WQ2, invws2)

    wta = Arena(wta_t, 32768)
    w_at = {}
    for w in ("wk", "wv", "wq", "wfc"):
        w_at[w] = wta.alloc(8 * D).rearrange("p (c n) -> p c n", c=8)
        for dch in range(8):
            dma("gpsimd", w_at[w][:, dch, :], ap[w][128 * dch:128 * (dch + 1), :], LW)
    p1_lw = LW.v

    bfa = Arena(bfa_t, 38400)
    yfm = [bfa.alloc(8 * G).rearrange("p (c n) -> p c n", c=8) for _ in range(2)]
    ofm = [bfa.alloc(8 * G).rearrange("p (c n) -> p c n", c=8) for _ in range(2)]
    Khold = bfa.alloc(D)
    Vext = bfa.alloc(HA * 65).rearrange("p (h c) -> p h c", h=HA)
    Qhold = bfa.alloc(8 * SEQ).rearrange("p (c n) -> p c n", c=8)
    KVblk = bfa.alloc(8 * 130).rearrange("p (q c) -> p q c", q=8)
    o_sb = bfa.alloc(D)
    ybf = bfa.alloc(D)
    xgr = f32a_t[:, 0:4096].rearrange("p (m c) -> p m c", m=4)
    xgx = f32a_t[:, 4096:8192].rearrange("p (m c) -> p m c", m=4)
    junk = f32a_t[:, 8192:9216]
    xbt = f32a_t[:, 9216:10240]
    st2 = per_t[:, 0:0]

    def group_ln(xg, base_col, n=NSUB, mu_arr=None, iv_arr=None):
        mu = mu_arr[:, base_col:base_col + n]
        iv = iv_arr[:, base_col:base_col + n]
        s_ = tiny[:, 0:n]; q_ = tiny[:, 4:4 + n]; m2 = tiny[:, 8:8 + n]
        for m in range(n):
            vec(lambda e, m=m: e.reduce_sum(s_[:, m:m + 1], xg[:, m, :], axis=AX.X), inc=(m == n - 1))
        b.wait("scalar", DV)
        for m in range(n):
            act(lambda e, m=m: e.activation(junk, xg[:, m, :], AF.Square, accum_out=q_[:, m:m + 1]), inc=(m == n - 1))
        b.wait("vector", AC)
        vtiny(lambda e: e.tensor_scalar(mu, s_, 1.0 / D, None, op0=ALU.mult))
        vtiny(lambda e: e.tensor_mul(m2, mu, mu))
        vtiny(lambda e: e.scalar_tensor_tensor(iv, q_, 1.0 / D, m2, op0=ALU.mult, op1=ALU.subtract))
        vtiny(lambda e: e.tensor_scalar(iv, iv, LN_EPS, None, op0=ALU.add))
        vtiny(lambda e: e.reciprocal(iv, iv))
        b.wait("scalar", DV)
        atiny(lambda e: e.activation(iv, iv, AF.Sqrt))
        b.wait("vector", AC)

    def kvcol(h):
        return (h // 7) * 512 + (h % 7) * 65 if h < 14 else 1024 + (h - 14) * 65

    if n_sublayers >= 1:
        xgr_guard = [DV.v]
        xgx_guard = [DV.v]
        yfm_guard = [0, 0]
        ofm_guard = [0, 0]
        ybf_guard = [0]
        osb_guard = [0]
        xbt_guard = [0]
        for s in range(NSEQ):
            ydone_st = [0] * GPS
            for gi in range(GPS + 2):
                if gi >= 2:
                    g = gi - 2; t0 = (s * GPS + g) * G
                    yf = yfm[g % 2]
                    b.wait("sync", ST, ydone_st[g])
                    b.wait("sync", PE, yfm_guard[g % 2])
                    for dch in range(8):
                        dmaT("sync", yf[:, dch, :], YB[t0:t0 + G, 128 * dch:128 * (dch + 1)], LT)
                    b.wait("tensor", LT)
                    b.wait("tensor", LW, p1_lw)
                    if s == 0 and g == 0:
                        b.wait("vector", LD, 0)
                        vec(lambda e: e.memset(Vext[:, :, 64:65], 1.0))
                        b.wait("tensor", DV)
                    for m in range(NSUB):
                        for wname, bank in (("wk", 0), ("wv", 2)):
                            for n in range(2):
                                for dch in range(8):
                                    pe(lambda e, wname=wname, bank=bank, n=n, dch=dch, m=m, yf=yf: e.matmul(
                                        pb[bank + n][:, :],
                                        yf[:, dch, m * 128:(m + 1) * 128],
                                        w_at[wname][:, dch, n * 512:(n + 1) * 512],
                                        start=(dch == 0), stop=(dch == 7)),
                                        inc=(1 if dch == 7 else 0))
                        kv_pe = PE.v
                        b.wait("scalar", PE, kv_pe)
                        for n in range(2):
                            act(lambda e, n=n: e.activation(Khold[:, n * 512:(n + 1) * 512], pb[n][:, :], AF.Sigmoid))
                        k_ac = AC.v
                        b.wait("vector", PE, kv_pe)
                        for n in range(2):
                            vec(lambda e, n=n: e.tensor_copy(
                                Vext[:, 8 * n:8 * (n + 1), 0:64],
                                pb[2 + n][:, :]))
                        v_dv = DV.v
                        for qch in range(8):
                            if qch == 0:
                                bank = 4
                            else:
                                bank = (qch - 1) % 2
                                b.wait("tensor", AC, k_ac)
                            if qch >= 2:
                                b.wait("tensor", AC, AC.v - 2)
                            for dch in range(8):
                                pe(lambda e, bank=bank, qch=qch, dch=dch, m=m, yf=yf: e.matmul(
                                    pb[bank][:, 0:128],
                                    w_at["wq"][:, dch, qch * 128:(qch + 1) * 128],
                                    yf[:, dch, m * 128:(m + 1) * 128],
                                    start=(dch == 0), stop=(dch == 7)),
                                    inc=(1 if dch == 7 else 0))
                            b.wait("scalar", PE)
                            act(lambda e, bank=bank, qch=qch, g=g, m=m: e.activation(
                                Qhold[:, qch, (g * NSUB + m) * 128:(g * NSUB + m + 1) * 128],
                                pb[bank][:, 0:128], AF.Sigmoid))
                        b.wait("tensor", AC)
                        b.wait("tensor", DV, v_dv)
                        firstmn = (g == 0 and m == 0)
                        lastmn = (g == GPS - 1 and m == NSUB - 1)
                        for h in range(HA):
                            first = firstmn and h in (0, 7, 14)
                            last = lastmn and h in (6, 13, 15)
                            pe(lambda e, h=h, first=first, last=last: e.matmul(
                                kvps[0:64, kvcol(h):kvcol(h) + 65],
                                Khold[:, 64 * h:64 * (h + 1)],
                                Vext[:, h, :],
                                start=first, stop=last),
                                inc=(1 if (lastmn and h == HA - 1) else 0))
                    yfm_guard[g % 2] = PE.v
                if gi < GPS:
                    g = gi; t0 = (s * GPS + g) * G
                    b.wait("sync", DV, xgr_guard[0])
                    dma("sync", xgr, ap["xin"][t0:t0 + G, :].rearrange("(m p) c -> p m c", p=128), LD)
                    b.wait("vector", LD)
                    gcol = (s * GPS + g) * NSUB
                    group_ln(xgr, gcol, NSUB, stat_mu, stat_iv)
                    for m in range(NSUB):
                        st_ = t0 + m * 128
                        col = gcol + m
                        b.wait("vector", ST, ybf_guard[0])
                        vec(lambda e, m=m, col=col: e.tensor_scalar(
                            ybf, xgr[:, m, :], stat_mu[:, col:col + 1], stat_iv[:, col:col + 1],
                            op0=ALU.subtract, op1=ALU.mult))
                        b.wait("gpsimd", DV)
                        dma("gpsimd", YB[st_:st_ + 128, :], ybf, ST)
                        ybf_guard[0] = ST.v
                    xgr_guard[0] = DV.v
                    ydone_st[g] = ST.v
            b.wait("vector", PE)
            vec(lambda e: e.memset(KVblk[:, :, :], 0.0), inc=0)
            for h in range(HA):
                q_, r_ = h // 2, h % 2
                vec(lambda e, h=h, q_=q_, r_=r_: e.tensor_copy(
                    KVblk[64 * r_:64 * (r_ + 1), q_, 65 * r_:65 * r_ + 65],
                    kvps[0:64, kvcol(h):kvcol(h) + 65]), inc=(1 if h == HA - 1 else 0))
            kv_dv = DV.v

            odone_st = [0] * GPS
            evdv = [DV.v, DV.v]
            nbuf_ac = [AC.v]
            for gi in range(GPS + 2):
                if gi < GPS:
                    g = gi; t0 = (s * GPS + g) * G
                    b.wait("tensor", DV, kv_dv)
                    for m in range(NSUB):
                        st_ = t0 + m * 128
                        b.wait("vector", AC, nbuf_ac[0])
                        for p_ in range(8):
                            bank = p_ % 2
                            b.wait("tensor", DV, evdv[bank])
                            pe(lambda e, p_=p_, bank=bank, g=g, m=m: e.matmul(
                                pb[bank][:, 0:130],
                                Qhold[:, p_, (g * NSUB + m) * 128:(g * NSUB + m + 1) * 128],
                                KVblk[:, p_, :],
                                start=True, stop=True))
                            b.wait("vector", PE)
                            vec(lambda e, bank=bank, p_=p_: e.tensor_copy(
                                numbuf[:, 130 * p_:130 * p_ + 130], pb[bank][:, 0:130]))
                            evdv[bank] = DV.v
                        nb3 = numbuf[:, 0:1040].rearrange("p (q c) -> p q c", c=130)
                        for j in range(2):
                            vec(lambda e, j=j, nb3=nb3: e.tensor_scalar(
                                den8[:, 8 * j:8 * j + 8], nb3[:, :, 64 + 65 * j:65 + 65 * j],
                                1e-8, None, op0=ALU.add), inc=(j == 1))
                        b.wait("vector", DV)
                        vtiny(lambda e: e.reciprocal(den8[:, 0:16], den8[:, 0:16]))
                        b.wait("scalar", DV)
                        b.wait("scalar", ST, osb_guard[0])
                        for p_ in range(8):
                            for j in range(2):
                                act(lambda e, p_=p_, j=j: e.activation(
                                    o_sb[:, 128 * p_ + 64 * j:128 * p_ + 64 * (j + 1)],
                                    numbuf[:, 130 * p_ + 65 * j:130 * p_ + 65 * j + 64],
                                    AF.Copy, scale=den8[:, 8 * j + p_:8 * j + p_ + 1]),
                                    inc=(1 if (p_ == 7 and j == 1) else 0))
                        nbuf_ac[0] = AC.v
                        b.wait("gpsimd", AC)
                        dma("gpsimd", OB[st_:st_ + 128, :], o_sb, ST)
                        osb_guard[0] = ST.v
                    odone_st[g] = ST.v
                if gi >= 2:
                    g = gi - 2; t0 = (s * GPS + g) * G
                    of = ofm[g % 2]
                    b.wait("sync", ST, odone_st[g])
                    b.wait("sync", PE, ofm_guard[g % 2])
                    for dch in range(8):
                        dmaT("sync", of[:, dch, :], OB[t0:t0 + G, 128 * dch:128 * (dch + 1)], LT)
                    b.wait("sync", DV, xgx_guard[0])
                    dma("sync", xgx, ap["xin"][t0:t0 + G, :].rearrange("(m p) c -> p m c", p=128), LD)
                    b.wait("tensor", LT)
                    gcol = (s * GPS + g) * NSUB
                    for m in range(NSUB):
                        col = gcol + m
                        b.wait("vector", LD)
                        b.wait("vector", ST, xbt_guard[0])
                        vec(lambda e, m=m, col=col: e.tensor_scalar(
                            xbt, xgx[:, m, :], stat_mu[:, col:col + 1], stat_iv[:, col:col + 1],
                            op0=ALU.subtract, op1=ALU.mult))
                        fcdv = DV.v
                        b.wait("tensor", DV, fcdv)
                        for n in range(2):
                            for dch in range(8):
                                pe(lambda e, n=n, dch=dch, m=m, of=of: e.matmul(
                                    pb[2 + n][:, :],
                                    of[:, dch, m * 128:(m + 1) * 128],
                                    w_at["wfc"][:, dch, n * 512:(n + 1) * 512],
                                    start=(dch == 0), stop=(dch == 7)),
                                    inc=(1 if dch == 7 else 0))
                        b.wait("vector", PE)
                        for n in range(2):
                            vec(lambda e, n=n, m=m: e.tensor_add(
                                xgr[:, m, n * 512:(n + 1) * 512], pb[2 + n][:, :],
                                xbt[:, n * 512:(n + 1) * 512]), inc=(1 if n == 1 else 0))
                        xbt_guard[0] = 0
                    group_ln(xgr, 0, NSUB, stat_mu2, stat_iv2)
                    for m in range(NSUB):
                        st_ = t0 + m * 128
                        vec(lambda e, m=m: e.tensor_scalar(
                            junk, xgr[:, m, :], stat_mu2[:, m:m + 1], stat_iv2[:, m:m + 1],
                            op0=ALU.subtract, op1=ALU.mult), inc=0)
                        vec(lambda e: e.tensor_mul(junk, junk, gamma[:, :]), inc=0)
                        b.wait("vector", ST, xbt_guard[0])
                        vec(lambda e, m=m: e.tensor_add(xbt, xgx[:, m, :], beta[:, :]), inc=0)
                        vec(lambda e: e.tensor_add(xbt, xbt, junk))
                        b.wait("gpsimd", DV)
                        dma("gpsimd", XS[st_:st_ + 128, :], xbt, ST)
                        xbt_guard[0] = ST.v
                    xgx_guard[0] = DV.v
                    xgr_guard[0] = DV.v
                    ofm_guard[g % 2] = PE.v

    f_xg   = f32a_t[:, 0:4096].rearrange("p (m c) -> p m c", m=4)
    f_xg2  = f32a_t[:, 4096:8192].rearrange("p (m c) -> p m c", m=4)
    f_hs   = [f32a_t[:, 4096:4612], f32a_t[:, 4612:5128]]
    f_cva  = f32a_t[:, 5128:5644]
    f_h2s  = [f32a_t[:, 5644:6157], f32a_t[:, 6157:6670]]
    f_m1x  = [f32a_t[:, 6670:7183], f32a_t[:, 7183:7696]]
    f_m1n  = [f32a_t[:, 7696:8192][:, 0:496], None]
    f_junk = f32a_t[:, 8192:9216]
    f_xbt  = f32a_t[:, 9216:10240]
    del f_hs, f_cva, f_h2s, f_m1x, f_m1n
    f_hs  = [f32a_t[:, 4096:4612], f32a_t[:, 4612:5128]]
    f_cva = f32a_t[:, 5128:5644]
    f_cva2 = [f32a_t[:, 5128:5644], f32b_t[:, 0:516]]
    f_h2s = f32a_t[:, 5644:6157]
    f_h2s2 = [f32a_t[:, 5644:6157], f32b_t[:, 516:1029]]
    f_m1x = f32a_t[:, 6157:6670]
    f_m1n = f32a_t[:, 6670:7183]
    f_k3x = f32a_t[:, 7183:7311]
    f_k3n = f32a_t[:, 7311:7439]
    f_bc  = f32a_t[:, 7439:7951]
    f_dg  = f32a_t[:, 5128:5640]
    f_h2i = [f32a_t[:, 8192:8704], f32a_t[:, 8704:9216]]
    f_mq  = f32a_t[:, 9216:9728]
    f_bcB = f32a_t[:, 9728:10240]

    halo = per.alloc(64)

    def emit_ffn(phase, src_XS, dst_XS, dbg=None):
        b.wait("gpsimd", PE)
        b.wait("gpsimd", ST)
        w1q = wta_t[:, 0:32768].rearrange("p (c n) -> p c n", c=8)
        for dch in range(8):
            dma("gpsimd", w1q[:, dch, :], WQ1[128 * dch:128 * (dch + 1), :], LW)
        lw_a = LW.v
        xs_st = ST.v
        h1bank_guard = [DV.v] * 6
        xg_guard = [DV.v]
        xg_guard_ac = [AC.v]
        yfm_guard2 = [PE.v, PE.v]
        ybf_guard2 = [ST.v, ST.v]
        ybf2 = bfa_t[:, 17408:18432]
        prevk3 = [None]
        h2st_done = [0]
        h2s_guard = [ST.v, ST.v]
        xbt_st_guard = [ST.v]

        def finish_amax_batch(col0, srcs):
            n = len(srcs)
            b.wait("tensor", DV)
            b.wait("tensor", GP)
            for i, (sx, sn) in enumerate(srcs):
                pe(lambda e, sx=sx, i=i: e.transpose(pb[3][:, 0:128] if False else pb[3][:, 0:128], sx, ident), inc=0) if False else None
            waves = [srcs[i:i + 2] for i in range(0, n, 2)]
            base = 0
            for w_i, wave in enumerate(waves):
                for i, (sx, sn) in enumerate(wave):
                    pe(lambda e, sx=sx, o=(2 * i) * 128: e.transpose(pb[3][:, o:o + 128], sx, ident), inc=0)
                    pe(lambda e, sn=sn, o=(2 * i + 1) * 128: e.transpose(pb[3][:, o:o + 128], sn, ident),
                       inc=(1 if i == len(wave) - 1 else 0))
                b.wait("vector", PE)
                for i in range(len(wave)):
                    j = w_i * 2 + i
                    vec(lambda e, i=i, j=j: e.reduce_max(tiny[:, 16 + j:17 + j], pb[3][:, (2 * i) * 128:(2 * i) * 128 + 128], axis=AX.X), inc=0)
                    vec(lambda e, i=i, j=j: e.tensor_reduce(tiny[:, 21 + j:22 + j], pb[3][:, (2 * i + 1) * 128:(2 * i + 1) * 128 + 128], axis=AX.X, op=ALU.min),
                        inc=(1 if i == len(wave) - 1 else 0))
                if w_i + 1 < len(waves):
                    b.wait("tensor", DV)
            b.wait("vector", DV)
            xs_ = tiny[:, 16:16 + n]; ns_ = tiny[:, 21:21 + n]
            vtiny(lambda e, xs_=xs_, ns_=ns_: e.scalar_tensor_tensor(xs_, ns_, -1.0, xs_, op0=ALU.mult, op1=ALU.max))
            vtiny(lambda e, xs_=xs_: e.tensor_scalar(xs_, xs_, 1e-5, None, op0=ALU.max))
            vtiny(lambda e, col0=col0, n=n, xs_=xs_: e.tensor_scalar(rAq[:, col0:col0 + n], xs_, invws2[:, 0:1], 1.0 / 127.0, op0=ALU.mult, op1=ALU.mult))
            vtiny(lambda e, ns_=ns_, xs_=xs_: e.reciprocal(ns_, xs_))
            vtiny(lambda e, col0=col0, n=n, ns_=ns_: e.tensor_scalar(Aq[:, col0:col0 + n], ns_, 127.0, None, op0=ALU.mult))

        for gi in range(NG + 2):
            if gi >= 2:
                g = gi - 2; t0 = g * G
                gcol = g * NSUB
                seq_start = (g % GPS == 0)
                seq_end = (g % GPS == GPS - 1)
                w = 511 if seq_start else (513 if seq_end else 512)
                wstart_off = 0 if seq_start else 1
                yf = yfm[g % 2]
                b.wait("sync", ST)
                b.wait("sync", PE, yfm_guard2[g % 2])
                for dch in range(8):
                    dmaT("sync", yf[:, dch, :], YB[t0:t0 + G, 128 * dch:128 * (dch + 1)], LT)
                b.wait("vector", PE)
                b.wait("vector", AC)
                for m in range(NSUB):
                    vec(lambda e, m=m, gcol=gcol: e.tensor_scalar(
                        f_dg[:, 128 * m:128 * (m + 1)], ident[:, :], rabq[:, gcol + m:gcol + m + 1], None,
                        op0=ALU.mult), inc=(m == NSUB - 1))
                b.wait("tensor", DV)
                b.wait("tensor", LT)
                b.wait("tensor", LW, lw_a)
                for m in range(NSUB):
                    pe(lambda e, m=m: e.matmul(pb[2][:, 128 * m:128 * (m + 1)], onesf[:, :], f_dg[:, 128 * m:128 * (m + 1)], start=True, stop=True), inc=(m == NSUB - 1))
                b.wait("vector", PE)
                vec(lambda e: e.tensor_copy(f_bc, pb[2][:, :]))
                if seq_start:
                    vec(lambda e: e.memset(halo[:, :], 0.0))
                b.wait("gpsimd", PE)
                b.wait("gpsimd", DV)
                gp(lambda e: e.memset(f_m1x, -3.0e38), inc=0)
                gp(lambda e: e.memset(f_m1n, 3.0e38))
                b.wait("vector", GP)
                silu_ac = [AC.v, AC.v]
                m1_dv = [DV.v, DV.v]
                h1banks = [pb[0][:, :], pb[1][:, :], pb[4][:, :],
                           kvps[:, 0:512], kvps[:, 512:1024], kvps[:, 1024:1536]]
                m1_pend = []
                halo_pend = [None]
                gp_done = {}
                gsilu_ac = [AC.v]
                for c in range(32):
                    is_gp = False
                    hbk = h1banks[c % 6]
                    hsb = f_hs[c % 2]
                    b.wait("tensor", DV, h1bank_guard[c % 6])
                    for dch in range(8):
                        pe(lambda e, hbk=hbk, c=c, dch=dch, yf=yf: e.matmul(
                            hbk,
                            w1q[:, dch, 128 * c:128 * (c + 1)],
                            yf[:, dch, :],
                            start=(dch == 0), stop=(dch == 7)),
                            inc=(1 if dch == 7 else 0))
                    if len(m1_pend) >= 2:
                        m1_pend.pop(0)()
                    if halo_pend[0] is not None:
                        halo_pend[0](); halo_pend[0] = None
                    if (c - 2) in gp_done:
                        b.wait("vector", GP, gp_done.pop(c - 2))
                    vec(lambda e, hsb=hsb, c=c: e.tensor_copy(hsb[:, 0:2], halo[:, 2 * c:2 * c + 2]), inc=0)
                    if seq_end:
                        vec(lambda e, hsb=hsb: e.memset(hsb[:, 514:516], 0.0), inc=0)
                    b.wait("vector", PE)
                    vec(lambda e, hsb=hsb, hbk=hbk: e.tensor_mul(hsb[:, 2:514], hbk, f_bc))
                    h1bank_guard[c % 6] = DV.v
                    halo_pend[0] = (lambda hsb=hsb, c=c: vec(
                        lambda e: e.tensor_copy(halo[:, 2 * c:2 * c + 2], hsb[:, 512:514])))
                    h2s = f_h2s2[c % 2]
                    lo = 1 if seq_start else 0
                    if is_gp:
                        gA = gcv_t[:, 0:516]; gB = gcv_t[:, 516:1032]
                        b.wait("gpsimd", DV)
                        b.wait("gpsimd", AC, gsilu_ac[0])
                        gp(lambda e, hsb=hsb, c=c, lo=lo, w=w, gA=gA: e.tensor_scalar(
                            gA[:, 0:w], hsb[:, lo + 1:lo + 1 + w], cw[1][:, c:c + 1], None, op0=ALU.mult), inc=0)
                        gp(lambda e, hsb=hsb, c=c, lo=lo, w=w, gB=gB: e.tensor_scalar(
                            gB[:, 0:w], hsb[:, lo:lo + w], cw[0][:, c:c + 1], None, op0=ALU.mult), inc=0)
                        gp(lambda e, w=w, gA=gA, gB=gB: e.tensor_tensor(gA[:, 0:w], gA[:, 0:w], gB[:, 0:w], op=ALU.add), inc=0)
                        gp(lambda e, hsb=hsb, c=c, lo=lo, w=w, gB=gB: e.tensor_scalar(
                            gB[:, 0:w], hsb[:, lo + 2:lo + 2 + w], cw[2][:, c:c + 1], None, op0=ALU.mult), inc=0)
                        gp(lambda e, w=w, gA=gA, gB=gB: e.tensor_tensor(gA[:, 0:w], gA[:, 0:w], gB[:, 0:w], op=ALU.add))
                        gp_done[c] = GP.v
                        b.wait("scalar", GP)
                        b.wait("scalar", ST, h2s_guard[c % 2])
                        b.wait("scalar", DV, m1_dv[c % 2])
                        act(lambda e, h2s=h2s, c=c, w=w, gA=gA: e.activation(h2s[:, 0:w], gA[:, 0:w], AF.Silu, bias=cbv[:, c:c + 1]))
                        gsilu_ac[0] = AC.v
                    else:
                        cva = f_cva2[c % 2]
                        b.wait("vector", AC, silu_ac[c % 2])
                        vec(lambda e, hsb=hsb, cva=cva, c=c, lo=lo, w=w: e.tensor_scalar(
                            cva[:, 0:w], hsb[:, lo + 1:lo + 1 + w], cw[1][:, c:c + 1], None, op0=ALU.mult), inc=0)
                        vec(lambda e, hsb=hsb, cva=cva, c=c, lo=lo, w=w: e.scalar_tensor_tensor(
                            cva[:, 0:w], hsb[:, lo:lo + w], cw[0][:, c:c + 1], cva[:, 0:w],
                            op0=ALU.mult, op1=ALU.add), inc=0)
                        vec(lambda e, hsb=hsb, cva=cva, c=c, lo=lo, w=w: e.scalar_tensor_tensor(
                            cva[:, 0:w], hsb[:, lo + 2:lo + 2 + w], cw[2][:, c:c + 1], cva[:, 0:w],
                            op0=ALU.mult, op1=ALU.add))
                        b.wait("scalar", DV)
                        b.wait("scalar", ST, h2s_guard[c % 2])
                        b.wait("scalar", DV, m1_dv[c % 2])
                        act(lambda e, cva=cva, h2s=h2s, c=c, w=w: e.activation(h2s[:, 0:w], cva[:, 0:w], AF.Silu, bias=cbv[:, c:c + 1]))
                        silu_ac[c % 2] = AC.v
                    def m1_emit(h2s=h2s, w=w, cc=c, ac_at=AC.v, gp_at=GP.v):
                        b.wait("vector", AC, ac_at)
                        b.wait("vector", GP, gp_at)
                        vec(lambda e: e.tensor_tensor(f_m1x[:, 0:w], f_m1x[:, 0:w], h2s[:, 0:w], op=ALU.max), inc=0)
                        vec(lambda e: e.tensor_tensor(f_m1n[:, 0:w], f_m1n[:, 0:w], h2s[:, 0:w], op=ALU.min))
                        m1_dv[cc % 2] = DV.v
                    m1_pend.append(m1_emit)
                    b.wait("sync", AC)
                    tkn0 = t0 - wstart_off
                    dma("sync", H2[128 * c:128 * (c + 1), tkn0:tkn0 + w], h2s[:, 0:w], ST)
                    h2s_guard[c % 2] = ST.v
                while m1_pend:
                    m1_pend.pop(0)()
                if halo_pend[0] is not None:
                    halo_pend[0](); halo_pend[0] = None
                h2st_done[0] = ST.v
                batch = []
                col0 = gcol
                b.wait("vector", GP)
                if prevk3[0] is not None:
                    if not seq_start:
                        vtiny(lambda e: e.tensor_copy(f_k3x[:, 127:128], f_m1x[:, 0:1]))
                        vtiny(lambda e: e.tensor_copy(f_k3n[:, 127:128], f_m1n[:, 0:1]))
                    col0 = prevk3[0][0]
                    batch.append((f_k3x, f_k3n))
                    prevk3[0] = None
                for k in range(3):
                    sl = (128 * k + (0 if seq_start else 1))
                    batch.append((f_m1x[:, sl:sl + 128], f_m1n[:, sl:sl + 128]))
                sl3 = 128 * 3 + (0 if seq_start else 1)
                if seq_end:
                    batch.append((f_m1x[:, sl3:sl3 + 128], f_m1n[:, sl3:sl3 + 128]))
                else:
                    vtiny(lambda e, sl3=sl3: e.tensor_copy(f_k3x[:, 0:127], f_m1x[:, sl3:sl3 + 127]))
                    vtiny(lambda e, sl3=sl3: e.tensor_copy(f_k3n[:, 0:127], f_m1n[:, sl3:sl3 + 127]))
                    prevk3[0] = (gcol + 3,)
                finish_amax_batch(col0, batch)
                yfm_guard2[g % 2] = PE.v

            if gi < NG:
                g = gi; t0 = g * G
                gcol = g * NSUB
                b.wait("sync", DV, xg_guard[0])
                b.wait("sync", AC, xg_guard_ac[0] if isinstance(xg_guard_ac, list) else 0)
                b.wait("sync", ST, xs_st)
                dma("sync", f_xg[:, :, :], src_XS[t0:t0 + G, :].rearrange("(m p) c -> p m c", p=128), LD)
                b.wait("vector", LD)
                group_ln(f_xg, gcol, NSUB, stat_mu, stat_iv)
                for m in range(NSUB):
                    col = gcol + m
                    b.wait("vector", ST, xbt_st_guard[0])
                    vec(lambda e, m=m, col=col: e.tensor_scalar(
                        f_xbt, f_xg[:, m, :], stat_mu[:, col:col + 1], None, op0=ALU.subtract), inc=0)
                    vec(lambda e, m=m: e.reduce_max(tiny[:, m:m + 1], f_xbt, axis=AX.X, apply_absolute_value=True))
                    b.wait("vector", DV)
                iv4 = stat_iv[:, gcol:gcol + 4]
                vtiny(lambda e, iv4=iv4: e.tensor_mul(tiny[:, 4:8], tiny[:, 0:4], iv4))
                vtiny(lambda e: e.tensor_scalar(tiny[:, 4:8], tiny[:, 4:8], 1e-5, None, op0=ALU.max))
                vtiny(lambda e, gcol=gcol: e.tensor_scalar(rabq[:, gcol:gcol + 4], tiny[:, 4:8], invws1[:, 0:1], 1.0 / 127.0, op0=ALU.mult, op1=ALU.mult))
                vtiny(lambda e: e.reciprocal(tiny[:, 8:12], tiny[:, 4:8]))
                vtiny(lambda e: e.tensor_scalar(tiny[:, 8:12], tiny[:, 8:12], 127.0, None, op0=ALU.mult))
                vtiny(lambda e, gcol=gcol, iv4=iv4: e.tensor_mul(aq[:, gcol:gcol + 4], tiny[:, 8:12], iv4))
                vtiny(lambda e, gcol=gcol: e.tensor_mul(tiny[:, 0:4], stat_mu[:, gcol:gcol + 4], aq[:, gcol:gcol + 4]))
                vtiny(lambda e: e.tensor_scalar(tiny[:, 0:4], tiny[:, 0:4], -1.0, None, op0=ALU.mult))
                b.wait("scalar", DV)
                for m in range(NSUB):
                    st_ = t0 + m * 128
                    col = gcol + m
                    yb2 = ybf if m % 2 == 0 else ybf2
                    b.wait("scalar", ST, ybf_guard2[m % 2])
                    act(lambda e, m=m, col=col: e.activation(
                        i32_t[:, 0:D], f_xg[:, m, :], AF.Identity,
                        scale=aq[:, col:col + 1], bias=tiny[:, m:m + 1]), inc=0)
                    act(lambda e, yb2=yb2: e.activation(yb2, i32_t[:, 0:D], AF.Copy))
                    b.wait("gpsimd", AC)
                    dma("gpsimd", YB[st_:st_ + 128, :], yb2, ST)
                    ybf_guard2[m % 2] = ST.v
                xg_guard[0] = DV.v
                xg_guard_ac[0] = AC.v
        b.wait("gpsimd", PE)
        w2q = wta_t[:, 0:32768].rearrange("p (c n) -> p c n", c=32)
        for c2 in range(0, 32, 4):
            dma("gpsimd", w2q[:, c2:c2 + 4, :], WQ2[128 * c2:128 * (c2 + 4), :].rearrange("(c p) n -> p c n", p=128), LW)
        lw_b = LW.v
        hq = bfa_t[:, 16384:32768].rearrange("p (c n) -> p c n", c=32)
        if dbg == "W2Q":
            b.wait("gpsimd", LW)
            dma("gpsimd", H2[0:4096, 0:1024].rearrange("(x p) c -> p x c", p=128), wta_t[:, :].rearrange("p (x c) -> p x c", x=32), ST)
        b.wait("sync", ST, h2st_done[0])
        xo_guard = [ST.v]
        x2_guard = [DV.v]
        wevac_dv = [DV.v]
        for g in range(NG):
            t0 = g * G
            gcol = g * NSUB
            b.wait("vector", PE)
            b.wait("vector", AC)
            for m in range(NSUB):
                vec(lambda e, m=m, gcol=gcol: e.tensor_scalar(
                    f_mq[:, 128 * m:128 * (m + 1)], ident[:, :], Aq[:, gcol + m:gcol + m + 1], None,
                    op0=ALU.mult), inc=(m == NSUB - 1))
            b.wait("tensor", DV)
            b.wait("tensor", ST, h2st_done[0])
            b.wait("tensor", LW, lw_b)
            for m in range(NSUB):
                pe(lambda e, m=m: e.matmul(pb[2][:, 128 * m:128 * (m + 1)], onesf[:, :], f_mq[:, 128 * m:128 * (m + 1)], start=True, stop=True), inc=(m == NSUB - 1))
            b.wait("vector", PE)
            vec(lambda e: e.tensor_copy(f_bcB, pb[2][:, :]))
            b.wait("sync", DV, x2_guard[0])
            b.wait("sync", ST, xs_st)
            dma("sync", f_xg2[:, :, :], src_XS[t0:t0 + G, :].rearrange("(m p) c -> p m c", p=128), LD)
            hq_gp = [0] * 32
            b.wait("gpsimd", DV)
            mq2 = [f_mq, f32b_t[:, 0:512]]
            mq_ac = [AC.v, AC.v]
            cast_ac = [0] * 32
            wbanks = [pb[0][:, :], pb[1][:, :], pb[3][:, :], pb[4][:, :],
                      kvps[:, 0:512], kvps[:, 512:1024], kvps[:, 1024:1536], pb[2][:, :]]
            b.wait("tensor", DV, wevac_dv[0])
            b.wait("vector", LD)
            for cc in range(34):
                if cc < 32:
                    c = cc
                    hb = f_h2i[c % 2]
                    mq = mq2[c % 2]
                    b.wait("sync", GP, hq_gp[c - 2] if c >= 2 else 0)
                    dma("sync", hb, H2[128 * c:128 * (c + 1), t0:t0 + G], LD)
                    b.wait("gpsimd", LD)
                    b.wait("gpsimd", AC, mq_ac[c % 2])
                    gp(lambda e, hb=hb, mq=mq: e.tensor_tensor(mq, hb, f_bcB, op=ALU.mult))
                    hq_gp[c] = GP.v
                    b.wait("scalar", GP)
                    act(lambda e, mq=mq: e.activation(i32_t[:, 0:G], mq, AF.Copy), inc=0)
                    act(lambda e, c=c: e.activation(hq[:, c, :], i32_t[:, 0:G], AF.Copy))
                    mq_ac[c % 2] = AC.v
                    cast_ac[c] = AC.v
                if cc >= 2:
                    c = cc - 2
                    b.wait("tensor", AC, cast_ac[c])
                    for k8 in range(8):
                        n, m = k8 // 4, k8 % 4
                        pe(lambda e, n=n, m=m, c=c, k8=k8, wbanks=wbanks: e.matmul(
                            wbanks[k8],
                            hq[:, c, 128 * m:128 * (m + 1)],
                            w2q[:, c, 512 * n:512 * (n + 1)],
                            start=(c == 0), stop=(c == 31)),
                            inc=(1 if (c == 31 and k8 == 7) else 0))
            if dbg == "HQ" and g == 0:
                b.wait("gpsimd", DV)
                dma("gpsimd", OB[0:2048, :].rearrange("(x p) c -> p x c", p=128), hq[:, :, :].rearrange("p c t -> p (c t)").rearrange("p (x c) -> p x c", x=16), ST)
            b.wait("vector", PE)
            b.wait("vector", ST, xo_guard[0])
            for k8 in range(8):
                n, m = k8 // 4, k8 % 4
                vec(lambda e, n=n, m=m, k8=k8, gcol=gcol, wbanks=wbanks: e.scalar_tensor_tensor(
                    f_xg[:, m, 512 * n:512 * (n + 1)], wbanks[k8], rAq[:, gcol + m:gcol + m + 1],
                    f_xg2[:, m, 512 * n:512 * (n + 1)], op0=ALU.mult, op1=ALU.add),
                    inc=(1 if k8 == 7 else 0))
            wevac_dv[0] = DV.v
            x2_guard[0] = DV.v
            b.wait("gpsimd", DV)
            for m in range(NSUB):
                st_ = t0 + m * 128
                dma("gpsimd", dst_XS[st_:st_ + 128, :], f_xg[:, m, :], ST)
            xo_guard[0] = ST.v

    if n_sublayers >= 2:
        emit_ffn(2, XS, XS if n_sublayers > 2 else XS, dbg=dbg)

    HB, DKB = 32, 32
    if n_sublayers >= 3:
        b.wait("gpsimd", PE)
        wta3 = Arena(wta_t, 32768)
        wqkv_sb = wta3.alloc(8 * 3 * D).rearrange("p (c n) -> p c n", c=8)
        wproj_sb = wta3.alloc(8 * D).rearrange("p (c n) -> p c n", c=8)
        for dch in range(8):
            dma("gpsimd", wqkv_sb[:, dch, :], ap["wqkvT"][128 * dch:128 * (dch + 1), :], LW)
            dma("gpsimd", wproj_sb[:, dch, :], ap["wprojT"][128 * dch:128 * (dch + 1), :], LW)
        p3_lw = LW.v
        bfa3 = Arena(bfa_t, 38400)
        yfm3 = [bfa3.alloc(8 * G).rearrange("p (c n) -> p c n", c=8) for _ in range(2)]
        ofm3 = [bfa3.alloc(8 * G).rearrange("p (c n) -> p c n", c=8) for _ in range(2)]
        k2_sb = bfa3.alloc(D)
        v2ext = bfa3.alloc(HB * 33).rearrange("p (h c) -> p h c", h=HB)
        Q2hold = bfa3.alloc(8 * SEQ).rearrange("p (c n) -> p c n", c=8)
        vkblk = bfa3.alloc(8 * 132).rearrange("p (q c) -> p q c", q=8)
        o2_sb = bfa3.alloc(D)
        ybf3 = bfa3.alloc(D)

        def vk_col(h):
            b_ = min(h // 15, 2)
            return 512 * b_ + 33 * (h - 15 * b_)

        xs_st3 = ST.v
        xgr_guard = [DV.v]
        xgx_guard = [DV.v]
        yfm_guard = [PE.v, PE.v]
        ofm_guard = [PE.v, PE.v]
        ybf_guard = [ST.v]
        osb_guard = [ST.v]
        xbt_guard = [ST.v]
        first_phase = [True]
        for s in range(NSEQ):
            ydone_st = [0] * GPS
            for gi in range(GPS + 2):
                if gi >= 2:
                    g = gi - 2; t0 = (s * GPS + g) * G
                    yf = yfm3[g % 2]
                    b.wait("sync", ST, ydone_st[g])
                    b.wait("sync", PE, yfm_guard[g % 2])
                    for dch in range(8):
                        dmaT("sync", yf[:, dch, :], YB[t0:t0 + G, 128 * dch:128 * (dch + 1)], LT)
                    b.wait("tensor", LT)
                    b.wait("tensor", LW, p3_lw)
                    if first_phase[0] and g == 0:
                        b.wait("vector", PE)
                        vec(lambda e: e.memset(v2ext[:, :, 32:33], 1.0))
                        b.wait("tensor", DV)
                        first_phase[0] = False
                    for m in range(NSUB):
                        for off, bank in ((D, 0), (2 * D, 2)):
                            for n in range(2):
                                for dch in range(8):
                                    pe(lambda e, off=off, bank=bank, n=n, dch=dch, m=m, yf=yf: e.matmul(
                                        pb[bank + n][:, :],
                                        yf[:, dch, m * 128:(m + 1) * 128],
                                        wqkv_sb[:, dch, off + n * 512:off + (n + 1) * 512],
                                        start=(dch == 0), stop=(dch == 7)),
                                        inc=(1 if dch == 7 else 0))
                        kv_pe = PE.v
                        b.wait("scalar", PE, kv_pe)
                        for n in range(2):
                            act(lambda e, n=n: e.activation(k2_sb[:, n * 512:(n + 1) * 512], pb[n][:, :], AF.Relu))
                        k_ac = AC.v
                        b.wait("vector", PE, kv_pe)
                        for n in range(2):
                            vec(lambda e, n=n: e.tensor_copy(
                                v2ext[:, 16 * n:16 * (n + 1), 0:32],
                                pb[2 + n][:, :]))
                        v_dv = DV.v
                        for qch in range(8):
                            if qch == 0:
                                bank = 4
                            else:
                                bank = (qch - 1) % 2
                                b.wait("tensor", AC, k_ac)
                            if qch >= 2:
                                b.wait("tensor", AC, AC.v - 2)
                            for dch in range(8):
                                pe(lambda e, bank=bank, qch=qch, dch=dch, m=m, yf=yf: e.matmul(
                                    pb[bank][:, 0:128],
                                    wqkv_sb[:, dch, qch * 128:(qch + 1) * 128],
                                    yf[:, dch, m * 128:(m + 1) * 128],
                                    start=(dch == 0), stop=(dch == 7)),
                                    inc=(1 if dch == 7 else 0))
                            b.wait("scalar", PE)
                            act(lambda e, bank=bank, qch=qch, g=g, m=m: e.activation(
                                Q2hold[:, qch, (g * NSUB + m) * 128:(g * NSUB + m + 1) * 128],
                                pb[bank][:, 0:128], AF.Relu))
                        b.wait("tensor", AC)
                        b.wait("tensor", DV, v_dv)
                        firstmn = (g == 0 and m == 0)
                        lastmn = (g == GPS - 1 and m == NSUB - 1)
                        for h in range(HB):
                            first = firstmn and h in (0, 15, 30)
                            last = lastmn and h in (14, 29, 31)
                            pe(lambda e, h=h, first=first, last=last: e.matmul(
                                kvps[0:32, vk_col(h):vk_col(h) + 33],
                                k2_sb[:, 32 * h:32 * (h + 1)],
                                v2ext[:, h, :],
                                start=first, stop=last),
                                inc=(1 if (lastmn and h == HB - 1) else 0))
                    yfm_guard[g % 2] = PE.v
                if gi < GPS:
                    g = gi; t0 = (s * GPS + g) * G
                    b.wait("sync", DV, xgr_guard[0])
                    b.wait("sync", ST, xs_st3)
                    dma("sync", xgr, XS[t0:t0 + G, :].rearrange("(m p) c -> p m c", p=128), LD)
                    b.wait("vector", LD)
                    gcol = (s * GPS + g) * NSUB
                    group_ln(xgr, gcol, NSUB, stat_mu, stat_iv)
                    for m in range(NSUB):
                        st_ = t0 + m * 128
                        col = gcol + m
                        b.wait("vector", ST, ybf_guard[0])
                        vec(lambda e, m=m, col=col: e.tensor_scalar(
                            ybf3, xgr[:, m, :], stat_mu[:, col:col + 1], stat_iv[:, col:col + 1],
                            op0=ALU.subtract, op1=ALU.mult))
                        b.wait("gpsimd", DV)
                        dma("gpsimd", YB[st_:st_ + 128, :], ybf3, ST)
                        ybf_guard[0] = ST.v
                    xgr_guard[0] = DV.v
                    ydone_st[g] = ST.v
            b.wait("vector", PE)
            vec(lambda e: e.memset(vkblk[:, :, :], 0.0), inc=0)
            for h in range(HB):
                q_, r_ = h // 4, h % 4
                vec(lambda e, h=h, q_=q_, r_=r_: e.tensor_copy(
                    vkblk[32 * r_:32 * (r_ + 1), q_, 33 * r_:33 * r_ + 33],
                    kvps[0:32, vk_col(h):vk_col(h) + 33]), inc=(1 if h == HB - 1 else 0))
            kv_dv = DV.v

            odone_st = [0] * GPS
            evdv = [DV.v, DV.v]
            nbuf_ac = [AC.v]
            for gi in range(GPS + 2):
                if gi < GPS:
                    g = gi; t0 = (s * GPS + g) * G
                    b.wait("tensor", DV, kv_dv)
                    for m in range(NSUB):
                        st_ = t0 + m * 128
                        b.wait("vector", AC, nbuf_ac[0])
                        for q_ in range(8):
                            bank = q_ % 2
                            b.wait("tensor", DV, evdv[bank])
                            pe(lambda e, q_=q_, bank=bank, g=g, m=m: e.matmul(
                                pb[bank][:, 0:132],
                                Q2hold[:, q_, (g * NSUB + m) * 128:(g * NSUB + m + 1) * 128],
                                vkblk[:, q_, :],
                                start=True, stop=True))
                            b.wait("vector", PE)
                            vec(lambda e, bank=bank, q_=q_: e.tensor_copy(
                                numbuf[:, 132 * q_:132 * q_ + 132], pb[bank][:, 0:132]))
                            evdv[bank] = DV.v
                        nb3 = numbuf[:, 0:1056].rearrange("p (q c) -> p q c", c=132)
                        for j in range(4):
                            vec(lambda e, j=j, nb3=nb3: e.tensor_scalar(
                                den8[:, 8 * j:8 * j + 8], nb3[:, :, 33 * j + 32:33 * j + 33],
                                1e-15, None, op0=ALU.add), inc=(j == 3))
                        b.wait("vector", DV)
                        vtiny(lambda e: e.reciprocal(den8[:, 0:32], den8[:, 0:32]))
                        b.wait("scalar", DV)
                        b.wait("scalar", ST, osb_guard[0])
                        for q_ in range(8):
                            for j in range(4):
                                act(lambda e, q_=q_, j=j: e.activation(
                                    o2_sb[:, 128 * q_ + 32 * j:128 * q_ + 32 * (j + 1)],
                                    numbuf[:, 132 * q_ + 33 * j:132 * q_ + 33 * j + 32],
                                    AF.Copy, scale=den8[:, 8 * j + q_:8 * j + q_ + 1]),
                                    inc=(1 if (q_ == 7 and j == 3) else 0))
                        nbuf_ac[0] = AC.v
                        b.wait("gpsimd", AC)
                        dma("gpsimd", OB[st_:st_ + 128, :], o2_sb, ST)
                        osb_guard[0] = ST.v
                    odone_st[g] = ST.v
                if gi >= 2:
                    g = gi - 2; t0 = (s * GPS + g) * G
                    of = ofm3[g % 2]
                    b.wait("sync", ST, odone_st[g])
                    b.wait("sync", PE, ofm_guard[g % 2])
                    for dch in range(8):
                        dmaT("sync", of[:, dch, :], OB[t0:t0 + G, 128 * dch:128 * (dch + 1)], LT)
                    b.wait("sync", DV, xgx_guard[0])
                    b.wait("sync", ST, xs_st3)
                    dma("sync", xgx, XS[t0:t0 + G, :].rearrange("(m p) c -> p m c", p=128), LD)
                    b.wait("tensor", LT)
                    for m in range(NSUB):
                        st_ = t0 + m * 128
                        fcdv = DV.v
                        b.wait("tensor", DV, fcdv)
                        for n in range(2):
                            for dch in range(8):
                                pe(lambda e, n=n, dch=dch, m=m, of=of: e.matmul(
                                    pb[2 + n][:, :],
                                    of[:, dch, m * 128:(m + 1) * 128],
                                    wproj_sb[:, dch, n * 512:(n + 1) * 512],
                                    start=(dch == 0), stop=(dch == 7)),
                                    inc=(1 if dch == 7 else 0))
                        b.wait("vector", LD)
                        b.wait("vector", PE)
                        b.wait("vector", ST, xbt_guard[0])
                        for n in range(2):
                            vec(lambda e, n=n, m=m: e.tensor_add(
                                xbt[:, n * 512:(n + 1) * 512], pb[2 + n][:, :],
                                bproj[:, n * 512:(n + 1) * 512]), inc=0)
                        vec(lambda e, m=m: e.tensor_add(xbt, xbt, xgx[:, m, :]))
                        b.wait("gpsimd", DV)
                        dma("gpsimd", XS[st_:st_ + 128, :], xbt, ST)
                        xbt_guard[0] = ST.v
                    xgx_guard[0] = DV.v
                    ofm_guard[g % 2] = PE.v
            xgr_guard[0] = DV.v

    if n_sublayers >= 4:
        emit_ffn(4, XS, out, dbg=None)

    if n_sublayers < 4 or dbg is not None:
        b.wait("gpsimd", ST)
        b.wait("gpsimd", DV)
        b.wait("gpsimd", AC)
        if dbg == "PER":
            for i in range(3):
                dma("gpsimd", out[128 * i:128 * (i + 1), :], per_t[:, 1024 * i:1024 * (i + 1)], ST)
            dma("gpsimd", out[384:512, 0:3840 - 3072], per_t[:, 3072:3840], ST)
            b.wait("gpsimd", ST)
            srcT = None
        else:
            srcT = {None: XS, "XS": XS, "YB": YB, "OB": OB, "HQ": OB, "H2": H2, "W2Q": H2, "WQ1": WQ1, "WQ2": WQ2}[dbg]
        nrow = srcT.tensor.shape[0] if srcT is not None else 0
        for i in range(nrow // 128):
            dma("gpsimd", wld[i % 2][:, 0:D], srcT[128 * i:128 * (i + 1), 0:D], LD)
            b.wait("gpsimd", LD)
            dma("gpsimd", out[128 * i:128 * (i + 1), :], wld[i % 2][:, 0:D], ST)
        b.wait("gpsimd", ST)
    else:
        b.wait("gpsimd", ST)

    with nc.Block() as blk:
        for eng in B.ENGS:
            ops = b.q[eng]
            getattr(blk, eng)(lambda e, ops=ops: [f(e) for f in ops])
    es.close()
    return nc


def kernel(**inputs):
    per_core = host_prep(inputs)
    nc = build_program(n_sublayers=4)
    res = run_bass_kernel_spmd(nc, per_core, core_ids=list(range(NCORES)))
    return np.concatenate([r["out"] for r in res.results], axis=0).reshape(16, 2048, 1024)


# version 12 (final)
# speedup vs baseline: 1.2408x; 1.0026x over previous
import numpy as np
import ml_dtypes
import concourse.bass as bass
import concourse.mybir as mybir
from concourse.bass_utils import run_bass_kernel_spmd
from contextlib import ExitStack

F32 = mybir.dt.float32
BF16 = mybir.dt.bfloat16
I32 = mybir.dt.int32
AF = mybir.ActivationFunctionType
ALU = mybir.AluOpType
AX = mybir.AxisListType
bf16 = ml_dtypes.bfloat16

NCORES = 8
SEQ = 2048
NSEQ = 2
T = SEQ * NSEQ
D = 1024
DH = 4096
G = 512
NG = T // G
GPS = SEQ // G
NSUB = 4
HA, DKA = 16, 64
LN_EPS = 1e-5

class Ctr:
    def __init__(self, handle):
        self.h = handle; self.v = 0

class B:
    ENGS = ("sync", "gpsimd", "scalar", "vector", "tensor")
    def __init__(self):
        self.q = {e: [] for e in self.ENGS}
    def wait(self, eng, ctr, val=None):
        v = ctr.v if val is None else val
        if v <= 0: return
        h = ctr.h
        self.q[eng].append(lambda e: e.wait_ge(h, v))
    def _inc(self, eng, fn, ctr, amt):
        if ctr is None:
            self.q[eng].append(fn)
        else:
            h = ctr.h
            self.q[eng].append(lambda e: fn(e).then_inc(h, amt))
            ctr.v += amt

class Arena:
    def __init__(self, tens, width):
        self.t = tens; self.width = width; self.off = 0
    def alloc(self, n):
        assert self.off + n <= self.width, f"arena overflow {self.off}+{n}>{self.width}"
        v = self.t[:, self.off:self.off + n]
        self.off += n
        return v

def host_prep(inputs):
    x = np.asarray(inputs["L0_attn_a_inputs"])
    d = {}
    d["wq"] = np.ascontiguousarray(inputs["L0_attn_a_W_Q"])
    d["wk"] = np.ascontiguousarray(inputs["L0_attn_a_W_K"])
    d["wv"] = np.ascontiguousarray(inputs["L0_attn_a_W_V"])
    d["wfc"] = np.ascontiguousarray(inputs["L0_attn_a_W_fc"])
    d["gamma_bc"] = np.ascontiguousarray(np.broadcast_to(np.asarray(inputs["L0_attn_a_gamma"])[None, :], (128, D)))
    d["beta_bc"] = np.ascontiguousarray(np.broadcast_to(np.asarray(inputs["L0_attn_a_beta"])[None, :], (128, D)))
    d["wqkvT"] = np.ascontiguousarray(np.asarray(inputs["L0_attn_b_W_qkv"]).T)
    d["wprojT"] = np.ascontiguousarray(np.asarray(inputs["L0_attn_b_W_proj"]).T)
    d["bproj_bc"] = np.ascontiguousarray(np.broadcast_to(np.asarray(inputs["L0_attn_b_b_proj"])[None, :], (128, D)))
    d["w1T"] = np.ascontiguousarray(np.asarray(inputs["L0_ffn_w1"]).T)
    d["w2T"] = np.ascontiguousarray(np.asarray(inputs["L0_ffn_w2"]).T)
    cwa = np.asarray(inputs["L0_ffn_conv_w"])
    for j in range(3):
        d[f"cw{j}"] = np.ascontiguousarray(cwa[:, 0, j].reshape(32, 128).T)
    d["cb"] = np.ascontiguousarray(np.asarray(inputs["L0_ffn_conv_b"]).reshape(32, 128).T)
    d["ident"] = np.eye(128, dtype=np.float32)
    per_core = []
    for c in range(NCORES):
        m = dict(d)
        m["xin"] = np.ascontiguousarray(x[2 * c:2 * c + 2].reshape(T, D))
        per_core.append(m)
    return per_core


def build_program(n_sublayers=4, dbg=None):
    nc = bass.Bass("TRN2", target_bir_lowering=False)
    es = ExitStack()
    ap = {}
    def din(name, shape, dt=F32):
        ap[name] = nc.dram_tensor(name, list(shape), dt, kind="ExternalInput").ap()
    din("xin", (T, D))
    for w in ("wq", "wk", "wv", "wfc"):
        din(w, (D, D))
    din("gamma_bc", (128, D)); din("beta_bc", (128, D))
    din("wqkvT", (D, 3 * D)); din("wprojT", (D, D)); din("bproj_bc", (128, D))
    din("w1T", (D, DH)); din("w2T", (DH, D))
    for j in range(3):
        din(f"cw{j}", (128, 32))
    din("cb", (128, 32))
    din("ident", (128, 128))
    out = nc.dram_tensor("out", [T, D], F32, kind="ExternalOutput").ap()
    XS = nc.dram_tensor("XS", [T, D], F32, kind="Internal").ap()
    YB = nc.dram_tensor("YB", [T, D], BF16, kind="Internal").ap()
    OB = nc.dram_tensor("OB", [T, D], BF16, kind="Internal").ap()
    H2 = nc.dram_tensor("H2", [DH, T], F32, kind="Internal").ap()
    WQ1 = nc.dram_tensor("WQ1", [D, DH], BF16, kind="Internal").ap()
    WQ2 = nc.dram_tensor("WQ2", [DH, D], BF16, kind="Internal").ap()

    bfa_t = es.enter_context(nc.sbuf_tensor("bfa", [128, 38400], BF16))
    wta_t = es.enter_context(nc.sbuf_tensor("wta", [128, 32768], BF16))
    f32a_t = es.enter_context(nc.sbuf_tensor("f32a", [128, 10240], F32))
    per_t = es.enter_context(nc.sbuf_tensor("per", [128, 3904], F32))
    i32_t = es.enter_context(nc.sbuf_tensor("i32b", [128, 1024], I32))
    f32b_t = es.enter_context(nc.sbuf_tensor("f32b", [128, 1152], F32))
    gcv_t = es.enter_context(nc.sbuf_tensor("gcv", [128, 1032], F32))
    per = Arena(per_t, 3904)
    numbuf = f32b_t[:, 0:1088]

    pb = [es.enter_context(nc.psum_tensor(f"pb{i}", [128, 512], F32)) for i in range(5)]
    kvps = es.enter_context(nc.psum_tensor("kvps", [128, 1536], F32))

    ident = per.alloc(128)
    gamma = per.alloc(D); beta = per.alloc(D); bproj = per.alloc(D)
    cw = [per.alloc(32) for _ in range(3)]
    cbv = per.alloc(32)
    onesf = per.alloc(128)
    invws1 = per.alloc(1); invws2 = per.alloc(1)
    stat_mu = per.alloc(32); stat_iv = per.alloc(32)
    stat_mu2 = per.alloc(4); stat_iv2 = per.alloc(4)
    aq = per.alloc(32)
    rabq = per.alloc(32)
    Aq = per.alloc(32)
    rAq = per.alloc(32)
    scr = per.alloc(64)
    den8 = per.alloc(32)
    tiny = per.alloc(32)

    b = B()
    sems = {k: Ctr(es.enter_context(nc.semaphore(k))) for k in ("LD", "LW", "LT", "ST", "PE", "DV", "AC", "GP")}
    LD, LW, LT, ST, PE, DV, AC, GP = (sems[k] for k in ("LD", "LW", "LT", "ST", "PE", "DV", "AC", "GP"))

    def dma(qeng, outv, inv, ctr):
        b._inc(qeng, lambda e, o=outv, i=inv: e.dma_start(out=o, in_=i), ctr, 16)
    def dmaT(qeng, outv, inv, ctr):
        b._inc(qeng, lambda e, o=outv, i=inv: e.dma_start_transpose(o, i), ctr, 16)
    def vec(fn, inc=1):
        b._inc("vector", fn, DV if inc else None, 1)
    def act(fn, inc=1):
        b._inc("scalar", fn, AC if inc else None, 1)
    def gp(fn, inc=1):
        b._inc("gpsimd", fn, GP if inc else None, 1)
    def pe(fn, inc=1):
        b._inc("tensor", fn, PE if inc else None, 1)

    dma("sync", ident[:], ap["ident"][:, :], LD)
    dma("sync", gamma[:], ap["gamma_bc"][:, :], LD)
    dma("sync", beta[:], ap["beta_bc"][:, :], LD)
    dma("sync", bproj[:], ap["bproj_bc"][:, :], LD)
    for j in range(3):
        dma("sync", cw[j][:], ap[f"cw{j}"][:, :], LD)
    dma("sync", cbv[:], ap["cb"][:, :], LD)
    b.wait("vector", LD)
    vec(lambda e: e.memset(onesf[:], 1.0), inc=0)
    vec(lambda e: e.memset(scr[:], 0.0))

    def vtiny(fn):
        vec(fn, inc=1)
        for _ in range(16):
            b.q["vector"].append(lambda e: e.engine_nop())
    def atiny(fn):
        act(fn, inc=1)
        b.wait("scalar", AC)

    f32a = Arena(f32a_t, 10240)
    wld = [f32a.alloc(4096), f32a.alloc(4096)]
    wbf_a = Arena(bfa_t, 38400)
    wbf = wbf_a.alloc(4096)

    def quant_weight(wapT, nchunk, wqap, invws_col):
        cols = DH if nchunk == 8 else D
        guard = [DV.v, DV.v]
        accv = tiny[:, 0:nchunk] if nchunk <= 16 else None
        acc32 = per
        accview = tiny[:, 0:8] if nchunk == 8 else scr[:, 0:32]
        for c in range(nchunk):
            buf = wld[c % 2][:, :cols]
            b.wait("sync", DV, guard[c % 2])
            dma("sync", buf, wapT[128 * c:128 * (c + 1), :], LD)
            b.wait("vector", LD)
            vec(lambda e, buf=buf, c=c: e.reduce_sum(accview[:, c:c + 1], buf, axis=AX.X, apply_absolute_value=True))
            guard[c % 2] = DV.v
        b.wait("vector", DV)
        vec(lambda e: e.reduce_sum(scr[:, 40:41], accview, axis=AX.X))
        b.wait("vector", DV)
        b.wait("tensor", DV)
        pe(lambda e: e.matmul(pb[0][0:1, 0:1], onesf[:, 0:1], scr[:, 40:41], start=True, stop=True))
        nelem = float(128 * nchunk * cols)
        b.wait("vector", PE)
        vtiny(lambda e: e.tensor_scalar(scr[0:1, 41:42], pb[0][0:1, 0:1], 1.0 / nelem, 1e-5, op0=ALU.mult, op1=ALU.max))
        b.wait("tensor", DV)
        pe(lambda e: e.matmul(pb[0][:, 4:5], onesf[0:1, :], scr[0:1, 41:42], start=True, stop=True))
        b.wait("vector", PE)
        vtiny(lambda e: e.tensor_copy(invws_col[:, 0:1], pb[0][:, 4:5]))
        vtiny(lambda e: e.reciprocal(scr[:, 42:43], invws_col[:, 0:1]))
        wsv = scr[:, 42:43]
        st_guard = ST.v
        for c in range(nchunk):
            buf = wld[c % 2][:, :cols]
            b.wait("sync", DV, guard[c % 2])
            dma("sync", buf, wapT[128 * c:128 * (c + 1), :], LD)
            b.wait("vector", LD)
            b.wait("vector", ST, st_guard)
            for h0 in range(0, cols, 1024):
                hw = min(1024, cols - h0)
                vec(lambda e, buf=buf, h0=h0, hw=hw: e.tensor_scalar(i32_t[:, :hw], buf[:, h0:h0 + hw], wsv, None, op0=ALU.mult), inc=0)
                vec(lambda e, h0=h0, hw=hw: e.tensor_scalar(wbf[:, h0:h0 + hw], i32_t[:, :hw], -1.0, 1.0, op0=ALU.max, op1=ALU.min), inc=(1 if h0 + hw >= cols else 0))
            guard[c % 2] = DV.v
            b.wait("gpsimd", DV)
            dma("gpsimd", wqap[128 * c:128 * (c + 1), :], wbf[:, :cols], ST)
            st_guard = ST.v
    quant_weight(ap["w1T"], 8, WQ1, invws1)
    quant_weight(ap["w2T"], 32, WQ2, invws2)

    wta = Arena(wta_t, 32768)
    w_at = {}
    for w in ("wk", "wv", "wq", "wfc"):
        w_at[w] = wta.alloc(8 * D).rearrange("p (c n) -> p c n", c=8)
        for dch in range(8):
            dma("gpsimd", w_at[w][:, dch, :], ap[w][128 * dch:128 * (dch + 1), :], LW)
    p1_lw = LW.v

    bfa = Arena(bfa_t, 38400)
    yfm = [bfa.alloc(8 * G).rearrange("p (c n) -> p c n", c=8) for _ in range(2)]
    ofm = [bfa.alloc(8 * G).rearrange("p (c n) -> p c n", c=8) for _ in range(2)]
    Khold = bfa.alloc(D)
    Vext = bfa.alloc(HA * 65).rearrange("p (h c) -> p h c", h=HA)
    Qhold = bfa.alloc(8 * SEQ).rearrange("p (c n) -> p c n", c=8)
    KVblk = bfa.alloc(8 * 130).rearrange("p (q c) -> p q c", q=8)
    o_sb = bfa.alloc(D)
    ybf = bfa.alloc(D)
    xgr = f32a_t[:, 0:4096].rearrange("p (m c) -> p m c", m=4)
    xgx = f32a_t[:, 4096:8192].rearrange("p (m c) -> p m c", m=4)
    junk = f32a_t[:, 8192:9216]
    xbt = f32a_t[:, 9216:10240]
    st2 = per_t[:, 0:0]

    def group_ln(xg, base_col, n=NSUB, mu_arr=None, iv_arr=None):
        mu = mu_arr[:, base_col:base_col + n]
        iv = iv_arr[:, base_col:base_col + n]
        s_ = tiny[:, 0:n]; q_ = tiny[:, 4:4 + n]; m2 = tiny[:, 8:8 + n]
        for m in range(n):
            vec(lambda e, m=m: e.reduce_sum(s_[:, m:m + 1], xg[:, m, :], axis=AX.X), inc=(m == n - 1))
        b.wait("scalar", DV)
        for m in range(n):
            act(lambda e, m=m: e.activation(junk, xg[:, m, :], AF.Square, accum_out=q_[:, m:m + 1]), inc=(m == n - 1))
        b.wait("vector", AC)
        vtiny(lambda e: e.tensor_scalar(mu, s_, 1.0 / D, None, op0=ALU.mult))
        vtiny(lambda e: e.tensor_mul(m2, mu, mu))
        vtiny(lambda e: e.scalar_tensor_tensor(iv, q_, 1.0 / D, m2, op0=ALU.mult, op1=ALU.subtract))
        vtiny(lambda e: e.tensor_scalar(iv, iv, LN_EPS, None, op0=ALU.add))
        vtiny(lambda e: e.reciprocal(iv, iv))
        b.wait("scalar", DV)
        atiny(lambda e: e.activation(iv, iv, AF.Sqrt))
        b.wait("vector", AC)

    def kvcol(h):
        return (h // 7) * 512 + (h % 7) * 65 if h < 14 else 1024 + (h - 14) * 65

    if n_sublayers >= 1:
        qevac_ac = [AC.v, AC.v]
        xgr_guard = [DV.v]
        xgx_guard = [DV.v]
        yfm_guard = [0, 0]
        ofm_guard = [0, 0]
        ybf_guard = [0]
        osb_guard = [0]
        xbt_guard = [0]
        for s in range(NSEQ):
            ydone_st = [0] * GPS
            for gi in range(GPS + 2):
                if gi >= 2:
                    g = gi - 2; t0 = (s * GPS + g) * G
                    yf = yfm[g % 2]
                    b.wait("sync", ST, ydone_st[g])
                    b.wait("sync", PE, yfm_guard[g % 2])
                    for dch in range(8):
                        dmaT("sync", yf[:, dch, :], YB[t0:t0 + G, 128 * dch:128 * (dch + 1)], LT)
                    b.wait("tensor", LT)
                    b.wait("tensor", LW, p1_lw)
                    if s == 0 and g == 0:
                        b.wait("vector", LD, 0)
                        vec(lambda e: e.memset(Vext[:, :, 64:65], 1.0))
                        b.wait("tensor", DV)
                    for m in range(NSUB):
                        for wname, bank in (("wk", 0), ("wv", 2)):
                            for n in range(2):
                                for dch in range(8):
                                    pe(lambda e, wname=wname, bank=bank, n=n, dch=dch, m=m, yf=yf: e.matmul(
                                        pb[bank + n][:, :],
                                        yf[:, dch, m * 128:(m + 1) * 128],
                                        w_at[wname][:, dch, n * 512:(n + 1) * 512],
                                        start=(dch == 0), stop=(dch == 7)),
                                        inc=(1 if dch == 7 else 0))
                        kv_pe = PE.v
                        b.wait("scalar", PE, kv_pe)
                        for n in range(2):
                            act(lambda e, n=n: e.activation(Khold[:, n * 512:(n + 1) * 512], pb[n][:, :], AF.Sigmoid))
                        k_ac = AC.v
                        b.wait("vector", PE, kv_pe)
                        for n in range(2):
                            vec(lambda e, n=n: e.tensor_copy(
                                Vext[:, 8 * n:8 * (n + 1), 0:64],
                                pb[2 + n][:, :]))
                        v_dv = DV.v
                        for half in range(2):
                            bank = 4 if half == 0 else 0
                            if half == 1:
                                b.wait("tensor", AC, k_ac)
                            b.wait("tensor", AC, qevac_ac[half])
                            for q4 in range(4):
                                qch = 4 * half + q4
                                for dch in range(8):
                                    pe(lambda e, bank=bank, q4=q4, qch=qch, dch=dch, m=m, yf=yf: e.matmul(
                                        pb[bank][:, 128 * q4:128 * q4 + 128],
                                        w_at["wq"][:, dch, qch * 128:(qch + 1) * 128],
                                        yf[:, dch, m * 128:(m + 1) * 128],
                                        start=(q4 == 0 and dch == 0), stop=(q4 == 3 and dch == 7)),
                                        inc=(1 if (q4 == 3 and dch == 7) else 0))
                            b.wait("scalar", PE)
                            act(lambda e, bank=bank, half=half, g=g, m=m: e.activation(
                                Qhold[:, 4 * half:4 * half + 4, (g * NSUB + m) * 128:(g * NSUB + m + 1) * 128],
                                pb[bank][:, 0:512], AF.Sigmoid))
                            qevac_ac[half] = AC.v
                        b.wait("tensor", AC)
                        b.wait("tensor", DV, v_dv)
                        firstmn = (g == 0 and m == 0)
                        lastmn = (g == GPS - 1 and m == NSUB - 1)
                        for h in range(HA):
                            first = firstmn and h in (0, 7, 14)
                            last = lastmn and h in (6, 13, 15)
                            pe(lambda e, h=h, first=first, last=last: e.matmul(
                                kvps[0:64, kvcol(h):kvcol(h) + 65],
                                Khold[:, 64 * h:64 * (h + 1)],
                                Vext[:, h, :],
                                start=first, stop=last),
                                inc=(1 if (lastmn and h == HA - 1) else 0))
                    yfm_guard[g % 2] = PE.v
                if gi < GPS:
                    g = gi; t0 = (s * GPS + g) * G
                    b.wait("sync", DV, xgr_guard[0])
                    dma("sync", xgr, ap["xin"][t0:t0 + G, :].rearrange("(m p) c -> p m c", p=128), LD)
                    b.wait("vector", LD)
                    gcol = (s * GPS + g) * NSUB
                    group_ln(xgr, gcol, NSUB, stat_mu, stat_iv)
                    for m in range(NSUB):
                        st_ = t0 + m * 128
                        col = gcol + m
                        b.wait("vector", ST, ybf_guard[0])
                        vec(lambda e, m=m, col=col: e.tensor_scalar(
                            ybf, xgr[:, m, :], stat_mu[:, col:col + 1], stat_iv[:, col:col + 1],
                            op0=ALU.subtract, op1=ALU.mult))
                        b.wait("gpsimd", DV)
                        dma("gpsimd", YB[st_:st_ + 128, :], ybf, ST)
                        ybf_guard[0] = ST.v
                    xgr_guard[0] = DV.v
                    ydone_st[g] = ST.v
            b.wait("vector", PE)
            vec(lambda e: e.memset(KVblk[:, :, :], 0.0), inc=0)
            for h in range(HA):
                q_, r_ = h // 2, h % 2
                vec(lambda e, h=h, q_=q_, r_=r_: e.tensor_copy(
                    KVblk[64 * r_:64 * (r_ + 1), q_, 65 * r_:65 * r_ + 65],
                    kvps[0:64, kvcol(h):kvcol(h) + 65]), inc=(1 if h == HA - 1 else 0))
            kv_dv = DV.v

            odone_st = [0] * GPS
            evdv = [DV.v, DV.v]
            nbuf_ac = [AC.v]
            for gi in range(GPS + 2):
                if gi < GPS:
                    g = gi; t0 = (s * GPS + g) * G
                    b.wait("tensor", DV, kv_dv)
                    for m in range(NSUB):
                        st_ = t0 + m * 128
                        b.wait("vector", AC, nbuf_ac[0])
                        for p_ in range(8):
                            bank = p_ % 2
                            b.wait("tensor", DV, evdv[bank])
                            pe(lambda e, p_=p_, bank=bank, g=g, m=m: e.matmul(
                                pb[bank][:, 0:130],
                                Qhold[:, p_, (g * NSUB + m) * 128:(g * NSUB + m + 1) * 128],
                                KVblk[:, p_, :],
                                start=True, stop=True))
                            b.wait("vector", PE)
                            vec(lambda e, bank=bank, p_=p_: e.tensor_copy(
                                numbuf[:, 130 * p_:130 * p_ + 130], pb[bank][:, 0:130]))
                            evdv[bank] = DV.v
                        nb3 = numbuf[:, 0:1040].rearrange("p (q c) -> p q c", c=130)
                        for j in range(2):
                            vec(lambda e, j=j, nb3=nb3: e.tensor_scalar(
                                den8[:, 8 * j:8 * j + 8], nb3[:, :, 64 + 65 * j:65 + 65 * j],
                                1e-8, None, op0=ALU.add), inc=(j == 1))
                        b.wait("vector", DV)
                        vtiny(lambda e: e.reciprocal(den8[:, 0:16], den8[:, 0:16]))
                        b.wait("scalar", DV)
                        b.wait("scalar", ST, osb_guard[0])
                        for p_ in range(8):
                            for j in range(2):
                                act(lambda e, p_=p_, j=j: e.activation(
                                    o_sb[:, 128 * p_ + 64 * j:128 * p_ + 64 * (j + 1)],
                                    numbuf[:, 130 * p_ + 65 * j:130 * p_ + 65 * j + 64],
                                    AF.Copy, scale=den8[:, 8 * j + p_:8 * j + p_ + 1]),
                                    inc=(1 if (p_ == 7 and j == 1) else 0))
                        nbuf_ac[0] = AC.v
                        b.wait("gpsimd", AC)
                        dma("gpsimd", OB[st_:st_ + 128, :], o_sb, ST)
                        osb_guard[0] = ST.v
                    odone_st[g] = ST.v
                if gi >= 2:
                    g = gi - 2; t0 = (s * GPS + g) * G
                    of = ofm[g % 2]
                    b.wait("sync", ST, odone_st[g])
                    b.wait("sync", PE, ofm_guard[g % 2])
                    for dch in range(8):
                        dmaT("sync", of[:, dch, :], OB[t0:t0 + G, 128 * dch:128 * (dch + 1)], LT)
                    b.wait("sync", DV, xgx_guard[0])
                    dma("sync", xgx, ap["xin"][t0:t0 + G, :].rearrange("(m p) c -> p m c", p=128), LD)
                    b.wait("tensor", LT)
                    gcol = (s * GPS + g) * NSUB
                    for m in range(NSUB):
                        col = gcol + m
                        b.wait("vector", LD)
                        b.wait("vector", ST, xbt_guard[0])
                        vec(lambda e, m=m, col=col: e.tensor_scalar(
                            xbt, xgx[:, m, :], stat_mu[:, col:col + 1], stat_iv[:, col:col + 1],
                            op0=ALU.subtract, op1=ALU.mult))
                        fcdv = DV.v
                        b.wait("tensor", DV, fcdv)
                        for n in range(2):
                            for dch in range(8):
                                pe(lambda e, n=n, dch=dch, m=m, of=of: e.matmul(
                                    pb[2 + n][:, :],
                                    of[:, dch, m * 128:(m + 1) * 128],
                                    w_at["wfc"][:, dch, n * 512:(n + 1) * 512],
                                    start=(dch == 0), stop=(dch == 7)),
                                    inc=(1 if dch == 7 else 0))
                        b.wait("vector", PE)
                        for n in range(2):
                            vec(lambda e, n=n, m=m: e.tensor_add(
                                xgr[:, m, n * 512:(n + 1) * 512], pb[2 + n][:, :],
                                xbt[:, n * 512:(n + 1) * 512]), inc=(1 if n == 1 else 0))
                        xbt_guard[0] = 0
                    group_ln(xgr, 0, NSUB, stat_mu2, stat_iv2)
                    for m in range(NSUB):
                        st_ = t0 + m * 128
                        vec(lambda e, m=m: e.tensor_scalar(
                            junk, xgr[:, m, :], stat_mu2[:, m:m + 1], stat_iv2[:, m:m + 1],
                            op0=ALU.subtract, op1=ALU.mult), inc=0)
                        vec(lambda e: e.tensor_mul(junk, junk, gamma[:, :]), inc=0)
                        b.wait("vector", ST, xbt_guard[0])
                        vec(lambda e, m=m: e.tensor_add(xbt, xgx[:, m, :], beta[:, :]), inc=0)
                        vec(lambda e: e.tensor_add(xbt, xbt, junk))
                        b.wait("gpsimd", DV)
                        dma("gpsimd", XS[st_:st_ + 128, :], xbt, ST)
                        xbt_guard[0] = ST.v
                    xgx_guard[0] = DV.v
                    xgr_guard[0] = DV.v
                    ofm_guard[g % 2] = PE.v

    f_xg   = f32a_t[:, 0:4096].rearrange("p (m c) -> p m c", m=4)
    f_xg2  = f32a_t[:, 4096:8192].rearrange("p (m c) -> p m c", m=4)
    f_hs   = [f32a_t[:, 4096:4612], f32a_t[:, 4612:5128]]
    f_cva  = f32a_t[:, 5128:5644]
    f_h2s  = [f32a_t[:, 5644:6157], f32a_t[:, 6157:6670]]
    f_m1x  = [f32a_t[:, 6670:7183], f32a_t[:, 7183:7696]]
    f_m1n  = [f32a_t[:, 7696:8192][:, 0:496], None]
    f_junk = f32a_t[:, 8192:9216]
    f_xbt  = f32a_t[:, 9216:10240]
    del f_hs, f_cva, f_h2s, f_m1x, f_m1n
    f_hs  = [f32a_t[:, 4096:4612], f32a_t[:, 4612:5128]]
    f_cva = f32a_t[:, 5128:5644]
    f_cva2 = [f32a_t[:, 5128:5644], f32b_t[:, 0:516]]
    f_h2s = f32a_t[:, 5644:6157]
    f_h2s2 = [f32a_t[:, 5644:6157], f32b_t[:, 516:1029]]
    f_m1x = f32a_t[:, 6157:6670]
    f_m1n = f32a_t[:, 6670:7183]
    f_k3x = f32a_t[:, 7183:7311]
    f_k3n = f32a_t[:, 7311:7439]
    f_bc  = f32a_t[:, 7439:7951]
    f_dg  = f32a_t[:, 5128:5640]
    f_h2i = [f32a_t[:, 8192:8704], f32a_t[:, 8704:9216]]
    f_mq  = f32a_t[:, 9216:9728]
    f_bcB = f32a_t[:, 9728:10240]

    halo = per.alloc(64)

    def emit_ffn(phase, src_XS, dst_XS, dbg=None):
        b.wait("gpsimd", PE)
        b.wait("gpsimd", ST)
        w1q = wta_t[:, 0:32768].rearrange("p (c n) -> p c n", c=8)
        for dch in range(8):
            dma("gpsimd", w1q[:, dch, :], WQ1[128 * dch:128 * (dch + 1), :], LW)
        lw_a = LW.v
        xs_st = ST.v
        h1bank_guard = [DV.v] * 6
        qevac_ac = [AC.v, AC.v]
        xg_guard = [DV.v]
        xg_guard_ac = [AC.v]
        yfm_guard2 = [PE.v, PE.v]
        ybf_guard2 = [ST.v, ST.v]
        ybf2 = bfa_t[:, 17408:18432]
        prevk3 = [None]
        h2st_done = [0]
        h2s_guard = [ST.v, ST.v]
        xbt_st_guard = [ST.v]

        def finish_amax_batch(col0, srcs):
            n = len(srcs)
            b.wait("tensor", DV)
            b.wait("tensor", GP)
            for i, (sx, sn) in enumerate(srcs):
                pe(lambda e, sx=sx, i=i: e.transpose(pb[3][:, 0:128] if False else pb[3][:, 0:128], sx, ident), inc=0) if False else None
            waves = [srcs[i:i + 2] for i in range(0, n, 2)]
            base = 0
            for w_i, wave in enumerate(waves):
                for i, (sx, sn) in enumerate(wave):
                    pe(lambda e, sx=sx, o=(2 * i) * 128: e.transpose(pb[3][:, o:o + 128], sx, ident), inc=0)
                    pe(lambda e, sn=sn, o=(2 * i + 1) * 128: e.transpose(pb[3][:, o:o + 128], sn, ident),
                       inc=(1 if i == len(wave) - 1 else 0))
                b.wait("vector", PE)
                for i in range(len(wave)):
                    j = w_i * 2 + i
                    vec(lambda e, i=i, j=j: e.reduce_max(tiny[:, 16 + j:17 + j], pb[3][:, (2 * i) * 128:(2 * i) * 128 + 128], axis=AX.X), inc=0)
                    vec(lambda e, i=i, j=j: e.tensor_reduce(tiny[:, 21 + j:22 + j], pb[3][:, (2 * i + 1) * 128:(2 * i + 1) * 128 + 128], axis=AX.X, op=ALU.min),
                        inc=(1 if i == len(wave) - 1 else 0))
                if w_i + 1 < len(waves):
                    b.wait("tensor", DV)
            b.wait("vector", DV)
            xs_ = tiny[:, 16:16 + n]; ns_ = tiny[:, 21:21 + n]
            vtiny(lambda e, xs_=xs_, ns_=ns_: e.scalar_tensor_tensor(xs_, ns_, -1.0, xs_, op0=ALU.mult, op1=ALU.max))
            vtiny(lambda e, xs_=xs_: e.tensor_scalar(xs_, xs_, 1e-5, None, op0=ALU.max))
            vtiny(lambda e, col0=col0, n=n, xs_=xs_: e.tensor_scalar(rAq[:, col0:col0 + n], xs_, invws2[:, 0:1], 1.0 / 127.0, op0=ALU.mult, op1=ALU.mult))
            vtiny(lambda e, ns_=ns_, xs_=xs_: e.reciprocal(ns_, xs_))
            vtiny(lambda e, col0=col0, n=n, ns_=ns_: e.tensor_scalar(Aq[:, col0:col0 + n], ns_, 127.0, None, op0=ALU.mult))

        for gi in range(NG + 2):
            if gi >= 2:
                g = gi - 2; t0 = g * G
                gcol = g * NSUB
                seq_start = (g % GPS == 0)
                seq_end = (g % GPS == GPS - 1)
                w = 511 if seq_start else (513 if seq_end else 512)
                wstart_off = 0 if seq_start else 1
                yf = yfm[g % 2]
                b.wait("sync", ST)
                b.wait("sync", PE, yfm_guard2[g % 2])
                for dch in range(8):
                    dmaT("sync", yf[:, dch, :], YB[t0:t0 + G, 128 * dch:128 * (dch + 1)], LT)
                b.wait("vector", PE)
                b.wait("vector", AC)
                for m in range(NSUB):
                    vec(lambda e, m=m, gcol=gcol: e.tensor_scalar(
                        f_dg[:, 128 * m:128 * (m + 1)], ident[:, :], rabq[:, gcol + m:gcol + m + 1], None,
                        op0=ALU.mult), inc=(m == NSUB - 1))
                b.wait("tensor", DV)
                b.wait("tensor", LT)
                b.wait("tensor", LW, lw_a)
                for m in range(NSUB):
                    pe(lambda e, m=m: e.matmul(pb[2][:, 128 * m:128 * (m + 1)], onesf[:, :], f_dg[:, 128 * m:128 * (m + 1)], start=True, stop=True), inc=(m == NSUB - 1))
                b.wait("vector", PE)
                vec(lambda e: e.tensor_copy(f_bc, pb[2][:, :]))
                if seq_start:
                    vec(lambda e: e.memset(halo[:, :], 0.0))
                b.wait("gpsimd", PE)
                b.wait("gpsimd", DV)
                gp(lambda e: e.memset(f_m1x, -3.0e38), inc=0)
                gp(lambda e: e.memset(f_m1n, 3.0e38))
                b.wait("vector", GP)
                silu_ac = [AC.v, AC.v]
                m1_dv = [DV.v, DV.v]
                h1banks = [pb[0][:, :], pb[1][:, :], pb[4][:, :],
                           kvps[:, 0:512], kvps[:, 512:1024], kvps[:, 1024:1536]]
                m1_pend = []
                halo_pend = [None]
                gp_done = {}
                gsilu_ac = [AC.v]
                for c in range(32):
                    is_gp = False
                    hbk = h1banks[c % 6]
                    hsb = f_hs[c % 2]
                    b.wait("tensor", DV, h1bank_guard[c % 6])
                    for dch in range(8):
                        pe(lambda e, hbk=hbk, c=c, dch=dch, yf=yf: e.matmul(
                            hbk,
                            w1q[:, dch, 128 * c:128 * (c + 1)],
                            yf[:, dch, :],
                            start=(dch == 0), stop=(dch == 7)),
                            inc=(1 if dch == 7 else 0))
                    if len(m1_pend) >= 2:
                        m1_pend.pop(0)()
                    if halo_pend[0] is not None:
                        halo_pend[0](); halo_pend[0] = None
                    if (c - 2) in gp_done:
                        b.wait("vector", GP, gp_done.pop(c - 2))
                    vec(lambda e, hsb=hsb, c=c: e.tensor_copy(hsb[:, 0:2], halo[:, 2 * c:2 * c + 2]), inc=0)
                    if seq_end:
                        vec(lambda e, hsb=hsb: e.memset(hsb[:, 514:516], 0.0), inc=0)
                    b.wait("vector", PE)
                    vec(lambda e, hsb=hsb, hbk=hbk: e.tensor_mul(hsb[:, 2:514], hbk, f_bc))
                    h1bank_guard[c % 6] = DV.v
                    halo_pend[0] = (lambda hsb=hsb, c=c: vec(
                        lambda e: e.tensor_copy(halo[:, 2 * c:2 * c + 2], hsb[:, 512:514])))
                    h2s = f_h2s2[c % 2]
                    lo = 1 if seq_start else 0
                    if is_gp:
                        gA = gcv_t[:, 0:516]; gB = gcv_t[:, 516:1032]
                        b.wait("gpsimd", DV)
                        b.wait("gpsimd", AC, gsilu_ac[0])
                        gp(lambda e, hsb=hsb, c=c, lo=lo, w=w, gA=gA: e.tensor_scalar(
                            gA[:, 0:w], hsb[:, lo + 1:lo + 1 + w], cw[1][:, c:c + 1], None, op0=ALU.mult), inc=0)
                        gp(lambda e, hsb=hsb, c=c, lo=lo, w=w, gB=gB: e.tensor_scalar(
                            gB[:, 0:w], hsb[:, lo:lo + w], cw[0][:, c:c + 1], None, op0=ALU.mult), inc=0)
                        gp(lambda e, w=w, gA=gA, gB=gB: e.tensor_tensor(gA[:, 0:w], gA[:, 0:w], gB[:, 0:w], op=ALU.add), inc=0)
                        gp(lambda e, hsb=hsb, c=c, lo=lo, w=w, gB=gB: e.tensor_scalar(
                            gB[:, 0:w], hsb[:, lo + 2:lo + 2 + w], cw[2][:, c:c + 1], None, op0=ALU.mult), inc=0)
                        gp(lambda e, w=w, gA=gA, gB=gB: e.tensor_tensor(gA[:, 0:w], gA[:, 0:w], gB[:, 0:w], op=ALU.add))
                        gp_done[c] = GP.v
                        b.wait("scalar", GP)
                        b.wait("scalar", ST, h2s_guard[c % 2])
                        b.wait("scalar", DV, m1_dv[c % 2])
                        act(lambda e, h2s=h2s, c=c, w=w, gA=gA: e.activation(h2s[:, 0:w], gA[:, 0:w], AF.Silu, bias=cbv[:, c:c + 1]))
                        gsilu_ac[0] = AC.v
                    else:
                        cva = f_cva2[c % 2]
                        b.wait("vector", AC, silu_ac[c % 2])
                        vec(lambda e, hsb=hsb, cva=cva, c=c, lo=lo, w=w: e.tensor_scalar(
                            cva[:, 0:w], hsb[:, lo + 1:lo + 1 + w], cw[1][:, c:c + 1], None, op0=ALU.mult), inc=0)
                        vec(lambda e, hsb=hsb, cva=cva, c=c, lo=lo, w=w: e.scalar_tensor_tensor(
                            cva[:, 0:w], hsb[:, lo:lo + w], cw[0][:, c:c + 1], cva[:, 0:w],
                            op0=ALU.mult, op1=ALU.add), inc=0)
                        vec(lambda e, hsb=hsb, cva=cva, c=c, lo=lo, w=w: e.scalar_tensor_tensor(
                            cva[:, 0:w], hsb[:, lo + 2:lo + 2 + w], cw[2][:, c:c + 1], cva[:, 0:w],
                            op0=ALU.mult, op1=ALU.add))
                        b.wait("scalar", DV)
                        b.wait("scalar", ST, h2s_guard[c % 2])
                        b.wait("scalar", DV, m1_dv[c % 2])
                        act(lambda e, cva=cva, h2s=h2s, c=c, w=w: e.activation(h2s[:, 0:w], cva[:, 0:w], AF.Silu, bias=cbv[:, c:c + 1]))
                        silu_ac[c % 2] = AC.v
                    def m1_emit(h2s=h2s, w=w, cc=c, ac_at=AC.v, gp_at=GP.v):
                        b.wait("vector", AC, ac_at)
                        b.wait("vector", GP, gp_at)
                        vec(lambda e: e.tensor_tensor(f_m1x[:, 0:w], f_m1x[:, 0:w], h2s[:, 0:w], op=ALU.max), inc=0)
                        vec(lambda e: e.tensor_tensor(f_m1n[:, 0:w], f_m1n[:, 0:w], h2s[:, 0:w], op=ALU.min))
                        m1_dv[cc % 2] = DV.v
                    m1_pend.append(m1_emit)
                    b.wait("sync", AC)
                    tkn0 = t0 - wstart_off
                    dma("sync", H2[128 * c:128 * (c + 1), tkn0:tkn0 + w], h2s[:, 0:w], ST)
                    h2s_guard[c % 2] = ST.v
                while m1_pend:
                    m1_pend.pop(0)()
                if halo_pend[0] is not None:
                    halo_pend[0](); halo_pend[0] = None
                h2st_done[0] = ST.v
                batch = []
                col0 = gcol
                b.wait("vector", GP)
                if prevk3[0] is not None:
                    if not seq_start:
                        vtiny(lambda e: e.tensor_copy(f_k3x[:, 127:128], f_m1x[:, 0:1]))
                        vtiny(lambda e: e.tensor_copy(f_k3n[:, 127:128], f_m1n[:, 0:1]))
                    col0 = prevk3[0][0]
                    batch.append((f_k3x, f_k3n))
                    prevk3[0] = None
                for k in range(3):
                    sl = (128 * k + (0 if seq_start else 1))
                    batch.append((f_m1x[:, sl:sl + 128], f_m1n[:, sl:sl + 128]))
                sl3 = 128 * 3 + (0 if seq_start else 1)
                if seq_end:
                    batch.append((f_m1x[:, sl3:sl3 + 128], f_m1n[:, sl3:sl3 + 128]))
                else:
                    vtiny(lambda e, sl3=sl3: e.tensor_copy(f_k3x[:, 0:127], f_m1x[:, sl3:sl3 + 127]))
                    vtiny(lambda e, sl3=sl3: e.tensor_copy(f_k3n[:, 0:127], f_m1n[:, sl3:sl3 + 127]))
                    prevk3[0] = (gcol + 3,)
                finish_amax_batch(col0, batch)
                yfm_guard2[g % 2] = PE.v

            if gi < NG:
                g = gi; t0 = g * G
                gcol = g * NSUB
                b.wait("sync", DV, xg_guard[0])
                b.wait("sync", AC, xg_guard_ac[0] if isinstance(xg_guard_ac, list) else 0)
                b.wait("sync", ST, xs_st)
                dma("sync", f_xg[:, :, :], src_XS[t0:t0 + G, :].rearrange("(m p) c -> p m c", p=128), LD)
                b.wait("vector", LD)
                group_ln(f_xg, gcol, NSUB, stat_mu, stat_iv)
                for m in range(NSUB):
                    col = gcol + m
                    b.wait("vector", ST, xbt_st_guard[0])
                    vec(lambda e, m=m, col=col: e.tensor_scalar(
                        f_xbt, f_xg[:, m, :], stat_mu[:, col:col + 1], None, op0=ALU.subtract), inc=0)
                    vec(lambda e, m=m: e.reduce_max(tiny[:, m:m + 1], f_xbt, axis=AX.X, apply_absolute_value=True))
                    b.wait("vector", DV)
                iv4 = stat_iv[:, gcol:gcol + 4]
                vtiny(lambda e, iv4=iv4: e.tensor_mul(tiny[:, 4:8], tiny[:, 0:4], iv4))
                vtiny(lambda e: e.tensor_scalar(tiny[:, 4:8], tiny[:, 4:8], 1e-5, None, op0=ALU.max))
                vtiny(lambda e, gcol=gcol: e.tensor_scalar(rabq[:, gcol:gcol + 4], tiny[:, 4:8], invws1[:, 0:1], 1.0 / 127.0, op0=ALU.mult, op1=ALU.mult))
                vtiny(lambda e: e.reciprocal(tiny[:, 8:12], tiny[:, 4:8]))
                vtiny(lambda e: e.tensor_scalar(tiny[:, 8:12], tiny[:, 8:12], 127.0, None, op0=ALU.mult))
                vtiny(lambda e, gcol=gcol, iv4=iv4: e.tensor_mul(aq[:, gcol:gcol + 4], tiny[:, 8:12], iv4))
                vtiny(lambda e, gcol=gcol: e.tensor_mul(tiny[:, 0:4], stat_mu[:, gcol:gcol + 4], aq[:, gcol:gcol + 4]))
                vtiny(lambda e: e.tensor_scalar(tiny[:, 0:4], tiny[:, 0:4], -1.0, None, op0=ALU.mult))
                b.wait("scalar", DV)
                for m in range(NSUB):
                    st_ = t0 + m * 128
                    col = gcol + m
                    yb2 = ybf if m % 2 == 0 else ybf2
                    b.wait("scalar", ST, ybf_guard2[m % 2])
                    act(lambda e, m=m, col=col: e.activation(
                        i32_t[:, 0:D], f_xg[:, m, :], AF.Identity,
                        scale=aq[:, col:col + 1], bias=tiny[:, m:m + 1]), inc=0)
                    act(lambda e, yb2=yb2: e.activation(yb2, i32_t[:, 0:D], AF.Copy))
                    b.wait("gpsimd", AC)
                    dma("gpsimd", YB[st_:st_ + 128, :], yb2, ST)
                    ybf_guard2[m % 2] = ST.v
                xg_guard[0] = DV.v
                xg_guard_ac[0] = AC.v
        b.wait("gpsimd", PE)
        w2q = wta_t[:, 0:32768].rearrange("p (c n) -> p c n", c=32)
        for c2 in range(0, 32, 4):
            dma("gpsimd", w2q[:, c2:c2 + 4, :], WQ2[128 * c2:128 * (c2 + 4), :].rearrange("(c p) n -> p c n", p=128), LW)
        lw_b = LW.v
        hq = bfa_t[:, 16384:32768].rearrange("p (c n) -> p c n", c=32)
        if dbg == "W2Q":
            b.wait("gpsimd", LW)
            dma("gpsimd", H2[0:4096, 0:1024].rearrange("(x p) c -> p x c", p=128), wta_t[:, :].rearrange("p (x c) -> p x c", x=32), ST)
        b.wait("sync", ST, h2st_done[0])
        xo_guard = [ST.v]
        x2_guard = [DV.v]
        wevac_dv = [DV.v]
        for g in range(NG):
            t0 = g * G
            gcol = g * NSUB
            b.wait("vector", PE)
            b.wait("vector", AC)
            for m in range(NSUB):
                vec(lambda e, m=m, gcol=gcol: e.tensor_scalar(
                    f_mq[:, 128 * m:128 * (m + 1)], ident[:, :], Aq[:, gcol + m:gcol + m + 1], None,
                    op0=ALU.mult), inc=(m == NSUB - 1))
            b.wait("tensor", DV)
            b.wait("tensor", ST, h2st_done[0])
            b.wait("tensor", LW, lw_b)
            for m in range(NSUB):
                pe(lambda e, m=m: e.matmul(pb[2][:, 128 * m:128 * (m + 1)], onesf[:, :], f_mq[:, 128 * m:128 * (m + 1)], start=True, stop=True), inc=(m == NSUB - 1))
            b.wait("vector", PE)
            vec(lambda e: e.tensor_copy(f_bcB, pb[2][:, :]))
            b.wait("sync", DV, x2_guard[0])
            b.wait("sync", ST, xs_st)
            dma("sync", f_xg2[:, :, :], src_XS[t0:t0 + G, :].rearrange("(m p) c -> p m c", p=128), LD)
            hq_gp = [0] * 32
            b.wait("gpsimd", DV)
            mq2 = [f_mq, f32b_t[:, 0:512]]
            mq_ac = [AC.v, AC.v]
            cast_ac = [0] * 32
            wbanks = [pb[0][:, :], pb[1][:, :], pb[3][:, :], pb[4][:, :],
                      kvps[:, 0:512], kvps[:, 512:1024], kvps[:, 1024:1536], pb[2][:, :]]
            b.wait("tensor", DV, wevac_dv[0])
            b.wait("vector", LD)
            for cc in range(34):
                if cc < 32:
                    c = cc
                    hb = f_h2i[c % 2]
                    mq = mq2[c % 2]
                    b.wait("sync", GP, hq_gp[c - 2] if c >= 2 else 0)
                    dma("sync", hb, H2[128 * c:128 * (c + 1), t0:t0 + G], LD)
                    b.wait("gpsimd", LD)
                    b.wait("gpsimd", AC, mq_ac[c % 2])
                    gp(lambda e, hb=hb, mq=mq: e.tensor_tensor(mq, hb, f_bcB, op=ALU.mult))
                    hq_gp[c] = GP.v
                    b.wait("scalar", GP)
                    act(lambda e, mq=mq: e.activation(i32_t[:, 0:G], mq, AF.Copy), inc=0)
                    act(lambda e, c=c: e.activation(hq[:, c, :], i32_t[:, 0:G], AF.Copy))
                    mq_ac[c % 2] = AC.v
                    cast_ac[c] = AC.v
                if cc >= 2:
                    c = cc - 2
                    b.wait("tensor", AC, cast_ac[c])
                    for k8 in range(8):
                        n, m = k8 // 4, k8 % 4
                        pe(lambda e, n=n, m=m, c=c, k8=k8, wbanks=wbanks: e.matmul(
                            wbanks[k8],
                            hq[:, c, 128 * m:128 * (m + 1)],
                            w2q[:, c, 512 * n:512 * (n + 1)],
                            start=(c == 0), stop=(c == 31)),
                            inc=(1 if (c == 31 and k8 == 7) else 0))
            if dbg == "HQ" and g == 0:
                b.wait("gpsimd", DV)
                dma("gpsimd", OB[0:2048, :].rearrange("(x p) c -> p x c", p=128), hq[:, :, :].rearrange("p c t -> p (c t)").rearrange("p (x c) -> p x c", x=16), ST)
            b.wait("vector", PE)
            b.wait("vector", ST, xo_guard[0])
            for k8 in range(8):
                n, m = k8 // 4, k8 % 4
                vec(lambda e, n=n, m=m, k8=k8, gcol=gcol, wbanks=wbanks: e.scalar_tensor_tensor(
                    f_xg[:, m, 512 * n:512 * (n + 1)], wbanks[k8], rAq[:, gcol + m:gcol + m + 1],
                    f_xg2[:, m, 512 * n:512 * (n + 1)], op0=ALU.mult, op1=ALU.add),
                    inc=(1 if k8 == 7 else 0))
            wevac_dv[0] = DV.v
            x2_guard[0] = DV.v
            b.wait("gpsimd", DV)
            for m in range(NSUB):
                st_ = t0 + m * 128
                dma("gpsimd", dst_XS[st_:st_ + 128, :], f_xg[:, m, :], ST)
            xo_guard[0] = ST.v

    if n_sublayers >= 2:
        emit_ffn(2, XS, XS if n_sublayers > 2 else XS, dbg=dbg)

    HB, DKB = 32, 32
    if n_sublayers >= 3:
        b.wait("gpsimd", PE)
        wta3 = Arena(wta_t, 32768)
        wqkv_sb = wta3.alloc(8 * 3 * D).rearrange("p (c n) -> p c n", c=8)
        wproj_sb = wta3.alloc(8 * D).rearrange("p (c n) -> p c n", c=8)
        for dch in range(8):
            dma("gpsimd", wqkv_sb[:, dch, :], ap["wqkvT"][128 * dch:128 * (dch + 1), :], LW)
            dma("gpsimd", wproj_sb[:, dch, :], ap["wprojT"][128 * dch:128 * (dch + 1), :], LW)
        p3_lw = LW.v
        bfa3 = Arena(bfa_t, 38400)
        yfm3 = [bfa3.alloc(8 * G).rearrange("p (c n) -> p c n", c=8) for _ in range(2)]
        ofm3 = [bfa3.alloc(8 * G).rearrange("p (c n) -> p c n", c=8) for _ in range(2)]
        k2_sb = bfa3.alloc(D)
        v2ext = bfa3.alloc(HB * 33).rearrange("p (h c) -> p h c", h=HB)
        Q2hold = bfa3.alloc(8 * SEQ).rearrange("p (c n) -> p c n", c=8)
        vkblk = bfa3.alloc(8 * 132).rearrange("p (q c) -> p q c", q=8)
        o2_sb = bfa3.alloc(D)
        ybf3 = bfa3.alloc(D)

        def vk_col(h):
            b_ = min(h // 15, 2)
            return 512 * b_ + 33 * (h - 15 * b_)

        xs_st3 = ST.v
        qevac_ac = [AC.v, AC.v]
        xgr_guard = [DV.v]
        xgx_guard = [DV.v]
        yfm_guard = [PE.v, PE.v]
        ofm_guard = [PE.v, PE.v]
        ybf_guard = [ST.v]
        osb_guard = [ST.v]
        xbt_guard = [ST.v]
        first_phase = [True]
        for s in range(NSEQ):
            ydone_st = [0] * GPS
            for gi in range(GPS + 2):
                if gi >= 2:
                    g = gi - 2; t0 = (s * GPS + g) * G
                    yf = yfm3[g % 2]
                    b.wait("sync", ST, ydone_st[g])
                    b.wait("sync", PE, yfm_guard[g % 2])
                    for dch in range(8):
                        dmaT("sync", yf[:, dch, :], YB[t0:t0 + G, 128 * dch:128 * (dch + 1)], LT)
                    b.wait("tensor", LT)
                    b.wait("tensor", LW, p3_lw)
                    if first_phase[0] and g == 0:
                        b.wait("vector", PE)
                        vec(lambda e: e.memset(v2ext[:, :, 32:33], 1.0))
                        b.wait("tensor", DV)
                        first_phase[0] = False
                    for m in range(NSUB):
                        for off, bank in ((D, 0), (2 * D, 2)):
                            for n in range(2):
                                for dch in range(8):
                                    pe(lambda e, off=off, bank=bank, n=n, dch=dch, m=m, yf=yf: e.matmul(
                                        pb[bank + n][:, :],
                                        yf[:, dch, m * 128:(m + 1) * 128],
                                        wqkv_sb[:, dch, off + n * 512:off + (n + 1) * 512],
                                        start=(dch == 0), stop=(dch == 7)),
                                        inc=(1 if dch == 7 else 0))
                        kv_pe = PE.v
                        b.wait("scalar", PE, kv_pe)
                        for n in range(2):
                            act(lambda e, n=n: e.activation(k2_sb[:, n * 512:(n + 1) * 512], pb[n][:, :], AF.Relu))
                        k_ac = AC.v
                        b.wait("vector", PE, kv_pe)
                        for n in range(2):
                            vec(lambda e, n=n: e.tensor_copy(
                                v2ext[:, 16 * n:16 * (n + 1), 0:32],
                                pb[2 + n][:, :]))
                        v_dv = DV.v
                        for half in range(2):
                            bank = 4 if half == 0 else 0
                            if half == 1:
                                b.wait("tensor", AC, k_ac)
                            b.wait("tensor", AC, qevac_ac[half])
                            for q4 in range(4):
                                qch = 4 * half + q4
                                for dch in range(8):
                                    pe(lambda e, bank=bank, q4=q4, qch=qch, dch=dch, m=m, yf=yf: e.matmul(
                                        pb[bank][:, 128 * q4:128 * q4 + 128],
                                        wqkv_sb[:, dch, qch * 128:(qch + 1) * 128],
                                        yf[:, dch, m * 128:(m + 1) * 128],
                                        start=(q4 == 0 and dch == 0), stop=(q4 == 3 and dch == 7)),
                                        inc=(1 if (q4 == 3 and dch == 7) else 0))
                            b.wait("scalar", PE)
                            act(lambda e, bank=bank, half=half, g=g, m=m: e.activation(
                                Q2hold[:, 4 * half:4 * half + 4, (g * NSUB + m) * 128:(g * NSUB + m + 1) * 128],
                                pb[bank][:, 0:512], AF.Relu))
                            qevac_ac[half] = AC.v
                        b.wait("tensor", AC)
                        b.wait("tensor", DV, v_dv)
                        firstmn = (g == 0 and m == 0)
                        lastmn = (g == GPS - 1 and m == NSUB - 1)
                        for h in range(HB):
                            first = firstmn and h in (0, 15, 30)
                            last = lastmn and h in (14, 29, 31)
                            pe(lambda e, h=h, first=first, last=last: e.matmul(
                                kvps[0:32, vk_col(h):vk_col(h) + 33],
                                k2_sb[:, 32 * h:32 * (h + 1)],
                                v2ext[:, h, :],
                                start=first, stop=last),
                                inc=(1 if (lastmn and h == HB - 1) else 0))
                    yfm_guard[g % 2] = PE.v
                if gi < GPS:
                    g = gi; t0 = (s * GPS + g) * G
                    b.wait("sync", DV, xgr_guard[0])
                    b.wait("sync", ST, xs_st3)
                    dma("sync", xgr, XS[t0:t0 + G, :].rearrange("(m p) c -> p m c", p=128), LD)
                    b.wait("vector", LD)
                    gcol = (s * GPS + g) * NSUB
                    group_ln(xgr, gcol, NSUB, stat_mu, stat_iv)
                    for m in range(NSUB):
                        st_ = t0 + m * 128
                        col = gcol + m
                        b.wait("vector", ST, ybf_guard[0])
                        vec(lambda e, m=m, col=col: e.tensor_scalar(
                            ybf3, xgr[:, m, :], stat_mu[:, col:col + 1], stat_iv[:, col:col + 1],
                            op0=ALU.subtract, op1=ALU.mult))
                        b.wait("gpsimd", DV)
                        dma("gpsimd", YB[st_:st_ + 128, :], ybf3, ST)
                        ybf_guard[0] = ST.v
                    xgr_guard[0] = DV.v
                    ydone_st[g] = ST.v
            b.wait("vector", PE)
            vec(lambda e: e.memset(vkblk[:, :, :], 0.0), inc=0)
            for h in range(HB):
                q_, r_ = h // 4, h % 4
                vec(lambda e, h=h, q_=q_, r_=r_: e.tensor_copy(
                    vkblk[32 * r_:32 * (r_ + 1), q_, 33 * r_:33 * r_ + 33],
                    kvps[0:32, vk_col(h):vk_col(h) + 33]), inc=(1 if h == HB - 1 else 0))
            kv_dv = DV.v

            odone_st = [0] * GPS
            evdv = [DV.v, DV.v]
            nbuf_ac = [AC.v]
            for gi in range(GPS + 2):
                if gi < GPS:
                    g = gi; t0 = (s * GPS + g) * G
                    b.wait("tensor", DV, kv_dv)
                    for m in range(NSUB):
                        st_ = t0 + m * 128
                        b.wait("vector", AC, nbuf_ac[0])
                        for q_ in range(8):
                            bank = q_ % 2
                            b.wait("tensor", DV, evdv[bank])
                            pe(lambda e, q_=q_, bank=bank, g=g, m=m: e.matmul(
                                pb[bank][:, 0:132],
                                Q2hold[:, q_, (g * NSUB + m) * 128:(g * NSUB + m + 1) * 128],
                                vkblk[:, q_, :],
                                start=True, stop=True))
                            b.wait("vector", PE)
                            vec(lambda e, bank=bank, q_=q_: e.tensor_copy(
                                numbuf[:, 132 * q_:132 * q_ + 132], pb[bank][:, 0:132]))
                            evdv[bank] = DV.v
                        nb3 = numbuf[:, 0:1056].rearrange("p (q c) -> p q c", c=132)
                        for j in range(4):
                            vec(lambda e, j=j, nb3=nb3: e.tensor_scalar(
                                den8[:, 8 * j:8 * j + 8], nb3[:, :, 33 * j + 32:33 * j + 33],
                                1e-15, None, op0=ALU.add), inc=(j == 3))
                        b.wait("vector", DV)
                        vtiny(lambda e: e.reciprocal(den8[:, 0:32], den8[:, 0:32]))
                        b.wait("scalar", DV)
                        b.wait("scalar", ST, osb_guard[0])
                        for q_ in range(8):
                            for j in range(4):
                                act(lambda e, q_=q_, j=j: e.activation(
                                    o2_sb[:, 128 * q_ + 32 * j:128 * q_ + 32 * (j + 1)],
                                    numbuf[:, 132 * q_ + 33 * j:132 * q_ + 33 * j + 32],
                                    AF.Copy, scale=den8[:, 8 * j + q_:8 * j + q_ + 1]),
                                    inc=(1 if (q_ == 7 and j == 3) else 0))
                        nbuf_ac[0] = AC.v
                        b.wait("gpsimd", AC)
                        dma("gpsimd", OB[st_:st_ + 128, :], o2_sb, ST)
                        osb_guard[0] = ST.v
                    odone_st[g] = ST.v
                if gi >= 2:
                    g = gi - 2; t0 = (s * GPS + g) * G
                    of = ofm3[g % 2]
                    b.wait("sync", ST, odone_st[g])
                    b.wait("sync", PE, ofm_guard[g % 2])
                    for dch in range(8):
                        dmaT("sync", of[:, dch, :], OB[t0:t0 + G, 128 * dch:128 * (dch + 1)], LT)
                    b.wait("sync", DV, xgx_guard[0])
                    b.wait("sync", ST, xs_st3)
                    dma("sync", xgx, XS[t0:t0 + G, :].rearrange("(m p) c -> p m c", p=128), LD)
                    b.wait("tensor", LT)
                    for m in range(NSUB):
                        st_ = t0 + m * 128
                        fcdv = DV.v
                        b.wait("tensor", DV, fcdv)
                        for n in range(2):
                            for dch in range(8):
                                pe(lambda e, n=n, dch=dch, m=m, of=of: e.matmul(
                                    pb[2 + n][:, :],
                                    of[:, dch, m * 128:(m + 1) * 128],
                                    wproj_sb[:, dch, n * 512:(n + 1) * 512],
                                    start=(dch == 0), stop=(dch == 7)),
                                    inc=(1 if dch == 7 else 0))
                        b.wait("vector", LD)
                        b.wait("vector", PE)
                        b.wait("vector", ST, xbt_guard[0])
                        for n in range(2):
                            vec(lambda e, n=n, m=m: e.tensor_add(
                                xbt[:, n * 512:(n + 1) * 512], pb[2 + n][:, :],
                                bproj[:, n * 512:(n + 1) * 512]), inc=0)
                        vec(lambda e, m=m: e.tensor_add(xbt, xbt, xgx[:, m, :]))
                        b.wait("gpsimd", DV)
                        dma("gpsimd", XS[st_:st_ + 128, :], xbt, ST)
                        xbt_guard[0] = ST.v
                    xgx_guard[0] = DV.v
                    ofm_guard[g % 2] = PE.v
            xgr_guard[0] = DV.v

    if n_sublayers >= 4:
        emit_ffn(4, XS, out, dbg=None)

    if n_sublayers < 4 or dbg is not None:
        b.wait("gpsimd", ST)
        b.wait("gpsimd", DV)
        b.wait("gpsimd", AC)
        if dbg == "PER":
            for i in range(3):
                dma("gpsimd", out[128 * i:128 * (i + 1), :], per_t[:, 1024 * i:1024 * (i + 1)], ST)
            dma("gpsimd", out[384:512, 0:3840 - 3072], per_t[:, 3072:3840], ST)
            b.wait("gpsimd", ST)
            srcT = None
        else:
            srcT = {None: XS, "XS": XS, "YB": YB, "OB": OB, "HQ": OB, "H2": H2, "W2Q": H2, "WQ1": WQ1, "WQ2": WQ2}[dbg]
        nrow = srcT.tensor.shape[0] if srcT is not None else 0
        for i in range(nrow // 128):
            dma("gpsimd", wld[i % 2][:, 0:D], srcT[128 * i:128 * (i + 1), 0:D], LD)
            b.wait("gpsimd", LD)
            dma("gpsimd", out[128 * i:128 * (i + 1), :], wld[i % 2][:, 0:D], ST)
        b.wait("gpsimd", ST)
    else:
        b.wait("gpsimd", ST)

    with nc.Block() as blk:
        for eng in B.ENGS:
            ops = b.q[eng]
            getattr(blk, eng)(lambda e, ops=ops: [f(e) for f in ops])
    es.close()
    return nc


def kernel(**inputs):
    per_core = host_prep(inputs)
    nc = build_program(n_sublayers=4)
    res = run_bass_kernel_spmd(nc, per_core, core_ids=list(range(NCORES)))
    return np.concatenate([r["out"] for r in res.results], axis=0).reshape(16, 2048, 1024)
